# version 15
# baseline (speedup 1.0000x reference)
import os
import numpy as np
import contextlib
import concourse.bass as bass
import concourse.mybir as mybir
from concourse.bass_utils import run_bass_kernel_spmd

F32 = mybir.dt.float32
BF16 = mybir.dt.bfloat16
Alu = mybir.AluOpType
Act = mybir.ActivationFunctionType
AX = mybir.AxisListType

T = 4096
D = 1024
HID = 3072
NT = T // 128
MAGIC = 12582912.0
EPS = 1e-5
QMAX = 32767.0
TWO16 = 65536.0
INV1024 = 1.0 / 1024.0


class Tok:
    def __init__(self, sem, name):
        self.sem = sem
        self.v = 0
        self.name = name
    def bump(self, by=1):
        self.v += by
        return self.v


KSTOP = int(os.environ.get('KSTOP', '99'))


def build_nc():
    nc = bass.Bass("TRN2", target_bir_lowering=False)
    x_in = nc.dram_tensor("x", [T, D], F32, kind="ExternalInput").ap()
    Wq_d = nc.dram_tensor("Wq", [D, D], F32, kind="ExternalInput").ap()
    bq_d = nc.dram_tensor("bq", [1, D], F32, kind="ExternalInput").ap()
    Wdn_d = nc.dram_tensor("Wdn", [D, 256], F32, kind="ExternalInput").ap()
    bdn_d = nc.dram_tensor("bdn", [1, 256], F32, kind="ExternalInput").ap()
    Wua_d = nc.dram_tensor("Wua", [256, 2048], F32, kind="ExternalInput").ap()
    bua_d = nc.dram_tensor("bua", [1, 2048], F32, kind="ExternalInput").ap()
    Wo_d = nc.dram_tensor("Wo", [D, D], F32, kind="ExternalInput").ap()
    bo_d = nc.dram_tensor("bo", [1, D], F32, kind="ExternalInput").ap()
    wg_d = nc.dram_tensor("wg", [HID, D], F32, kind="ExternalInput").ap()
    wu_d = nc.dram_tensor("wu", [HID, D], F32, kind="ExternalInput").ap()
    wd_d = nc.dram_tensor("wd", [D, HID], F32, kind="ExternalInput").ap()
    y_out = nc.dram_tensor("y", [T, D], F32, kind="ExternalOutput").ap()
    dbg_out = nc.dram_tensor("dbg", [128, 256], F32, kind="ExternalOutput").ap()

    hbm_x = nc.dram_tensor("hbm_x", [T, D], F32).ap()
    hb_wgT = nc.dram_tensor("hb_wgT", [8, 128, HID], BF16).ap()
    hb_wuT = nc.dram_tensor("hb_wuT", [8, 128, HID], BF16).ap()
    hb_wdT = nc.dram_tensor("hb_wdT", [24, 128, D], BF16).ap()
    sp_gate = nc.dram_tensor("sp_gate", [24, 128, T], F32).ap()
    sp_up = nc.dram_tensor("sp_up", [24, 128, T], F32).ap()
    sp_g2v = nc.dram_tensor("sp_g2v", [24, 128, T], F32).ap()
    sp_p = nc.dram_tensor("sp_p", [24, 128, T], F32).ap()
    cc_ins = [nc.dram_tensor(f"cc_in{i}", [1, 2], F32).ap() for i in range(8)]
    cc_outs = [nc.dram_tensor(f"cc_out{i}", [8, 2], F32).ap() for i in range(8)]

    es = contextlib.ExitStack()
    with es:
        def sbuf(name, shape, dt=F32):
            return es.enter_context(nc.sbuf_tensor(name, shape, dt))
        AR_A = sbuf("AR_A", [128, 32768], BF16)
        AR_B = sbuf("AR_B", [128, 4096], F32)
        AR_C = sbuf("AR_C", [128, 4096], F32)
        AR_W = sbuf("AR_W", [128, 24576], BF16)
        xt = sbuf("xt", [128, 2 * D])
        lnsc = sbuf("lnsc", [128, D])
        rxb = sbuf("rxb", [128, 2 * D], BF16)
        rxsc = lnsc
        statM = sbuf("statM", [128, NT])
        statR = sbuf("statR", [128, NT])
        colS = sbuf("colS", [128, NT])
        colQ = sbuf("colQ", [128, NT])
        lmax = sbuf("lmax", [128, 32])
        red1 = sbuf("red1", [128, 1])
        tmp1 = sbuf("tmp1", [128, 1])
        flip = sbuf("flip", [1, 128])
        sm = sbuf("sm", [1, 64])
        agf = sbuf("agf", [1, 16])
        ones1 = sbuf("ones1", [1, 128])
        bcv = sbuf("bcv", [1, 8])
        bc = sbuf("bc", [128, 8])
        sevr = sbuf("sevr", [128, 1])
        epsc = sbuf("epsc", [128, 1])
        bmax = sbuf("bmax", [128, 96])
        bsS = sbuf("bsS", [128, 96])
        brs = sbuf("brs", [128, 96])
        idf = sbuf("idf", [128, 128])
        idb = sbuf("idb", [128, 128], BF16)
        qt = sbuf("qt", [128, D])
        kvt = sbuf("kvt", [128, 2048])
        klt = sbuf("klt", [128, 256])
        kltT = sbuf("kltT", [128, 256], BF16)
        prod = sbuf("prod", [128, D])
        sco = sbuf("sco", [128, 256])
        smax = sbuf("smax", [128, 16])
        rsum = sbuf("rsum", [128, 16])
        rden = sbuf("rden", [128, 16])
        acc = sbuf("acc", [128, D])
        accb = rxb
        aoT = sbuf("aoT", [128, 1024], BF16)
        rof = sbuf("rof", [128, 256])
        bias1 = sbuf("bias1", [1, 4352])
        PS7 = es.enter_context(nc.psum_tensor("PS7", [128, 3584], F32))
        PT = es.enter_context(nc.psum_tensor("PT", [128, 1024], BF16))

        SB = Tok(es.enter_context(nc.semaphore("sB")), "B")
        LD = Tok(es.enter_context(nc.semaphore("sLD")), "LD")
        GD = Tok(es.enter_context(nc.semaphore("sGD")), "GD")
        SV = Tok(es.enter_context(nc.semaphore("sV")), "V")
        SA = Tok(es.enter_context(nc.semaphore("sA")), "A")
        ST = Tok(es.enter_context(nc.semaphore("sT")), "T")
        SG = Tok(es.enter_context(nc.semaphore("sG")), "G")
        CCS = Tok(es.enter_context(nc.semaphore("sCC")), "CC")

        phase = [0]
        ENGS = ["sync", "vector", "scalar", "tensor", "gpsimd"]

        smc = {"r32767": 0, "stage": 1}
        def smcol(name):
            if name not in smc:
                smc[name] = len(smc)
            return sm[0:1, smc[name]:smc[name] + 1]

        def vgap(v, n=2):
            for _ in range(n):
                v.tensor_copy(out=sm[0:1, 62:63], in_=ones1[0:1, 0:1])

        with nc.Block() as block:
            def on(eng, fn):
                getattr(block, eng)(fn)

            def barrier():
                phase[0] += 1
                for e in ENGS:
                    def f(eng):
                        eng.maybe_drain_then_inc((SB.sem, 1))
                    on(e, f)

            def WP(eng):
                eng.wait_ge(SB.sem, 5 * phase[0])

            def p0g(g):
                g.memset(idf[:], 0.0)
                g.affine_select(out=idf[:], in_=idf[:], compare_op=Alu.not_equal,
                                fill=1.0, base=0, pattern=[[-1, 128]], channel_multiplier=1)
                g.memset(sevr[:], 7.0)
                g.memset(epsc[:], EPS)
                g.memset(ones1[:], 1.0)
                g.memset(sm[:], 0.0)
                g.memset(bcv[:], 0.0)
                g.maybe_drain_then_inc((SG.sem, 1))
            on("gpsimd", p0g)
            tok0 = SG.bump()
            def p0v(v):
                v.wait_ge(SG.sem, tok0)
                v.tensor_copy(out=idb[:], in_=idf[:])
                v.reciprocal(out=sevr[:], in_=sevr[:])
                v.tensor_scalar(out=smcol("stage"), in0=ones1[0:1, 0:1], scalar1=QMAX,
                                scalar2=None, op0=Alu.mult)
                vgap(v)
                v.reciprocal(out=smcol("r32767"), in_=smcol("stage"))
            on("vector", p0v)
            barrier()

            def dequant_matrix(w_src, n_tiles, in_cols, hbT, tag):
                assert in_cols == 1024
                nblk = in_cols // 32
                kch = in_cols // 128
                slab_stride = n_tiles * 128
                tT = {}
                tV = {}
                tL = {}
                tA = {}
                for i in range(n_tiles):
                    wb = (i % 2) * 1024
                    qb = (i % 2) * in_cols
                    def fsy(sy, i=i, wb=wb):
                        if i == 0:
                            WP(sy)
                        if i >= 2:
                            sy.wait_ge(SV.sem, tV[i - 2])
                        sy.dma_start(out=AR_B[:, wb:wb + in_cols],
                                     in_=w_src[i * 128:(i + 1) * 128, :]).then_inc(LD.sem, 16)
                        tL[i] = LD.bump(16)
                    on("sync", fsy)
                    def fv(v, i=i, wb=wb, qb=qb):
                        if i == 0:
                            WP(v)
                        v.wait_ge(LD.sem, tL[i])
                        if i > 1:
                            v.wait_ge(ST.sem, tT[i - 2])
                        w_ap = AR_B[:, wb:wb + in_cols]
                        w3 = w_ap.rearrange("p (a b) -> p a b", b=32)
                        v.tensor_reduce(out=bmax[:, 0:nblk], in_=w3, axis=AX.X,
                                        op=Alu.max, apply_absolute_value=True)
                        vgap(v)
                        v.tensor_scalar(out=bsS[:, 0:nblk], in0=bmax[:, 0:nblk],
                                        scalar1=sevr[:, 0:1], scalar2=1e-12,
                                        op0=Alu.mult, op1=Alu.max)
                        vgap(v)
                        v.reciprocal(out=brs[:, 0:nblk], in_=bsS[:, 0:nblk])
                        vgap(v)
                        rs_b = brs[:, 0:nblk].unsqueeze(2).broadcast_to((128, nblk, 32))
                        s_b = bsS[:, 0:nblk].unsqueeze(2).broadcast_to((128, nblk, 32))
                        v.tensor_tensor(out=w3, in0=w3, in1=rs_b, op=Alu.mult)
                        v.tensor_scalar(out=w_ap, in0=w_ap, scalar1=MAGIC, scalar2=MAGIC,
                                        op0=Alu.add, op1=Alu.subtract)
                        wqb = AR_W[:, qb:qb + in_cols]
                        v.tensor_tensor(out=wqb.rearrange("p (a b) -> p a b", b=32),
                                        in0=w3, in1=s_b, op=Alu.mult)
                        v.maybe_drain_then_inc((SV.sem, 1))
                        tV[i] = SV.bump()
                    on("vector", fv)
                    def ft(t, i=i, qb=qb):
                        if i == 0:
                            WP(t)
                        t.wait_ge(SV.sem, tV[i])
                        t.wait_ge(SA.sem, SA.v)
                        for k in range(kch):
                            t.transpose(out=PT[:, k * 128:k * 128 + 128],
                                        in_=AR_W[:, qb + k * 128:qb + k * 128 + 128],
                                        identity=idb[:])
                        t.maybe_drain_then_inc((ST.sem, 1))
                    on("tensor", ft)
                    tT[i] = ST.bump()
                    def fs(s, i=i):
                        if i == 0:
                            WP(s)
                        s.wait_ge(ST.sem, tT[i])
                        for k in range(kch):
                            s.copy(out=AR_A[:, k * slab_stride + i * 128:
                                            k * slab_stride + i * 128 + 128],
                                   in_=PT[:, k * 128:k * 128 + 128])
                        s.maybe_drain_then_inc((SA.sem, 1))
                        tA[i] = SA.bump()
                    on("scalar", fs)
                def fw(sy):
                    sy.wait_ge(SA.sem, SA.v)
                    for k in range(kch):
                        sy.dma_start(out=hbT[k],
                                     in_=AR_A[:, k * slab_stride:k * slab_stride + n_tiles * 128]
                                     ).then_inc(LD.sem, 16)
                        LD.bump(16)
                    sy.wait_ge(LD.sem, LD.v)
                on("sync", fw)

            dequant_matrix(wg_d, 24, D, hb_wgT, "dg")
            barrier()
            if KSTOP >= 3:
                dequant_matrix(wu_d, 24, D, hb_wuT, "du")
            barrier()
            if KSTOP >= 4:
                for cs in range(3):
                    dequant_matrix(wd_d[:, cs * 1024:(cs + 1) * 1024], 8, 1024,
                                   [hb_wdT[cs * 8 + j] for j in range(8)], f"dd{cs}")
            barrier()

            def ln_stats(x_src):
                tL = {}
                tLv = {}
                for i in range(NT):
                    off = (i % 2) * D
                    def fsy(sy, i=i, off=off):
                        if i == 0:
                            WP(sy)
                        if i >= 2:
                            sy.wait_ge(SV.sem, tLv[i - 2])
                        sy.dma_start(out=xt[:, off:off + D],
                                     in_=x_src[i * 128:(i + 1) * 128, :]).then_inc(LD.sem, 16)
                        tL[i] = LD.bump(16)
                    on("sync", fsy)
                    def fv(v, i=i, off=off):
                        if i == 0:
                            WP(v)
                        v.wait_ge(LD.sem, tL[i])
                        v.tensor_reduce(out=colS[:, i:i + 1], in_=xt[:, off:off + D],
                                        axis=AX.X, op=Alu.add)
                        v.scalar_tensor_tensor(out=lnsc[:], in0=xt[:, off:off + D], scalar=1.0,
                                               in1=xt[:, off:off + D], op0=Alu.mult, op1=Alu.mult,
                                               accum_out=colQ[:, i:i + 1])
                        v.maybe_drain_then_inc((SV.sem, 1))
                        tLv[i] = SV.bump()
                    on("vector", fv)
                def fv2(v):
                    v.tensor_scalar(out=statM[:], in0=colS[:], scalar1=-INV1024,
                                    scalar2=None, op0=Alu.mult)
                    v.tensor_scalar(out=colQ[:], in0=colQ[:], scalar1=INV1024,
                                    scalar2=None, op0=Alu.mult)
                    vgap(v)
                    v.tensor_tensor(out=colS[:], in0=statM[:], in1=statM[:], op=Alu.mult)
                    vgap(v)
                    v.tensor_tensor(out=colQ[:], in0=colQ[:], in1=colS[:], op=Alu.subtract)
                    v.maybe_drain_then_inc((SV.sem, 1))
                on("vector", fv2)
                tv = SV.bump()
                def fs(s):
                    WP(s)
                    s.wait_ge(SV.sem, tv)
                    s.activation(out=statR[:], in_=colQ[:], func=Act.Sqrt, bias=epsc[:, 0:1])
                    s.maybe_drain_then_inc((SA.sem, 1))
                on("scalar", fs)
                ta = SA.bump()
                def fv3(v):
                    v.wait_ge(SA.sem, ta)
                    v.reciprocal(out=statR[:], in_=statR[:])
                on("vector", fv3)

            def do_sync(cc_idx, names2):
                def fg(g):
                    WP(g)
                    g.dma_start(out=cc_ins[cc_idx], in_=sm[0:1, 60:62]).then_inc(GD.sem, 16)
                    g.wait_ge(GD.sem, GD.bump(16))
                    g.collective_compute("AllGather", Alu.bypass,
                                         replica_groups=[list(range(8))],
                                         ins=[cc_ins[cc_idx]], outs=[cc_outs[cc_idx]]
                                         ).then_inc(CCS.sem, 1)
                    g.wait_ge(CCS.sem, CCS.bump())
                    g.dma_start(out=agf[:], in_=cc_outs[cc_idx]).then_inc(GD.sem, 16)
                    g.wait_ge(GD.sem, GD.bump(16))
                    g.maybe_drain_then_inc((SG.sem, 1))
                on("gpsimd", fg)
                tg = SG.bump()
                def fv(v):
                    WP(v)
                    v.wait_ge(SG.sem, tg)
                    for j, nm in enumerate(names2):
                        src = bass.AP(agf[:].tensor, agf[:].offset + j, [[16, 1], [2, 8]])
                        v.tensor_reduce(out=smcol(nm), in_=src, axis=AX.X, op=Alu.max)
                    vgap(v)
                on("vector", fv)

            def stage_locals(*col_aps):
                def fv(v):
                    for j, ap in enumerate(col_aps):
                        v.tensor_copy(out=sm[0:1, 60 + j:61 + j], in_=ap)
                    v.maybe_drain_then_inc((SV.sem, 1))
                on("vector", fv)
                return SV.bump()

            def local_absmax(cols_ap, out_name):
                def fv(v):
                    v.tensor_reduce(out=red1[:], in_=cols_ap, axis=AX.X, op=Alu.max)
                    v.maybe_drain_then_inc((SV.sem, 1))
                on("vector", fv)
                tv = SV.bump()
                def fg(g):
                    WP(g)
                    g.wait_ge(SV.sem, tv)
                    g.dma_start(out=flip[:], in_=red1[:]).then_inc(GD.sem, 16)
                    g.wait_ge(GD.sem, GD.bump(16))
                    g.maybe_drain_then_inc((SG.sem, 1))
                on("gpsimd", fg)
                tg = SG.bump()
                def fv2(v):
                    v.wait_ge(SG.sem, tg)
                    v.tensor_reduce(out=smcol(out_name), in_=flip[:], axis=AX.X, op=Alu.max)
                    vgap(v)
                on("vector", fv2)

            def bcast_consts(names_in):
                def fv(v):
                    for j, nm in enumerate(names_in):
                        v.tensor_copy(out=bcv[0:1, j:j + 1], in_=smcol(nm))
                    v.maybe_drain_then_inc((SV.sem, 1))
                on("vector", fv)
                tv = SV.bump()
                def ft(t):
                    WP(t)
                    t.wait_ge(SV.sem, tv)
                    t.matmul(out=PS7[:, 0:8], lhsT=ones1[:], rhs=bcv[:], start=True, stop=True)
                    t.maybe_drain_then_inc((ST.sem, 1))
                on("tensor", ft)
                tt = ST.bump()
                def fv2(v):
                    v.wait_ge(ST.sem, tt)
                    v.tensor_copy(out=bc[:, 0:8], in_=PS7[:, 0:8])
                    v.maybe_drain_then_inc((SV.sem, 1))
                on("vector", fv2)
                SV.bump()

            def ffn_sublayer(x_src, x_dst, sync_base, upto=99):
                ln_stats(x_src)
                barrier()
                if upto < 1:
                    return
                tL = {}
                tV = {}
                for i in range(NT):
                    off = (i % 2) * D
                    def fsy(sy, i=i, off=off):
                        if i == 0:
                            WP(sy)
                        if i >= 2:
                            sy.wait_ge(SV.sem, tV[i - 2])
                        sy.dma_start(out=xt[:, off:off + D],
                                     in_=x_src[i * 128:(i + 1) * 128, :]).then_inc(LD.sem, 16)
                        tL[i] = LD.bump(16)
                    on("sync", fsy)
                    def fv(v, i=i, off=off):
                        if i == 0:
                            WP(v)
                        v.wait_ge(LD.sem, tL[i])
                        v.tensor_scalar(out=lnsc[:], in0=xt[:, off:off + D],
                                        scalar1=statM[:, i:i + 1], scalar2=statR[:, i:i + 1],
                                        op0=Alu.add, op1=Alu.mult)
                        v.tensor_reduce(out=lmax[:, i:i + 1], in_=lnsc[:], axis=AX.X,
                                        op=Alu.max, apply_absolute_value=True)
                        v.maybe_drain_then_inc((SV.sem, 1))
                        tV[i] = SV.bump()
                    on("vector", fv)
                local_absmax(lmax[:, 0:NT], "lmx")
                st = stage_locals(smcol("lmx"))
                barrier()
                do_sync(sync_base + 0, ["gx"])
                def fsx(v):
                    v.tensor_scalar(out=smcol("s_x"), in0=smcol("gx"), scalar1=smcol("r32767"),
                                    scalar2=1e-12, op0=Alu.mult, op1=Alu.max)
                    vgap(v)
                    v.reciprocal(out=smcol("c1"), in_=smcol("s_x"))
                    vgap(v)
                on("vector", fsx)
                bcast_consts(["c1"])
                barrier()
                if upto < 2:
                    return
                RXL = 9
                tL = {}
                tV = {}
                tT = {}
                tA = {}
                RXS = 0
                for i in range(NT):
                    off = (i % 2) * D
                    def fsy(sy, i=i, off=off):
                        if i == 0:
                            WP(sy)
                        if RXS and i >= 1:
                            sy.wait_ge(SA.sem, tA[i - 1])
                        elif i >= 2:
                            sy.wait_ge(SV.sem, tV[i - 2])
                        sy.dma_start(out=xt[:, off:off + D],
                                     in_=x_src[i * 128:(i + 1) * 128, :]).then_inc(LD.sem, 16)
                        tL[i] = LD.bump(16)
                    on("sync", fsy)
                    def fv(v, i=i, off=off):
                        if i == 0:
                            WP(v)
                        v.wait_ge(LD.sem, tL[i])
                        if i > 1:
                            v.wait_ge(ST.sem, tT[i - 2])
                        v.tensor_tensor(out=tmp1[:], in0=statR[:, i:i + 1], in1=bc[:, 0:1],
                                        op=Alu.mult)
                        vgap(v)
                        v.tensor_scalar(out=rxsc[:], in0=xt[:, off:off + D],
                                        scalar1=statM[:, i:i + 1], scalar2=tmp1[:, 0:1],
                                        op0=Alu.add, op1=Alu.mult)
                        v.tensor_scalar(out=rxsc[:], in0=rxsc[:], scalar1=MAGIC, scalar2=MAGIC,
                                        op0=Alu.add, op1=Alu.subtract)
                        v.tensor_copy(out=rxb[:, (i % 2) * D:(i % 2) * D + D], in_=rxsc[:])
                        v.maybe_drain_then_inc((SV.sem, 1))
                        tV[i] = SV.bump()
                    on("vector", fv)
                    def ft(t, i=i):
                        if i == 0:
                            WP(t)
                        t.wait_ge(SV.sem, tV[i])
                        if i > 0:
                            t.wait_ge(SA.sem, tA[i - 1])
                        for k in range(8):
                            t.transpose(out=PT[:, k * 128:k * 128 + 128],
                                        in_=rxb[:, (i % 2) * D + k * 128:(i % 2) * D + k * 128 + 128],
                                        identity=idb[:])
                        t.maybe_drain_then_inc((ST.sem, 1))
                        tT[i] = ST.bump()
                    on("tensor", ft)
                    def fs(s, i=i):
                        if i == 0:
                            WP(s)
                        s.wait_ge(ST.sem, tT[i])
                        for k in range(8):
                            s.copy(out=AR_A[:, k * 4096 + i * 128:k * 4096 + i * 128 + 128],
                                   in_=PT[:, k * 128:k * 128 + 128])
                        s.maybe_drain_then_inc((SA.sem, 1))
                        tA[i] = SA.bump()
                    on("scalar", fs)
                barrier()

                def big_mm(hbT, sp_dst, maxname):
                    tEv = {}
                    tSp = {}
                    tMM = {}
                    tBlk = {}
                    tMGdone = {}
                    for mg in range(3):
                        wb_off = (mg % 2) * 8192
                        def fsy(sy, mg=mg, wb_off=wb_off):
                            if mg == 0:
                                WP(sy)
                            elif mg >= 2:
                                sy.wait_ge(ST.sem, tMGdone[mg - 2])
                            for k in range(8):
                                sy.dma_start(out=AR_W[:, wb_off + k * 1024:wb_off + k * 1024 + 1024],
                                             in_=hbT[k][:, mg * 1024:mg * 1024 + 1024]
                                             ).then_inc(LD.sem, 16)
                                LD.bump(16)
                        on("sync", fsy)
                        tBlk[mg] = LD.v
                        for ml in range(8):
                            m = mg * 8 + ml
                            for tg in range(2):
                                def ft(t, mg=mg, ml=ml, tg=tg, m=m, wb_off=wb_off):
                                    if mg == 0 and ml == 0 and tg == 0:
                                        WP(t)
                                    if ml == 0 and tg == 0:
                                        t.wait_ge(LD.sem, tBlk[mg])
                                    prev = (m, tg - 1) if tg == 1 else (m - 1, 1)
                                    if prev in tEv:
                                        t.wait_ge(SA.sem, tEv[prev])
                                    for k in range(8):
                                        lhsT = AR_W[:, wb_off + k * 1024 + ml * 128:
                                                    wb_off + k * 1024 + ml * 128 + 128]
                                        for tt in range(4):
                                            ttok = tg * 4 + tt
                                            rhs = AR_A[:, k * 4096 + ttok * 512:k * 4096 + ttok * 512 + 512]
                                            t.matmul(out=PS7[:, tt * 512:tt * 512 + 512],
                                                     lhsT=lhsT, rhs=rhs,
                                                     start=(k == 0), stop=(k == 7))
                                    t.maybe_drain_then_inc((ST.sem, 1))
                                on("tensor", ft)
                                tMM[(m, tg)] = ST.bump()
                                slab = AR_C if m % 2 == 0 else AR_B
                                def fs(s, m=m, tg=tg, slab=slab):
                                    if m == 0 and tg == 0:
                                        WP(s)
                                    s.wait_ge(ST.sem, tMM[(m, tg)])
                                    if tg == 0 and m >= 2:
                                        s.wait_ge(LD.sem, tSp[m - 2])
                                    for tt in range(4):
                                        ttok = tg * 4 + tt
                                        s.copy(out=slab[:, ttok * 512:ttok * 512 + 512],
                                               in_=PS7[:, tt * 512:tt * 512 + 512])
                                    s.maybe_drain_then_inc((SA.sem, 1))
                                on("scalar", fs)
                                tEv[(m, tg)] = SA.bump()
                            def fv(v, m=m, slab=slab):
                                if m == 0:
                                    WP(v)
                                v.wait_ge(SA.sem, tEv[(m, 1)])
                                v.tensor_reduce(out=lmax[:, m % 32:m % 32 + 1], in_=slab[:],
                                                axis=AX.X, op=Alu.max, apply_absolute_value=True)
                                v.maybe_drain_then_inc((SV.sem, 1))
                            on("vector", fv)
                            tVr = SV.v + 1
                            SV.bump()
                            def fsp(sy, m=m, slab=slab, tVr=tVr):
                                sy.wait_ge(SV.sem, tVr)
                                sy.dma_start(out=sp_dst[m], in_=slab[:]).then_inc(LD.sem, 16)
                                LD.bump(16)
                            on("sync", fsp)
                            tSp[m] = LD.v
                        tMGdone[mg] = ST.v
                    local_absmax(lmax[:, 0:24], maxname)

                big_mm(hb_wgT, sp_gate, "lgate")
                barrier()
                if upto < 4:
                    return
                big_mm(hb_wuT, sp_up, "lup")
                st = stage_locals(smcol("lgate"), smcol("lup"))
                barrier()
                do_sync(sync_base + 1, ["ggate", "gup"])
                def fs2v(v):
                    v.tensor_tensor(out=smcol("tg1"), in0=smcol("ggate"), in1=smcol("s_x"), op=Alu.mult)
                    v.tensor_tensor(out=smcol("tu1"), in0=smcol("gup"), in1=smcol("s_x"), op=Alu.mult)
                    vgap(v)
                    v.tensor_scalar(out=smcol("s_gate"), in0=smcol("tg1"), scalar1=smcol("r32767"),
                                    scalar2=1e-12, op0=Alu.mult, op1=Alu.max)
                    v.tensor_scalar(out=smcol("s_up"), in0=smcol("tu1"), scalar1=smcol("r32767"),
                                    scalar2=1e-12, op0=Alu.mult, op1=Alu.max)
                    vgap(v)
                    v.reciprocal(out=smcol("tg2"), in_=smcol("s_gate"))
                    v.reciprocal(out=smcol("tu2"), in_=smcol("s_up"))
                    vgap(v)
                    v.tensor_tensor(out=smcol("cg1"), in0=smcol("s_x"), in1=smcol("tg2"), op=Alu.mult)
                    v.tensor_tensor(out=smcol("cup"), in0=smcol("s_x"), in1=smcol("tu2"), op=Alu.mult)
                    vgap(v)
                on("vector", fs2v)
                bcast_consts(["cg1", "s_gate", "cup"])
                barrier()
                if upto < 5:
                    return

                tLh = {}
                tVh = {}
                tAh = {}
                tSh = {}
                def A_load(h):
                    m, hf, gb = h // 2, h % 2, (h % 2) * 2048
                    def fsy(sy, h=h, m=m, hf=hf, gb=gb):
                        if h == 0:
                            WP(sy)
                        if h >= 2:
                            sy.wait_ge(LD.sem, tSh[h - 2])
                        sy.dma_start(out=AR_B[:, gb:gb + 2048],
                                     in_=sp_gate[m][:, hf * 2048:hf * 2048 + 2048]
                                     ).then_inc(LD.sem, 16)
                        tLh[h] = LD.bump(16)
                    on("sync", fsy)
                def A_rg1(h):
                    gb = (h % 2) * 2048
                    def fv(v, h=h, gb=gb):
                        if h == 0:
                            WP(v)
                        v.wait_ge(LD.sem, tLh[h])
                        g_ap = AR_B[:, gb:gb + 2048]
                        v.tensor_scalar(out=g_ap, in0=g_ap, scalar1=bc[:, 0:1],
                                        scalar2=MAGIC, op0=Alu.mult, op1=Alu.add)
                        v.tensor_scalar(out=g_ap, in0=g_ap, scalar1=-MAGIC,
                                        scalar2=None, op0=Alu.add)
                        v.maybe_drain_then_inc((SV.sem, 1))
                    on("vector", fv)
                    return SV.bump()
                def A_sig(h, t1):
                    gb = (h % 2) * 2048
                    def fsg(s, h=h, gb=gb, t1=t1):
                        if h == 0:
                            WP(s)
                        s.wait_ge(SV.sem, t1)
                        s.activation(out=AR_C[:, gb:gb + 2048], in_=AR_B[:, gb:gb + 2048],
                                     func=Act.Sigmoid, scale=bc[:, 1:2])
                        s.maybe_drain_then_inc((SA.sem, 1))
                    on("scalar", fsg)
                    tAh[h] = SA.bump()
                def A_post(h):
                    m, hf, gb = h // 2, h % 2, (h % 2) * 2048
                    def fv2(v, h=h, gb=gb):
                        v.wait_ge(SA.sem, tAh[h])
                        c_ap = AR_C[:, gb:gb + 2048]
                        g_ap = AR_B[:, gb:gb + 2048]
                        v.tensor_scalar(out=c_ap, in0=c_ap, scalar1=TWO16,
                                        scalar2=MAGIC, op0=Alu.mult, op1=Alu.add)
                        v.tensor_scalar(out=c_ap, in0=c_ap, scalar1=-MAGIC,
                                        scalar2=65535.0, op0=Alu.add, op1=Alu.min)
                        v.tensor_tensor(out=g_ap, in0=g_ap, in1=c_ap, op=Alu.mult)
                        v.tensor_reduce(out=bmax[:, h:h + 1], in_=g_ap,
                                        axis=AX.X, op=Alu.max, apply_absolute_value=True)
                        v.maybe_drain_then_inc((SV.sem, 1))
                    on("vector", fv2)
                    tVh[h] = SV.bump()
                    def fsp(s, h=h, m=m, hf=hf, gb=gb):
                        s.wait_ge(SV.sem, tVh[h])
                        s.dma_start(out=sp_g2v[m][:, hf * 2048:hf * 2048 + 2048],
                                    in_=AR_B[:, gb:gb + 2048]).then_inc(LD.sem, 16)
                        tSh[h] = LD.bump(16)
                    on("scalar", fsp)
                A_load(0)
                t1s = {0: A_rg1(0)}
                A_sig(0, t1s[0])
                for h in range(48):
                    if h + 1 < 48:
                        A_load(h + 1)
                        t1s[h + 1] = A_rg1(h + 1)
                        A_sig(h + 1, t1s[h + 1])
                    A_post(h)
                def fwA(s):
                    s.wait_ge(LD.sem, LD.v)
                    s.maybe_drain_then_inc((SA.sem, 1))
                on("scalar", fwA)
                SA.bump()
                local_absmax(bmax[:, 0:48], "lg2v")
                st = stage_locals(smcol("lg2v"))
                barrier()
                do_sync(sync_base + 2, ["gg2v"])
                def fs3v(v):
                    v.tensor_scalar(out=smcol("k2"), in0=smcol("s_gate"), scalar1=1.0 / TWO16,
                                    scalar2=None, op0=Alu.mult)
                    vgap(v)
                    v.tensor_tensor(out=smcol("t31"), in0=smcol("gg2v"), in1=smcol("k2"), op=Alu.mult)
                    vgap(v)
                    v.tensor_scalar(out=smcol("s_g2"), in0=smcol("t31"), scalar1=smcol("r32767"),
                                    scalar2=1e-12, op0=Alu.mult, op1=Alu.max)
                    vgap(v)
                    v.reciprocal(out=smcol("t32"), in_=smcol("s_g2"))
                    vgap(v)
                    v.tensor_tensor(out=smcol("cg2"), in0=smcol("k2"), in1=smcol("t32"), op=Alu.mult)
                    vgap(v)
                on("vector", fs3v)
                bcast_consts(["cg2", "cup"])
                barrier()
                if upto < 6:
                    return

                tLh = {}
                tVh = {}
                tSh = {}
                for h in range(48):
                    m, hf = h // 2, h % 2
                    gb = (h % 2) * 2048
                    def fsy(sy, h=h, m=m, hf=hf, gb=gb):
                        if h == 0:
                            WP(sy)
                        if h >= 2:
                            sy.wait_ge(LD.sem, tSh[h - 2])
                        sy.dma_start(out=AR_B[:, gb:gb + 2048],
                                     in_=sp_g2v[m][:, hf * 2048:hf * 2048 + 2048]
                                     ).then_inc(LD.sem, 16)
                        LD.bump(16)
                        sy.dma_start(out=AR_C[:, gb:gb + 2048],
                                     in_=sp_up[m][:, hf * 2048:hf * 2048 + 2048]
                                     ).then_inc(LD.sem, 16)
                        tLh[h] = LD.bump(16)
                    on("sync", fsy)
                    def fv(v, h=h, gb=gb):
                        if h == 0:
                            WP(v)
                        v.wait_ge(LD.sem, tLh[h])
                        g_ap = AR_B[:, gb:gb + 2048]
                        u_ap = AR_C[:, gb:gb + 2048]
                        v.tensor_scalar(out=g_ap, in0=g_ap, scalar1=bc[:, 0:1],
                                        scalar2=MAGIC, op0=Alu.mult, op1=Alu.add)
                        v.tensor_scalar(out=g_ap, in0=g_ap, scalar1=-MAGIC,
                                        scalar2=None, op0=Alu.add)
                        v.tensor_scalar(out=u_ap, in0=u_ap, scalar1=bc[:, 1:2],
                                        scalar2=MAGIC, op0=Alu.mult, op1=Alu.add)
                        v.tensor_scalar(out=u_ap, in0=u_ap, scalar1=-MAGIC,
                                        scalar2=None, op0=Alu.add)
                        v.tensor_tensor(out=g_ap, in0=g_ap, in1=u_ap, op=Alu.mult)
                        v.tensor_reduce(out=bmax[:, h:h + 1], in_=g_ap,
                                        axis=AX.X, op=Alu.max, apply_absolute_value=True)
                        v.maybe_drain_then_inc((SV.sem, 1))
                    on("vector", fv)
                    tVh[h] = SV.bump()
                    def fsp(sy, h=h, m=m, hf=hf, gb=gb):
                        sy.wait_ge(SV.sem, tVh[h])
                        sy.dma_start(out=sp_p[m][:, hf * 2048:hf * 2048 + 2048],
                                     in_=AR_B[:, gb:gb + 2048]).then_inc(LD.sem, 16)
                        tSh[h] = LD.bump(16)
                    on("sync", fsp)
                def fwB(s):
                    s.wait_ge(LD.sem, LD.v)
                    s.maybe_drain_then_inc((SA.sem, 1))
                on("scalar", fwB)
                SA.bump()
                local_absmax(bmax[:, 0:48], "lp")
                st = stage_locals(smcol("lp"))
                barrier()
                do_sync(sync_base + 3, ["gp"])
                def fs4v(v):
                    v.tensor_tensor(out=smcol("k3"), in0=smcol("s_g2"), in1=smcol("s_up"), op=Alu.mult)
                    vgap(v)
                    v.tensor_tensor(out=smcol("t41"), in0=smcol("gp"), in1=smcol("k3"), op=Alu.mult)
                    vgap(v)
                    v.tensor_scalar(out=smcol("s_o"), in0=smcol("t41"), scalar1=smcol("r32767"),
                                    scalar2=1e-12, op0=Alu.mult, op1=Alu.max)
                    vgap(v)
                    v.reciprocal(out=smcol("t42"), in_=smcol("s_o"))
                    v.tensor_scalar(out=smcol("kout"), in0=smcol("s_o"), scalar1=0.5,
                                    scalar2=None, op0=Alu.mult)
                    vgap(v)
                    v.tensor_tensor(out=smcol("co"), in0=smcol("k3"), in1=smcol("t42"), op=Alu.mult)
                    vgap(v)
                on("vector", fs4v)
                bcast_consts(["co", "kout"])
                barrier()
                if upto < 7:
                    return

                def fldw(sy):
                    WP(sy)
                    for k in range(24):
                        sy.dma_start(out=AR_A[:, k * 1024:k * 1024 + 1024],
                                     in_=hb_wdT[k]).then_inc(LD.sem, 16)
                        LD.bump(16)
                on("sync", fldw)
                tldw = LD.v
                tgv_all = {}
                tMMc = {}
                for tt in range(16):
                    tgv = {}
                    tgl = {}
                    rb = (tt % 2) * 6144
                    for g in range(3):
                        gb = (g % 2) * 2048
                        def fsy(sy, tt=tt, g=g, gb=gb):
                            if tt == 0 and g == 0:
                                WP(sy)
                                sy.wait_ge(LD.sem, tldw)
                            if g == 0 and tt >= 2:
                                sy.wait_ge(ST.sem, tMMc[tt - 2])
                            gate = tgv.get(g - 2, tgv_all.get((tt - 1, g + 1)))
                            if gate is not None:
                                sy.wait_ge(SV.sem, gate)
                            for j in range(8):
                                sy.dma_start(out=AR_B[:, gb + j * 256:gb + j * 256 + 256],
                                             in_=sp_p[g * 8 + j][:, tt * 256:tt * 256 + 256]
                                             ).then_inc(LD.sem, 16)
                                LD.bump(16)
                            tgl[g] = LD.v
                        on("sync", fsy)
                        def fv(v, tt=tt, g=g, gb=gb, rb=rb):
                            if tt == 0 and g == 0:
                                WP(v)
                            v.wait_ge(LD.sem, tgl[g])
                            b_ap = AR_B[:, gb:gb + 2048]
                            v.tensor_scalar(out=b_ap, in0=b_ap, scalar1=bc[:, 0:1],
                                            scalar2=MAGIC, op0=Alu.mult, op1=Alu.add)
                            v.tensor_scalar(out=b_ap, in0=b_ap, scalar1=-MAGIC,
                                            scalar2=None, op0=Alu.add)
                            v.tensor_copy(out=AR_W[:, rb + g * 2048:rb + g * 2048 + 2048], in_=b_ap)
                            v.maybe_drain_then_inc((SV.sem, 1))
                        on("vector", fv)
                        tgv[g] = SV.bump()
                    tgv_all.update({(tt, g): t for g, t in tgv.items()})
                    tro = {23: tgv[2]}
                    def ft(t, tt=tt):
                        if tt == 0:
                            WP(t)
                        t.wait_ge(SV.sem, tro[23])
                        if tt > 0:
                            t.wait_ge(SA.sem, SA.v)
                            t.wait_ge(SV.sem, SV.v)
                        rb2 = (tt % 2) * 6144
                        for m in range(8):
                            for k in range(24):
                                t.matmul(out=PS7[:, m * 256:m * 256 + 256],
                                         lhsT=AR_A[:, k * 1024 + m * 128:k * 1024 + m * 128 + 128],
                                         rhs=AR_W[:, rb2 + k * 256:rb2 + k * 256 + 256],
                                         start=(k == 0), stop=(k == 23))
                        t.maybe_drain_then_inc((ST.sem, 1))
                    on("tensor", ft)
                    tmm = ST.bump()
                    tMMc[tt] = tmm
                    def fs(s, tt=tt, tmm=tmm):
                        if tt == 0:
                            WP(s)
                        s.wait_ge(ST.sem, tmm)
                        for m in range(8):
                            s.copy(out=AR_C[:, m * 256:m * 256 + 256],
                                   in_=PS7[:, m * 256:m * 256 + 256])
                        s.maybe_drain_then_inc((SA.sem, 1))
                    on("scalar", fs)
                    tev = SA.bump()
                    for tb in range(2):
                        def ft2(t, tt=tt, tb=tb, tev=tev):
                            t.wait_ge(SA.sem, tev)
                            if tb == 1:
                                t.wait_ge(SV.sem, SV.v)
                            for m in range(8):
                                t.transpose(out=PS7[:, 2048 + m * 128:2048 + m * 128 + 128],
                                            in_=AR_C[:, m * 256 + tb * 128:
                                                      m * 256 + tb * 128 + 128],
                                            identity=idf[:])
                            t.maybe_drain_then_inc((ST.sem, 1))
                        on("tensor", ft2)
                        ttr = ST.bump()
                        def fsy2(sy, tt=tt, tb=tb):
                            row = tt * 256 + tb * 128
                            sy.dma_start(out=xt[:, 0:D], in_=x_src[row:row + 128, :]
                                         ).then_inc(LD.sem, 16)
                            LD.bump(16)
                        on("sync", fsy2)
                        tlx = LD.v
                        def fv3(v, tt=tt, tb=tb, ttr=ttr, tlx=tlx):
                            v.wait_ge(ST.sem, ttr)
                            v.wait_ge(LD.sem, tlx)
                            v.scalar_tensor_tensor(out=xt[:, 0:D], in0=PS7[:, 2048:3072],
                                                   scalar=bc[:, 1:2], in1=xt[:, 0:D],
                                                   op0=Alu.mult, op1=Alu.add)
                            v.maybe_drain_then_inc((SV.sem, 1))
                        on("vector", fv3)
                        tvx = SV.bump()
                        def fsy3(sy, tt=tt, tb=tb, tvx=tvx):
                            row = tt * 256 + tb * 128
                            sy.wait_ge(SV.sem, tvx)
                            sy.dma_start(out=x_dst[row:row + 128, :], in_=xt[:, 0:D]
                                         ).then_inc(LD.sem, 16)
                            LD.bump(16)
                        on("sync", fsy3)
                barrier()

            def attn_sublayer(x_src, x_dst):
                ln_stats(x_src)
                barrier()
                def fldw(g):
                    WP(g)
                    for k in range(8):
                        g.dma_start(out=AR_W[:, k * 1024:k * 1024 + 1024],
                                    in_=Wq_d[k * 128:(k + 1) * 128, :]).then_inc(GD.sem, 16)
                        GD.bump(16)
                    for k in range(8):
                        g.dma_start(out=AR_W[:, 8192 + k * 256:8192 + k * 256 + 256],
                                    in_=Wdn_d[k * 128:(k + 1) * 128, :]).then_inc(GD.sem, 16)
                        GD.bump(16)
                    for k in range(2):
                        g.dma_start(out=AR_W[:, 10240 + k * 2048:10240 + k * 2048 + 2048],
                                    in_=Wua_d[k * 128:(k + 1) * 128, :]).then_inc(GD.sem, 16)
                        GD.bump(16)
                    for k in range(8):
                        g.dma_start(out=AR_W[:, 14336 + k * 1024:14336 + k * 1024 + 1024],
                                    in_=Wo_d[k * 128:(k + 1) * 128, :]).then_inc(GD.sem, 16)
                        GD.bump(16)
                    g.maybe_drain_then_inc((SG.sem, 1))
                on("gpsimd", fldw)
                tgw = SG.bump()
                def flb(sy):
                    WP(sy)
                    sy.dma_start(out=bias1[0:1, 0:1024], in_=bq_d).then_inc(LD.sem, 16)
                    sy.dma_start(out=bias1[0:1, 1024:1280], in_=bdn_d).then_inc(LD.sem, 16)
                    sy.dma_start(out=bias1[0:1, 1280:3328], in_=bua_d).then_inc(LD.sem, 16)
                    sy.dma_start(out=bias1[0:1, 3328:4352], in_=bo_d).then_inc(LD.sem, 16)
                    LD.bump(64)
                on("sync", flb)
                tlb = LD.v
                def ftb1(t, tlb=tlb):
                    WP(t)
                    t.wait_ge(LD.sem, tlb)
                    for j in range(7):
                        t.matmul(out=PS7[:, j * 512:j * 512 + 512], lhsT=ones1[:],
                                 rhs=bias1[0:1, j * 512:j * 512 + 512], start=True, stop=True)
                    t.maybe_drain_then_inc((ST.sem, 1))
                on("tensor", ftb1)
                ST.bump()
                def fvb1(v, tgt=ST.v):
                    WP(v)
                    v.wait_ge(ST.sem, tgt)
                    v.tensor_copy(out=AR_C[:, 0:3328], in_=PS7[:, 0:3328])
                    v.maybe_drain_then_inc((SV.sem, 1))
                on("vector", fvb1)
                SV.bump()
                def ftb2(t, tgt=SV.v):
                    t.wait_ge(SV.sem, tgt)
                    t.matmul(out=PS7[:, 0:512], lhsT=ones1[:], rhs=bias1[0:1, 3328:3840],
                             start=True, stop=True)
                    t.matmul(out=PS7[:, 512:1024], lhsT=ones1[:], rhs=bias1[0:1, 3840:4352],
                             start=True, stop=True)
                    t.maybe_drain_then_inc((ST.sem, 1))
                on("tensor", ftb2)
                ST.bump()
                def fvb2(v, tgt=ST.v):
                    v.wait_ge(ST.sem, tgt)
                    v.tensor_copy(out=AR_B[:, 0:1024], in_=PS7[:, 0:1024])
                    v.maybe_drain_then_inc((SV.sem, 1))
                on("vector", fvb2)
                SV.bump()
                tbias = SV.v
                tL = {}
                tV = {}
                tT = {}
                tA = {}
                for i in range(NT):
                    off = (i % 2) * D
                    def fsy(sy, i=i, off=off):
                        if i == 0:
                            WP(sy)
                        if i >= 1:
                            sy.wait_ge(SA.sem, tA[i - 1])
                        sy.dma_start(out=xt[:, off:off + D],
                                     in_=x_src[i * 128:(i + 1) * 128, :]).then_inc(LD.sem, 16)
                        tL[i] = LD.bump(16)
                    on("sync", fsy)
                    def fv(v, i=i, off=off):
                        if i == 0:
                            WP(v)
                            v.wait_ge(SV.sem, tbias)
                        v.wait_ge(LD.sem, tL[i])
                        if i > 0:
                            v.wait_ge(ST.sem, tT[i - 1])
                        v.tensor_scalar(out=lnsc[:], in0=xt[:, off:off + D],
                                        scalar1=statM[:, i:i + 1], scalar2=statR[:, i:i + 1],
                                        op0=Alu.add, op1=Alu.mult)
                        v.tensor_copy(out=rxb[:, 0:D], in_=lnsc[:])
                        v.maybe_drain_then_inc((SV.sem, 1))
                        tV[i] = SV.bump()
                    on("vector", fv)
                    def ft(t, i=i):
                        if i == 0:
                            WP(t)
                        t.wait_ge(SV.sem, tV[i])
                        if i > 0:
                            t.wait_ge(SA.sem, tA[i - 1])
                        for k in range(8):
                            t.transpose(out=PT[:, k * 128:k * 128 + 128],
                                        in_=rxb[:, k * 128:k * 128 + 128], identity=idb[:])
                        t.maybe_drain_then_inc((ST.sem, 1))
                        tT[i] = ST.bump()
                    on("tensor", ft)
                    def fs(s, i=i):
                        if i == 0:
                            WP(s)
                        s.wait_ge(ST.sem, tT[i])
                        for k in range(8):
                            s.copy(out=AR_A[:, k * 4096 + i * 128:k * 4096 + i * 128 + 128],
                                   in_=PT[:, k * 128:k * 128 + 128])
                        s.maybe_drain_then_inc((SA.sem, 1))
                        tA[i] = SA.bump()
                    on("scalar", fs)
                def fwg(t, tgw=tgw):
                    t.wait_ge(SG.sem, tgw)
                on("tensor", fwg)
                barrier()

                kv3 = kvt[:].rearrange("p (g c) -> p g c", c=128)
                tQE = {}
                tKVE = {}
                tSCD = {}
                tAVD = {}
                tOPE = {}
                tP1 = {}
                tP2 = {}
                tP3 = {}
                tP4 = {}
                tP5 = {}
                tAO = {}
                tKLT = {}
                def emit_ft1(i):
                    def ft1(t, i=i):
                        if i == 0:
                            WP(t)
                        else:
                            t.wait_ge(SA.sem, tQE[i - 1])
                            t.wait_ge(SV.sem, tKLT[i - 1])
                        for k in range(8):
                            lhsT = AR_A[:, k * 4096 + i * 128:k * 4096 + i * 128 + 128]
                            t.matmul(out=PS7[:, 0:512], lhsT=lhsT,
                                     rhs=AR_W[:, k * 1024:k * 1024 + 512],
                                     start=(k == 0), stop=False)
                            t.matmul(out=PS7[:, 512:1024], lhsT=lhsT,
                                     rhs=AR_W[:, k * 1024 + 512:k * 1024 + 1024],
                                     start=(k == 0), stop=False)
                        t.matmul(out=PS7[:, 0:512], lhsT=ones1[:], rhs=bias1[0:1, 0:512],
                                 start=False, stop=True)
                        t.matmul(out=PS7[:, 512:1024], lhsT=ones1[:], rhs=bias1[0:1, 512:1024],
                                 start=False, stop=True)
                        for k in range(8):
                            lhsT = AR_A[:, k * 4096 + i * 128:k * 4096 + i * 128 + 128]
                            t.matmul(out=PS7[:, 3072:3200],
                                     lhsT=AR_W[:, 8192 + k * 256:8192 + k * 256 + 128],
                                     rhs=lhsT, start=(k == 0), stop=False)
                        t.matmul(out=PS7[:, 3072:3200], lhsT=bias1[0:1, 1024:1152],
                                 rhs=ones1[0:1, 0:128], start=False, stop=True)
                        for k in range(8):
                            lhsT = AR_A[:, k * 4096 + i * 128:k * 4096 + i * 128 + 128]
                            t.matmul(out=PS7[:, 3200:3328],
                                     lhsT=AR_W[:, 8192 + k * 256 + 128:8192 + k * 256 + 256],
                                     rhs=lhsT, start=(k == 0), stop=False)
                        t.matmul(out=PS7[:, 3200:3328], lhsT=bias1[0:1, 1152:1280],
                                 rhs=ones1[0:1, 0:128], start=False, stop=True)
                        t.maybe_drain_then_inc((ST.sem, 1))
                    on("tensor", ft1)
                    tP1[i] = ST.bump()

                emit_ft1(0)
                for i in range(NT):
                    def fa1(s, i=i):
                        if i == 0:
                            WP(s)
                        s.wait_ge(ST.sem, tP1[i])
                        if i > 0:
                            s.wait_ge(SV.sem, tSCD[i - 1])
                        s.copy(out=qt[:], in_=PS7[:, 0:1024])
                        s.maybe_drain_then_inc((SA.sem, 1))
                    on("scalar", fa1)
                    tQE[i] = SA.bump()
                    def fvk(v, i=i):
                        if i == 0:
                            WP(v)
                        v.wait_ge(ST.sem, tP1[i])
                        v.tensor_copy(out=kltT[:], in_=PS7[:, 3072:3328])
                        v.maybe_drain_then_inc((SV.sem, 1))
                    on("vector", fvk)
                    tKLT[i] = SV.bump()
                    def ft3(t, i=i):
                        t.wait_ge(SV.sem, tKLT[i])
                        if i > 0:
                            t.wait_ge(SA.sem, tKVE[i - 1])
                            t.wait_ge(SA.sem, tOPE[i - 1])
                        for k2 in range(2):
                            lhsT = kltT[:, k2 * 128:k2 * 128 + 128]
                            for n in range(4):
                                t.matmul(out=PS7[:, 1024 + n * 512:1024 + n * 512 + 512],
                                         lhsT=lhsT,
                                         rhs=AR_W[:, 10240 + k2 * 2048 + n * 512:
                                                   10240 + k2 * 2048 + n * 512 + 512],
                                         start=(k2 == 0), stop=False)
                        for n in range(4):
                            t.matmul(out=PS7[:, 1024 + n * 512:1024 + n * 512 + 512],
                                     lhsT=ones1[:],
                                     rhs=bias1[0:1, 1280 + n * 512:1280 + n * 512 + 512],
                                     start=False, stop=True)
                        t.maybe_drain_then_inc((ST.sem, 1))
                    on("tensor", ft3)
                    tP3[i] = ST.bump()
                    def fa3(s, i=i):
                        s.wait_ge(ST.sem, tP3[i])
                        if i > 0:
                            s.wait_ge(SV.sem, tAVD[i - 1])
                        s.copy(out=kvt[:], in_=PS7[:, 1024:3072])
                        s.maybe_drain_then_inc((SA.sem, 1))
                    on("scalar", fa3)
                    tKVE[i] = SA.bump()
                    def fv3(v, i=i):
                        if i == 0:
                            WP(v)
                        v.wait_ge(SA.sem, tKVE[i])
                        for h in range(16):
                            qh = qt[:, h * 64:h * 64 + 64].unsqueeze(1).broadcast_to((128, 16, 64))
                            v.tensor_tensor(out=prod[:].rearrange("p (g c) -> p g c", c=64),
                                            in0=qh, in1=kv3[:, :, 0:64], op=Alu.mult)
                            v.tensor_reduce(out=sco[:, h * 16:h * 16 + 16],
                                            in_=prod[:].rearrange("p (g c) -> p g c", c=64),
                                            axis=AX.X, op=Alu.add)
                        v.tensor_reduce(out=smax[:], in_=sco[:].rearrange("p (h g) -> p h g", g=16),
                                        axis=AX.X, op=Alu.max)
                        vgap(v)
                        v.tensor_tensor(out=sco[:].rearrange("p (h g) -> p h g", g=16),
                                        in0=sco[:].rearrange("p (h g) -> p h g", g=16),
                                        in1=smax[:].unsqueeze(2).broadcast_to((128, 16, 16)),
                                        op=Alu.subtract)
                        v.maybe_drain_then_inc((SV.sem, 1))
                    on("vector", fv3)
                    tSCD[i] = SV.bump()
                    def fsc(s, i=i):
                        s.wait_ge(SV.sem, tSCD[i])
                        s.activation(out=sco[:], in_=sco[:], func=Act.Exp, scale=0.125)
                        s.maybe_drain_then_inc((SA.sem, 1))
                    on("scalar", fsc)
                    t7 = SA.bump()
                    def fv4(v, i=i, t7=t7):
                        v.wait_ge(SA.sem, t7)
                        v.tensor_reduce(out=rsum[:], in_=sco[:].rearrange("p (h g) -> p h g", g=16),
                                        axis=AX.X, op=Alu.add)
                        vgap(v)
                        v.reciprocal(out=rden[:], in_=rsum[:])
                        vgap(v)
                        v.tensor_tensor(out=sco[:].rearrange("p (h g) -> p h g", g=16),
                                        in0=sco[:].rearrange("p (h g) -> p h g", g=16),
                                        in1=rden[:].unsqueeze(2).broadcast_to((128, 16, 16)),
                                        op=Alu.mult)
                        sc0 = sco[:]
                        for m in [0] + [mm for mm in range(-15, 16) if mm != 0]:
                            hs = max(0, m)
                            cnt = 16 - abs(m)
                            a_ap = bass.AP(sc0.tensor, sc0.offset + 17 * hs - m,
                                           [[256, 128], [17, cnt], [0, 64]])
                            v_ap = bass.AP(kvt[:].tensor, kvt[:].offset + 128 * (hs - m) + 64,
                                           [[2048, 128], [128, cnt], [1, 64]])
                            if m == 0:
                                o_ap = acc[:].rearrange("p (h c) -> p h c", c=64)
                                v.tensor_tensor(out=o_ap, in0=a_ap, in1=v_ap, op=Alu.mult)
                            else:
                                p_ap = bass.AP(prod[:].tensor, prod[:].offset + 64 * hs,
                                               [[1024, 128], [64, cnt], [1, 64]])
                                oa = bass.AP(acc[:].tensor, acc[:].offset + 64 * hs,
                                             [[1024, 128], [64, cnt], [1, 64]])
                                v.tensor_tensor(out=p_ap, in0=a_ap, in1=v_ap, op=Alu.mult)
                                v.tensor_tensor(out=oa, in0=oa, in1=p_ap, op=Alu.add)
                        v.tensor_copy(out=accb[:, 0:D], in_=acc[:])
                        v.maybe_drain_then_inc((SV.sem, 1))
                    on("vector", fv4)
                    tAVD[i] = SV.bump()
                    if i + 1 < NT:
                        emit_ft1(i + 1)
                    def ft4(t, i=i):
                        t.wait_ge(SV.sem, tAVD[i])
                        for k in range(8):
                            t.transpose(out=PT[:, k * 128:k * 128 + 128],
                                        in_=accb[:, k * 128:k * 128 + 128], identity=idb[:])
                        t.maybe_drain_then_inc((ST.sem, 1))
                    on("tensor", ft4)
                    tP4[i] = ST.bump()
                    def fsa(s, i=i):
                        s.wait_ge(ST.sem, tP4[i])
                        s.copy(out=aoT[:], in_=PT[:, 0:1024])
                        s.maybe_drain_then_inc((SA.sem, 1))
                    on("scalar", fsa)
                    tAO[i] = SA.bump()
                    def ft5(t, i=i):
                        t.wait_ge(SA.sem, tAO[i])
                        for k in range(8):
                            lhsT = aoT[:, k * 128:k * 128 + 128]
                            t.matmul(out=PS7[:, 1024:1536], lhsT=lhsT,
                                     rhs=AR_W[:, 14336 + k * 1024:14336 + k * 1024 + 512],
                                     start=(k == 0), stop=False)
                            t.matmul(out=PS7[:, 1536:2048], lhsT=lhsT,
                                     rhs=AR_W[:, 14336 + k * 1024 + 512:14336 + k * 1024 + 1024],
                                     start=(k == 0), stop=False)
                        t.matmul(out=PS7[:, 1024:1536], lhsT=ones1[:],
                                 rhs=bias1[0:1, 3328:3840], start=False, stop=True)
                        t.matmul(out=PS7[:, 1536:2048], lhsT=ones1[:],
                                 rhs=bias1[0:1, 3840:4352], start=False, stop=True)
                        t.maybe_drain_then_inc((ST.sem, 1))
                    on("tensor", ft5)
                    tP5[i] = ST.bump()
                    def fa5(s, i=i):
                        s.wait_ge(ST.sem, tP5[i])
                        s.copy(out=lnsc[:], in_=PS7[:, 1024:2048])
                        s.maybe_drain_then_inc((SA.sem, 1))
                    on("scalar", fa5)
                    tOPE[i] = SA.bump()
                    def fsy(sy, i=i):
                        sy.dma_start(out=xt[:, 0:D], in_=x_src[i * 128:(i + 1) * 128, :]
                                     ).then_inc(LD.sem, 16)
                        LD.bump(16)
                    on("sync", fsy)
                    t12 = LD.v
                    def fv5(v, i=i, t12=t12):
                        v.wait_ge(SA.sem, tOPE[i])
                        v.wait_ge(LD.sem, t12)
                        v.tensor_tensor(out=xt[:, 0:D], in0=lnsc[:], in1=xt[:, 0:D], op=Alu.add)
                        v.maybe_drain_then_inc((SV.sem, 1))
                    on("vector", fv5)
                    t13 = SV.bump()
                    def fsy2(sy, i=i, t13=t13):
                        sy.wait_ge(SV.sem, t13)
                        sy.dma_start(out=x_dst[i * 128:(i + 1) * 128, :], in_=xt[:, 0:D]
                                     ).then_inc(LD.sem, 16)
                        LD.bump(16)
                    on("sync", fsy2)
                barrier()

            if KSTOP >= 5:
                ffn_sublayer(x_in, hbm_x, 0, upto=KSTOP - 5)
            if KSTOP >= 13:
                attn_sublayer(hbm_x, hbm_x)
            if KSTOP >= 14:
                ffn_sublayer(hbm_x, y_out, 4, upto=(99 if KSTOP >= 99 else KSTOP - 14))

            def fdbg(g):
                WP(g)
                g.dma_start(out=dbg_out[0:1, 0:64], in_=sm[:]).then_inc(GD.sem, 16)
                GD.bump(16)
                g.dma_start(out=dbg_out[0:128, 64:72], in_=bc[:]).then_inc(GD.sem, 16)
                GD.bump(16)
                g.dma_start(out=dbg_out[0:128, 72:104], in_=statM[:]).then_inc(GD.sem, 16)
                GD.bump(16)
                g.dma_start(out=dbg_out[0:128, 104:136], in_=statR[:]).then_inc(GD.sem, 16)
                GD.bump(16)
                g.dma_start(out=dbg_out[0:128, 136:168], in_=lmax[:]).then_inc(GD.sem, 16)
                GD.bump(16)
                g.dma_start(out=dbg_out[0:128, 168:200], in_=colS[:]).then_inc(GD.sem, 16)
                GD.bump(16)
                g.dma_start(out=dbg_out[0:128, 200:232], in_=colQ[:]).then_inc(GD.sem, 16)
                GD.bump(16)
                kd = os.environ.get('KDUMP', '')
                if kd == 'hbm_x':
                    g.dma_start(out=y_out, in_=hbm_x).then_inc(GD.sem, 16)
                    GD.bump(16)
                elif kd in ('spg', 'spu', 'spv', 'spp'):
                    s = {'spg': sp_gate, 'spu': sp_up, 'spv': sp_g2v, 'spp': sp_p}[kd]
                    g.dma_start(out=y_out, in_=s[0:8]).then_inc(GD.sem, 16)
                    GD.bump(16)
                elif kd in ('wgT', 'wuT'):
                    s = {'wgT': hb_wgT, 'wuT': hb_wuT}[kd]
                    g.dma_start(out=y_out[0:3072, :], in_=s).then_inc(GD.sem, 16)
                    GD.bump(16)
                elif kd == 'wdT':
                    g.dma_start(out=y_out[0:3072, :], in_=hb_wdT).then_inc(GD.sem, 16)
                    GD.bump(16)
                elif kd == 'arA':
                    g.dma_start(out=y_out, in_=AR_A[:]).then_inc(GD.sem, 16)
                    GD.bump(16)
                elif kd == 'arB':
                    g.dma_start(out=y_out[0:512, :], in_=AR_B[:]).then_inc(GD.sem, 16)
                    GD.bump(16)
                elif kd == 'attn':
                    g.dma_start(out=y_out[0:128, :], in_=qt[:]).then_inc(GD.sem, 16)
                    GD.bump(16)
                    g.dma_start(out=y_out[128:384, :], in_=kvt[:]).then_inc(GD.sem, 16)
                    GD.bump(16)
                    g.dma_start(out=y_out[384:416, :], in_=sco[:]).then_inc(GD.sem, 16)
                    GD.bump(16)
                    g.dma_start(out=y_out[416:544, :], in_=acc[:]).then_inc(GD.sem, 16)
                    GD.bump(16)
                elif kd == 'arW':
                    g.dma_start(out=y_out[0:768, :], in_=AR_W[:, 0:6144]).then_inc(GD.sem, 16)
                    GD.bump(16)
            on("gpsimd", fdbg)
            def ffin(sy):
                WP(sy)
                sy.wait_ge(LD.sem, LD.v)
                sy.wait_ge(GD.sem, GD.v)
            on("sync", ffin)
            barrier()
    return nc


_NC_CACHE = [None]

def kernel(L0_attn_x, L0_attn_Wq, L0_attn_bq, L0_attn_Wdown, L0_attn_bdown,
           L0_attn_Wup, L0_attn_bup, L0_attn_Wout, L0_attn_bout,
           L0_ffn_w_gate, L0_ffn_w_up, L0_ffn_w_down):
    if _NC_CACHE[0] is None:
        _NC_CACHE[0] = build_nc()
    nc = _NC_CACHE[0]
    x = np.ascontiguousarray(np.asarray(L0_attn_x, dtype=np.float32))
    common = {
        "Wq": np.ascontiguousarray(np.asarray(L0_attn_Wq, np.float32)),
        "bq": np.asarray(L0_attn_bq, np.float32).reshape(1, D),
        "Wdn": np.ascontiguousarray(np.asarray(L0_attn_Wdown, np.float32)),
        "bdn": np.asarray(L0_attn_bdown, np.float32).reshape(1, 256),
        "Wua": np.ascontiguousarray(np.asarray(L0_attn_Wup, np.float32)),
        "bua": np.asarray(L0_attn_bup, np.float32).reshape(1, 2048),
        "Wo": np.ascontiguousarray(np.asarray(L0_attn_Wout, np.float32)),
        "bo": np.asarray(L0_attn_bout, np.float32).reshape(1, D),
        "wg": np.ascontiguousarray(np.asarray(L0_ffn_w_gate, np.float32)),
        "wu": np.ascontiguousarray(np.asarray(L0_ffn_w_up, np.float32)),
        "wd": np.ascontiguousarray(np.asarray(L0_ffn_w_down, np.float32)),
    }
    in_maps = [dict(common, x=np.ascontiguousarray(x[c])) for c in range(8)]
    res = run_bass_kernel_spmd(nc, in_maps, core_ids=list(range(8)),
                               trace=(os.environ.get('KTRACE') == '1'))
    if os.environ.get('KTRACE') == '1':
        print('EXEC_NS', res.exec_time_ns)
    if os.environ.get('KDBG'):
        np.save('/tmp/dbg0.npy', np.asarray(res.results[0]['dbg']))
    out = np.stack([np.asarray(res.results[c]["y"], dtype=np.float32) for c in range(8)], axis=0)
    return out


# version 16
# speedup vs baseline: 1.1623x; 1.1623x over previous
import os
import numpy as np
import contextlib
import concourse.bass as bass
import concourse.mybir as mybir
from concourse.bass_utils import run_bass_kernel_spmd

F32 = mybir.dt.float32
BF16 = mybir.dt.bfloat16
Alu = mybir.AluOpType
Act = mybir.ActivationFunctionType
AX = mybir.AxisListType

T = 4096
D = 1024
HID = 3072
NT = T // 128
MAGIC = 12582912.0
EPS = 1e-5
QMAX = 32767.0
TWO16 = 65536.0
INV1024 = 1.0 / 1024.0


class Tok:
    def __init__(self, sem, name):
        self.sem = sem
        self.v = 0
        self.name = name
    def bump(self, by=1):
        self.v += by
        return self.v


KSTOP = int(os.environ.get('KSTOP', '99'))


def build_nc():
    nc = bass.Bass("TRN2", target_bir_lowering=False)
    x_in = nc.dram_tensor("x", [T, D], F32, kind="ExternalInput").ap()
    Wq_d = nc.dram_tensor("Wq", [D, D], F32, kind="ExternalInput").ap()
    bq_d = nc.dram_tensor("bq", [1, D], F32, kind="ExternalInput").ap()
    Wdn_d = nc.dram_tensor("Wdn", [D, 256], F32, kind="ExternalInput").ap()
    bdn_d = nc.dram_tensor("bdn", [1, 256], F32, kind="ExternalInput").ap()
    Wua_d = nc.dram_tensor("Wua", [256, 2048], F32, kind="ExternalInput").ap()
    bua_d = nc.dram_tensor("bua", [1, 2048], F32, kind="ExternalInput").ap()
    Wo_d = nc.dram_tensor("Wo", [D, D], F32, kind="ExternalInput").ap()
    bo_d = nc.dram_tensor("bo", [1, D], F32, kind="ExternalInput").ap()
    wg_d = nc.dram_tensor("wg", [HID, D], F32, kind="ExternalInput").ap()
    wu_d = nc.dram_tensor("wu", [HID, D], F32, kind="ExternalInput").ap()
    wd_d = nc.dram_tensor("wd", [D, HID], F32, kind="ExternalInput").ap()
    y_out = nc.dram_tensor("y", [T, D], F32, kind="ExternalOutput").ap()
    dbg_out = nc.dram_tensor("dbg", [128, 256], F32, kind="ExternalOutput").ap()

    hbm_x = nc.dram_tensor("hbm_x", [T, D], F32).ap()
    hb_wgT = nc.dram_tensor("hb_wgT", [8, 128, HID], BF16).ap()
    hb_wuT = nc.dram_tensor("hb_wuT", [8, 128, HID], BF16).ap()
    hb_wdT = nc.dram_tensor("hb_wdT", [24, 128, D], BF16).ap()
    sp_gate = nc.dram_tensor("sp_gate", [24, 128, T], F32).ap()
    sp_up = nc.dram_tensor("sp_up", [24, 128, T], F32).ap()
    sp_g2v = nc.dram_tensor("sp_g2v", [24, 128, T], F32).ap()
    sp_p = nc.dram_tensor("sp_p", [24, 128, T], F32).ap()
    cc_ins = [nc.dram_tensor(f"cc_in{i}", [1, 2], F32).ap() for i in range(8)]
    cc_outs = [nc.dram_tensor(f"cc_out{i}", [8, 2], F32).ap() for i in range(8)]

    es = contextlib.ExitStack()
    with es:
        def sbuf(name, shape, dt=F32):
            return es.enter_context(nc.sbuf_tensor(name, shape, dt))
        AR_A = sbuf("AR_A", [128, 32768], BF16)
        AR_B = sbuf("AR_B", [128, 4096], F32)
        AR_C = sbuf("AR_C", [128, 4096], F32)
        AR_W = sbuf("AR_W", [128, 24576], BF16)
        xt = sbuf("xt", [128, 2 * D])
        lnsc = sbuf("lnsc", [128, D])
        rxb = sbuf("rxb", [128, 2 * D], BF16)
        rxsc = lnsc
        statM = sbuf("statM", [128, NT])
        statR = sbuf("statR", [128, NT])
        colS = sbuf("colS", [128, NT])
        colQ = sbuf("colQ", [128, NT])
        lmax = sbuf("lmax", [128, 32])
        red1 = sbuf("red1", [128, 1])
        tmp1 = sbuf("tmp1", [128, 1])
        flip = sbuf("flip", [1, 128])
        sm = sbuf("sm", [1, 64])
        agf = sbuf("agf", [1, 16])
        ones1 = sbuf("ones1", [1, 128])
        bcv = sbuf("bcv", [1, 8])
        bc = sbuf("bc", [128, 8])
        sevr = sbuf("sevr", [128, 1])
        epsc = sbuf("epsc", [128, 1])
        bmax = sbuf("bmax", [128, 96])
        bsS = sbuf("bsS", [128, 96])
        brs = sbuf("brs", [128, 96])
        idf = sbuf("idf", [128, 128])
        idb = sbuf("idb", [128, 128], BF16)
        qt = sbuf("qt", [128, D])
        kvt = sbuf("kvt", [128, 2048])
        klt = sbuf("klt", [128, 256])
        kltT = sbuf("kltT", [128, 256], BF16)
        prod = sbuf("prod", [128, D])
        sco = sbuf("sco", [128, 256])
        smax = sbuf("smax", [128, 16])
        rsum = sbuf("rsum", [128, 16])
        rden = sbuf("rden", [128, 16])
        acc = sbuf("acc", [128, D])
        accb = rxb
        aoT = sbuf("aoT", [128, 1024], BF16)
        rof = sbuf("rof", [128, 256])
        bias1 = sbuf("bias1", [1, 4352])
        PS7 = es.enter_context(nc.psum_tensor("PS7", [128, 3584], F32))
        PT = es.enter_context(nc.psum_tensor("PT", [128, 1024], BF16))

        SB = Tok(es.enter_context(nc.semaphore("sB")), "B")
        LD = Tok(es.enter_context(nc.semaphore("sLD")), "LD")
        GD = Tok(es.enter_context(nc.semaphore("sGD")), "GD")
        SV = Tok(es.enter_context(nc.semaphore("sV")), "V")
        SA = Tok(es.enter_context(nc.semaphore("sA")), "A")
        ST = Tok(es.enter_context(nc.semaphore("sT")), "T")
        SG = Tok(es.enter_context(nc.semaphore("sG")), "G")
        CCS = Tok(es.enter_context(nc.semaphore("sCC")), "CC")

        phase = [0]
        ENGS = ["sync", "vector", "scalar", "tensor", "gpsimd"]

        smc = {"r32767": 0, "stage": 1}
        def smcol(name):
            if name not in smc:
                smc[name] = len(smc)
            return sm[0:1, smc[name]:smc[name] + 1]

        def vgap(v, n=2):
            for _ in range(n):
                v.tensor_copy(out=sm[0:1, 62:63], in_=ones1[0:1, 0:1])

        with nc.Block() as block:
            def on(eng, fn):
                getattr(block, eng)(fn)

            def barrier():
                phase[0] += 1
                for e in ENGS:
                    def f(eng):
                        eng.maybe_drain_then_inc((SB.sem, 1))
                    on(e, f)

            def WP(eng):
                eng.wait_ge(SB.sem, 5 * phase[0])

            def p0g(g):
                g.memset(idf[:], 0.0)
                g.affine_select(out=idf[:], in_=idf[:], compare_op=Alu.not_equal,
                                fill=1.0, base=0, pattern=[[-1, 128]], channel_multiplier=1)
                g.memset(sevr[:], 7.0)
                g.memset(epsc[:], EPS)
                g.memset(ones1[:], 1.0)
                g.memset(sm[:], 0.0)
                g.memset(bcv[:], 0.0)
                g.maybe_drain_then_inc((SG.sem, 1))
            on("gpsimd", p0g)
            tok0 = SG.bump()
            def p0v(v):
                v.wait_ge(SG.sem, tok0)
                v.tensor_copy(out=idb[:], in_=idf[:])
                v.reciprocal(out=sevr[:], in_=sevr[:])
                v.tensor_scalar(out=smcol("stage"), in0=ones1[0:1, 0:1], scalar1=QMAX,
                                scalar2=None, op0=Alu.mult)
                vgap(v)
                v.reciprocal(out=smcol("r32767"), in_=smcol("stage"))
            on("vector", p0v)
            barrier()

            def dequant_matrix(w_src, n_tiles, in_cols, hbT, tag):
                assert in_cols == 1024
                nblk = in_cols // 32
                kch = in_cols // 128
                slab_stride = n_tiles * 128
                tT = {}
                tV = {}
                tL = {}
                tA = {}
                for i in range(n_tiles):
                    wb = (i % 2) * 1024
                    qb = (i % 2) * in_cols
                    def fsy(sy, i=i, wb=wb):
                        if i == 0:
                            WP(sy)
                        if i >= 2:
                            sy.wait_ge(SV.sem, tV[i - 2])
                        sy.dma_start(out=AR_B[:, wb:wb + in_cols],
                                     in_=w_src[i * 128:(i + 1) * 128, :]).then_inc(LD.sem, 16)
                        tL[i] = LD.bump(16)
                    on("sync", fsy)
                    def fv(v, i=i, wb=wb, qb=qb):
                        if i == 0:
                            WP(v)
                        v.wait_ge(LD.sem, tL[i])
                        if i > 1:
                            v.wait_ge(ST.sem, tT[i - 2])
                        w_ap = AR_B[:, wb:wb + in_cols]
                        w3 = w_ap.rearrange("p (a b) -> p a b", b=32)
                        v.tensor_reduce(out=bmax[:, 0:nblk], in_=w3, axis=AX.X,
                                        op=Alu.max, apply_absolute_value=True)
                        vgap(v)
                        v.tensor_scalar(out=bsS[:, 0:nblk], in0=bmax[:, 0:nblk],
                                        scalar1=sevr[:, 0:1], scalar2=1e-12,
                                        op0=Alu.mult, op1=Alu.max)
                        vgap(v)
                        v.reciprocal(out=brs[:, 0:nblk], in_=bsS[:, 0:nblk])
                        vgap(v)
                        rs_b = brs[:, 0:nblk].unsqueeze(2).broadcast_to((128, nblk, 32))
                        s_b = bsS[:, 0:nblk].unsqueeze(2).broadcast_to((128, nblk, 32))
                        v.tensor_tensor(out=w3, in0=w3, in1=rs_b, op=Alu.mult)
                        v.tensor_scalar(out=w_ap, in0=w_ap, scalar1=MAGIC, scalar2=MAGIC,
                                        op0=Alu.add, op1=Alu.subtract)
                        wqb = AR_W[:, qb:qb + in_cols]
                        v.tensor_tensor(out=wqb.rearrange("p (a b) -> p a b", b=32),
                                        in0=w3, in1=s_b, op=Alu.mult)
                        v.maybe_drain_then_inc((SV.sem, 1))
                        tV[i] = SV.bump()
                    on("vector", fv)
                    def ft(t, i=i, qb=qb):
                        if i == 0:
                            WP(t)
                        t.wait_ge(SV.sem, tV[i])
                        t.wait_ge(SA.sem, SA.v)
                        for k in range(kch):
                            t.transpose(out=PT[:, k * 128:k * 128 + 128],
                                        in_=AR_W[:, qb + k * 128:qb + k * 128 + 128],
                                        identity=idb[:])
                        t.maybe_drain_then_inc((ST.sem, 1))
                    on("tensor", ft)
                    tT[i] = ST.bump()
                    def fs(s, i=i):
                        if i == 0:
                            WP(s)
                        s.wait_ge(ST.sem, tT[i])
                        for k in range(kch):
                            s.copy(out=AR_A[:, k * slab_stride + i * 128:
                                            k * slab_stride + i * 128 + 128],
                                   in_=PT[:, k * 128:k * 128 + 128])
                        s.maybe_drain_then_inc((SA.sem, 1))
                        tA[i] = SA.bump()
                    on("scalar", fs)
                def fw(sy):
                    sy.wait_ge(SA.sem, SA.v)
                    for k in range(kch):
                        sy.dma_start(out=hbT[k],
                                     in_=AR_A[:, k * slab_stride:k * slab_stride + n_tiles * 128]
                                     ).then_inc(LD.sem, 16)
                        LD.bump(16)
                    sy.wait_ge(LD.sem, LD.v)
                on("sync", fw)

            dequant_matrix(wg_d, 24, D, hb_wgT, "dg")
            barrier()
            if KSTOP >= 3:
                dequant_matrix(wu_d, 24, D, hb_wuT, "du")
            barrier()
            if KSTOP >= 4:
                for cs in range(3):
                    dequant_matrix(wd_d[:, cs * 1024:(cs + 1) * 1024], 8, 1024,
                                   [hb_wdT[cs * 8 + j] for j in range(8)], f"dd{cs}")
            barrier()

            def ln_stats(x_src, want_minmax=True):
                tL = {}
                tLv = {}
                for i in range(NT):
                    off = (i % 2) * D
                    def fsy(sy, i=i, off=off):
                        if i == 0:
                            WP(sy)
                        if i >= 2:
                            sy.wait_ge(SV.sem, tLv[i - 2])
                        sy.dma_start(out=xt[:, off:off + D],
                                     in_=x_src[i * 128:(i + 1) * 128, :]).then_inc(LD.sem, 16)
                        tL[i] = LD.bump(16)
                    on("sync", fsy)
                    def fv(v, i=i, off=off):
                        if i == 0:
                            WP(v)
                        v.wait_ge(LD.sem, tL[i])
                        v.tensor_reduce(out=colS[:, i:i + 1], in_=xt[:, off:off + D],
                                        axis=AX.X, op=Alu.add)
                        v.scalar_tensor_tensor(out=lnsc[:], in0=xt[:, off:off + D], scalar=1.0,
                                               in1=xt[:, off:off + D], op0=Alu.mult, op1=Alu.mult,
                                               accum_out=colQ[:, i:i + 1])
                        if want_minmax:
                            v.tensor_reduce(out=bmax[:, i:i + 1], in_=xt[:, off:off + D],
                                            axis=AX.X, op=Alu.max)
                            v.tensor_reduce(out=bmax[:, 32 + i:33 + i], in_=xt[:, off:off + D],
                                            axis=AX.X, op=Alu.min)
                        v.maybe_drain_then_inc((SV.sem, 1))
                        tLv[i] = SV.bump()
                    on("vector", fv)
                def fv2(v):
                    v.tensor_scalar(out=statM[:], in0=colS[:], scalar1=-INV1024,
                                    scalar2=None, op0=Alu.mult)
                    v.tensor_scalar(out=colQ[:], in0=colQ[:], scalar1=INV1024,
                                    scalar2=None, op0=Alu.mult)
                    vgap(v)
                    v.tensor_tensor(out=colS[:], in0=statM[:], in1=statM[:], op=Alu.mult)
                    vgap(v)
                    v.tensor_tensor(out=colQ[:], in0=colQ[:], in1=colS[:], op=Alu.subtract)
                    v.maybe_drain_then_inc((SV.sem, 1))
                on("vector", fv2)
                tv = SV.bump()
                def fs(s):
                    WP(s)
                    s.wait_ge(SV.sem, tv)
                    s.activation(out=statR[:], in_=colQ[:], func=Act.Sqrt, bias=epsc[:, 0:1])
                    s.maybe_drain_then_inc((SA.sem, 1))
                on("scalar", fs)
                ta = SA.bump()
                def fv3(v):
                    v.wait_ge(SA.sem, ta)
                    v.reciprocal(out=statR[:], in_=statR[:])
                on("vector", fv3)

            def do_sync(cc_idx, names2):
                def fg(g):
                    WP(g)
                    g.dma_start(out=cc_ins[cc_idx], in_=sm[0:1, 60:62]).then_inc(GD.sem, 16)
                    g.wait_ge(GD.sem, GD.bump(16))
                    g.collective_compute("AllGather", Alu.bypass,
                                         replica_groups=[list(range(8))],
                                         ins=[cc_ins[cc_idx]], outs=[cc_outs[cc_idx]]
                                         ).then_inc(CCS.sem, 1)
                    g.wait_ge(CCS.sem, CCS.bump())
                    g.dma_start(out=agf[:], in_=cc_outs[cc_idx]).then_inc(GD.sem, 16)
                    g.wait_ge(GD.sem, GD.bump(16))
                    g.maybe_drain_then_inc((SG.sem, 1))
                on("gpsimd", fg)
                tg = SG.bump()
                def fv(v):
                    WP(v)
                    v.wait_ge(SG.sem, tg)
                    for j, nm in enumerate(names2):
                        src = bass.AP(agf[:].tensor, agf[:].offset + j, [[16, 1], [2, 8]])
                        v.tensor_reduce(out=smcol(nm), in_=src, axis=AX.X, op=Alu.max)
                    vgap(v)
                on("vector", fv)

            def stage_locals(*col_aps):
                def fv(v):
                    for j, ap in enumerate(col_aps):
                        v.tensor_copy(out=sm[0:1, 60 + j:61 + j], in_=ap)
                    v.maybe_drain_then_inc((SV.sem, 1))
                on("vector", fv)
                return SV.bump()

            def local_absmax(cols_ap, out_name):
                def fv(v):
                    v.tensor_reduce(out=red1[:], in_=cols_ap, axis=AX.X, op=Alu.max)
                    v.maybe_drain_then_inc((SV.sem, 1))
                on("vector", fv)
                tv = SV.bump()
                def fg(g):
                    WP(g)
                    g.wait_ge(SV.sem, tv)
                    g.dma_start(out=flip[:], in_=red1[:]).then_inc(GD.sem, 16)
                    g.wait_ge(GD.sem, GD.bump(16))
                    g.maybe_drain_then_inc((SG.sem, 1))
                on("gpsimd", fg)
                tg = SG.bump()
                def fv2(v):
                    v.wait_ge(SG.sem, tg)
                    v.tensor_reduce(out=smcol(out_name), in_=flip[:], axis=AX.X, op=Alu.max)
                    vgap(v)
                on("vector", fv2)

            def bcast_consts(names_in):
                def fv(v):
                    for j, nm in enumerate(names_in):
                        v.tensor_copy(out=bcv[0:1, j:j + 1], in_=smcol(nm))
                    v.maybe_drain_then_inc((SV.sem, 1))
                on("vector", fv)
                tv = SV.bump()
                def ft(t):
                    WP(t)
                    t.wait_ge(SV.sem, tv)
                    t.matmul(out=PS7[:, 0:8], lhsT=ones1[:], rhs=bcv[:], start=True, stop=True)
                    t.maybe_drain_then_inc((ST.sem, 1))
                on("tensor", ft)
                tt = ST.bump()
                def fv2(v):
                    v.wait_ge(ST.sem, tt)
                    v.tensor_copy(out=bc[:, 0:8], in_=PS7[:, 0:8])
                    v.maybe_drain_then_inc((SV.sem, 1))
                on("vector", fv2)
                SV.bump()

            def ffn_sublayer(x_src, x_dst, sync_base, upto=99):
                ln_stats(x_src)
                barrier()
                if upto < 1:
                    return
                def flm(v):
                    v.tensor_tensor(out=bmax[:, 0:32], in0=bmax[:, 0:32], in1=statM[:, 0:32],
                                    op=Alu.add)
                    v.tensor_tensor(out=bmax[:, 32:64], in0=bmax[:, 32:64], in1=statM[:, 0:32],
                                    op=Alu.add)
                    vgap(v)
                    v.tensor_scalar(out=bmax[:, 32:64], in0=bmax[:, 32:64], scalar1=-1.0,
                                    scalar2=None, op0=Alu.mult)
                    vgap(v)
                    v.tensor_tensor(out=bmax[:, 0:32], in0=bmax[:, 0:32], in1=bmax[:, 32:64],
                                    op=Alu.max)
                    vgap(v)
                    v.tensor_tensor(out=lmax[:, 0:32], in0=bmax[:, 0:32], in1=statR[:, 0:32],
                                    op=Alu.mult)
                    v.maybe_drain_then_inc((SV.sem, 1))
                on("vector", flm)
                SV.bump()
                local_absmax(lmax[:, 0:NT], "lmx")
                st = stage_locals(smcol("lmx"))
                barrier()
                do_sync(sync_base + 0, ["gx"])
                def fsx(v):
                    v.tensor_scalar(out=smcol("s_x"), in0=smcol("gx"), scalar1=smcol("r32767"),
                                    scalar2=1e-12, op0=Alu.mult, op1=Alu.max)
                    vgap(v)
                    v.reciprocal(out=smcol("c1"), in_=smcol("s_x"))
                    vgap(v)
                on("vector", fsx)
                bcast_consts(["c1"])
                barrier()
                if upto < 2:
                    return
                RXL = 9
                tL = {}
                tV = {}
                tT = {}
                tA = {}
                RXS = 0
                for i in range(NT):
                    off = (i % 2) * D
                    def fsy(sy, i=i, off=off):
                        if i == 0:
                            WP(sy)
                        if RXS and i >= 1:
                            sy.wait_ge(SA.sem, tA[i - 1])
                        elif i >= 2:
                            sy.wait_ge(SV.sem, tV[i - 2])
                        sy.dma_start(out=xt[:, off:off + D],
                                     in_=x_src[i * 128:(i + 1) * 128, :]).then_inc(LD.sem, 16)
                        tL[i] = LD.bump(16)
                    on("sync", fsy)
                    def fv(v, i=i, off=off):
                        if i == 0:
                            WP(v)
                        v.wait_ge(LD.sem, tL[i])
                        if i > 1:
                            v.wait_ge(ST.sem, tT[i - 2])
                        v.tensor_tensor(out=tmp1[:], in0=statR[:, i:i + 1], in1=bc[:, 0:1],
                                        op=Alu.mult)
                        vgap(v)
                        v.tensor_scalar(out=rxsc[:], in0=xt[:, off:off + D],
                                        scalar1=statM[:, i:i + 1], scalar2=tmp1[:, 0:1],
                                        op0=Alu.add, op1=Alu.mult)
                        v.tensor_scalar(out=rxsc[:], in0=rxsc[:], scalar1=MAGIC, scalar2=MAGIC,
                                        op0=Alu.add, op1=Alu.subtract)
                        v.tensor_copy(out=rxb[:, (i % 2) * D:(i % 2) * D + D], in_=rxsc[:])
                        v.maybe_drain_then_inc((SV.sem, 1))
                        tV[i] = SV.bump()
                    on("vector", fv)
                    def ft(t, i=i):
                        if i == 0:
                            WP(t)
                        t.wait_ge(SV.sem, tV[i])
                        if i > 0:
                            t.wait_ge(SA.sem, tA[i - 1])
                        for k in range(8):
                            t.transpose(out=PT[:, k * 128:k * 128 + 128],
                                        in_=rxb[:, (i % 2) * D + k * 128:(i % 2) * D + k * 128 + 128],
                                        identity=idb[:])
                        t.maybe_drain_then_inc((ST.sem, 1))
                        tT[i] = ST.bump()
                    on("tensor", ft)
                    def fs(s, i=i):
                        if i == 0:
                            WP(s)
                        s.wait_ge(ST.sem, tT[i])
                        for k in range(8):
                            s.copy(out=AR_A[:, k * 4096 + i * 128:k * 4096 + i * 128 + 128],
                                   in_=PT[:, k * 128:k * 128 + 128])
                        s.maybe_drain_then_inc((SA.sem, 1))
                        tA[i] = SA.bump()
                    on("scalar", fs)
                barrier()

                def big_mm(hbT, sp_dst, maxname):
                    tEv = {}
                    tSp = {}
                    tMM = {}
                    tBlk = {}
                    tMGdone = {}
                    for mg in range(3):
                        wb_off = (mg % 2) * 8192
                        def fsy(sy, mg=mg, wb_off=wb_off):
                            if mg == 0:
                                WP(sy)
                            elif mg >= 2:
                                sy.wait_ge(ST.sem, tMGdone[mg - 2])
                            for k in range(8):
                                sy.dma_start(out=AR_W[:, wb_off + k * 1024:wb_off + k * 1024 + 1024],
                                             in_=hbT[k][:, mg * 1024:mg * 1024 + 1024]
                                             ).then_inc(LD.sem, 16)
                                LD.bump(16)
                        on("sync", fsy)
                        tBlk[mg] = LD.v
                        for ml in range(8):
                            m = mg * 8 + ml
                            for tg in range(2):
                                def ft(t, mg=mg, ml=ml, tg=tg, m=m, wb_off=wb_off):
                                    if mg == 0 and ml == 0 and tg == 0:
                                        WP(t)
                                    if ml == 0 and tg == 0:
                                        t.wait_ge(LD.sem, tBlk[mg])
                                    prev = (m, tg - 1) if tg == 1 else (m - 1, 1)
                                    if prev in tEv:
                                        t.wait_ge(SA.sem, tEv[prev])
                                    for k in range(8):
                                        lhsT = AR_W[:, wb_off + k * 1024 + ml * 128:
                                                    wb_off + k * 1024 + ml * 128 + 128]
                                        for tt in range(4):
                                            ttok = tg * 4 + tt
                                            rhs = AR_A[:, k * 4096 + ttok * 512:k * 4096 + ttok * 512 + 512]
                                            t.matmul(out=PS7[:, tt * 512:tt * 512 + 512],
                                                     lhsT=lhsT, rhs=rhs,
                                                     start=(k == 0), stop=(k == 7))
                                    t.maybe_drain_then_inc((ST.sem, 1))
                                on("tensor", ft)
                                tMM[(m, tg)] = ST.bump()
                                slab = AR_C if m % 2 == 0 else AR_B
                                def fs(s, m=m, tg=tg, slab=slab):
                                    if m == 0 and tg == 0:
                                        WP(s)
                                    s.wait_ge(ST.sem, tMM[(m, tg)])
                                    if tg == 0 and m >= 2:
                                        s.wait_ge(LD.sem, tSp[m - 2])
                                    for tt in range(4):
                                        ttok = tg * 4 + tt
                                        s.copy(out=slab[:, ttok * 512:ttok * 512 + 512],
                                               in_=PS7[:, tt * 512:tt * 512 + 512])
                                    s.maybe_drain_then_inc((SA.sem, 1))
                                on("scalar", fs)
                                tEv[(m, tg)] = SA.bump()
                            def fv(v, m=m, slab=slab):
                                if m == 0:
                                    WP(v)
                                v.wait_ge(SA.sem, tEv[(m, 1)])
                                v.tensor_reduce(out=lmax[:, m % 32:m % 32 + 1], in_=slab[:],
                                                axis=AX.X, op=Alu.max, apply_absolute_value=True)
                                v.maybe_drain_then_inc((SV.sem, 1))
                            on("vector", fv)
                            tVr = SV.v + 1
                            SV.bump()
                            def fsp(sy, m=m, slab=slab, tVr=tVr):
                                sy.wait_ge(SV.sem, tVr)
                                sy.dma_start(out=sp_dst[m], in_=slab[:]).then_inc(LD.sem, 16)
                                LD.bump(16)
                            on("sync", fsp)
                            tSp[m] = LD.v
                        tMGdone[mg] = ST.v
                    local_absmax(lmax[:, 0:24], maxname)

                big_mm(hb_wgT, sp_gate, "lgate")
                barrier()
                if upto < 4:
                    return
                big_mm(hb_wuT, sp_up, "lup")
                st = stage_locals(smcol("lgate"), smcol("lup"))
                barrier()
                do_sync(sync_base + 1, ["ggate", "gup"])
                def fs2v(v):
                    v.tensor_tensor(out=smcol("tg1"), in0=smcol("ggate"), in1=smcol("s_x"), op=Alu.mult)
                    v.tensor_tensor(out=smcol("tu1"), in0=smcol("gup"), in1=smcol("s_x"), op=Alu.mult)
                    vgap(v)
                    v.tensor_scalar(out=smcol("s_gate"), in0=smcol("tg1"), scalar1=smcol("r32767"),
                                    scalar2=1e-12, op0=Alu.mult, op1=Alu.max)
                    v.tensor_scalar(out=smcol("s_up"), in0=smcol("tu1"), scalar1=smcol("r32767"),
                                    scalar2=1e-12, op0=Alu.mult, op1=Alu.max)
                    vgap(v)
                    v.reciprocal(out=smcol("tg2"), in_=smcol("s_gate"))
                    v.reciprocal(out=smcol("tu2"), in_=smcol("s_up"))
                    vgap(v)
                    v.tensor_tensor(out=smcol("cg1"), in0=smcol("s_x"), in1=smcol("tg2"), op=Alu.mult)
                    v.tensor_tensor(out=smcol("cup"), in0=smcol("s_x"), in1=smcol("tu2"), op=Alu.mult)
                    vgap(v)
                on("vector", fs2v)
                bcast_consts(["cg1", "s_gate", "cup"])
                barrier()
                if upto < 5:
                    return

                tLh = {}
                tVh = {}
                tAh = {}
                tSh = {}
                def A_load(h):
                    m, hf, gb = h // 2, h % 2, (h % 2) * 2048
                    def fsy(sy, h=h, m=m, hf=hf, gb=gb):
                        if h == 0:
                            WP(sy)
                        if h >= 2:
                            sy.wait_ge(LD.sem, tSh[h - 2])
                        sy.dma_start(out=AR_B[:, gb:gb + 2048],
                                     in_=sp_gate[m][:, hf * 2048:hf * 2048 + 2048]
                                     ).then_inc(LD.sem, 16)
                        tLh[h] = LD.bump(16)
                    on("sync", fsy)
                def A_rg1(h):
                    gb = (h % 2) * 2048
                    def fv(v, h=h, gb=gb):
                        if h == 0:
                            WP(v)
                        v.wait_ge(LD.sem, tLh[h])
                        g_ap = AR_B[:, gb:gb + 2048]
                        v.tensor_scalar(out=g_ap, in0=g_ap, scalar1=bc[:, 0:1],
                                        scalar2=MAGIC, op0=Alu.mult, op1=Alu.add)
                        v.tensor_scalar(out=g_ap, in0=g_ap, scalar1=-MAGIC,
                                        scalar2=None, op0=Alu.add)
                        v.maybe_drain_then_inc((SV.sem, 1))
                    on("vector", fv)
                    return SV.bump()
                def A_sig(h, t1):
                    gb = (h % 2) * 2048
                    def fsg(s, h=h, gb=gb, t1=t1):
                        if h == 0:
                            WP(s)
                        s.wait_ge(SV.sem, t1)
                        s.activation(out=AR_C[:, gb:gb + 2048], in_=AR_B[:, gb:gb + 2048],
                                     func=Act.Sigmoid, scale=bc[:, 1:2])
                        s.maybe_drain_then_inc((SA.sem, 1))
                    on("scalar", fsg)
                    tAh[h] = SA.bump()
                def A_post(h):
                    m, hf, gb = h // 2, h % 2, (h % 2) * 2048
                    def fv2(v, h=h, gb=gb):
                        v.wait_ge(SA.sem, tAh[h])
                        c_ap = AR_C[:, gb:gb + 2048]
                        g_ap = AR_B[:, gb:gb + 2048]
                        v.tensor_scalar(out=c_ap, in0=c_ap, scalar1=TWO16,
                                        scalar2=MAGIC, op0=Alu.mult, op1=Alu.add)
                        v.tensor_scalar(out=c_ap, in0=c_ap, scalar1=-MAGIC,
                                        scalar2=65535.0, op0=Alu.add, op1=Alu.min)
                        v.tensor_tensor(out=g_ap, in0=g_ap, in1=c_ap, op=Alu.mult)
                        v.tensor_reduce(out=bmax[:, h:h + 1], in_=g_ap,
                                        axis=AX.X, op=Alu.max, apply_absolute_value=True)
                        v.maybe_drain_then_inc((SV.sem, 1))
                    on("vector", fv2)
                    tVh[h] = SV.bump()
                    def fsp(s, h=h, m=m, hf=hf, gb=gb):
                        s.wait_ge(SV.sem, tVh[h])
                        s.dma_start(out=sp_g2v[m][:, hf * 2048:hf * 2048 + 2048],
                                    in_=AR_B[:, gb:gb + 2048]).then_inc(LD.sem, 16)
                        tSh[h] = LD.bump(16)
                    on("scalar", fsp)
                A_load(0)
                t1s = {0: A_rg1(0)}
                A_sig(0, t1s[0])
                for h in range(48):
                    if h + 1 < 48:
                        A_load(h + 1)
                        t1s[h + 1] = A_rg1(h + 1)
                        A_sig(h + 1, t1s[h + 1])
                    A_post(h)
                def fwA(s):
                    s.wait_ge(LD.sem, LD.v)
                    s.maybe_drain_then_inc((SA.sem, 1))
                on("scalar", fwA)
                SA.bump()
                local_absmax(bmax[:, 0:48], "lg2v")
                st = stage_locals(smcol("lg2v"))
                barrier()
                do_sync(sync_base + 2, ["gg2v"])
                def fs3v(v):
                    v.tensor_scalar(out=smcol("k2"), in0=smcol("s_gate"), scalar1=1.0 / TWO16,
                                    scalar2=None, op0=Alu.mult)
                    vgap(v)
                    v.tensor_tensor(out=smcol("t31"), in0=smcol("gg2v"), in1=smcol("k2"), op=Alu.mult)
                    vgap(v)
                    v.tensor_scalar(out=smcol("s_g2"), in0=smcol("t31"), scalar1=smcol("r32767"),
                                    scalar2=1e-12, op0=Alu.mult, op1=Alu.max)
                    vgap(v)
                    v.reciprocal(out=smcol("t32"), in_=smcol("s_g2"))
                    vgap(v)
                    v.tensor_tensor(out=smcol("cg2"), in0=smcol("k2"), in1=smcol("t32"), op=Alu.mult)
                    vgap(v)
                on("vector", fs3v)
                bcast_consts(["cg2", "cup"])
                barrier()
                if upto < 6:
                    return

                tLh = {}
                tVh = {}
                tSh = {}
                for h in range(48):
                    m, hf = h // 2, h % 2
                    gb = (h % 2) * 2048
                    def fsy(sy, h=h, m=m, hf=hf, gb=gb):
                        if h == 0:
                            WP(sy)
                        if h >= 2:
                            sy.wait_ge(LD.sem, tSh[h - 2])
                        sy.dma_start(out=AR_B[:, gb:gb + 2048],
                                     in_=sp_g2v[m][:, hf * 2048:hf * 2048 + 2048]
                                     ).then_inc(LD.sem, 16)
                        LD.bump(16)
                        sy.dma_start(out=AR_C[:, gb:gb + 2048],
                                     in_=sp_up[m][:, hf * 2048:hf * 2048 + 2048]
                                     ).then_inc(LD.sem, 16)
                        tLh[h] = LD.bump(16)
                    on("sync", fsy)
                    def fv(v, h=h, gb=gb):
                        if h == 0:
                            WP(v)
                        v.wait_ge(LD.sem, tLh[h])
                        g_ap = AR_B[:, gb:gb + 2048]
                        u_ap = AR_C[:, gb:gb + 2048]
                        v.tensor_scalar(out=g_ap, in0=g_ap, scalar1=bc[:, 0:1],
                                        scalar2=MAGIC, op0=Alu.mult, op1=Alu.add)
                        v.tensor_scalar(out=g_ap, in0=g_ap, scalar1=-MAGIC,
                                        scalar2=None, op0=Alu.add)
                        v.tensor_scalar(out=u_ap, in0=u_ap, scalar1=bc[:, 1:2],
                                        scalar2=MAGIC, op0=Alu.mult, op1=Alu.add)
                        v.tensor_scalar(out=u_ap, in0=u_ap, scalar1=-MAGIC,
                                        scalar2=None, op0=Alu.add)
                        v.tensor_tensor(out=g_ap, in0=g_ap, in1=u_ap, op=Alu.mult)
                        v.tensor_reduce(out=bmax[:, h:h + 1], in_=g_ap,
                                        axis=AX.X, op=Alu.max, apply_absolute_value=True)
                        v.maybe_drain_then_inc((SV.sem, 1))
                    on("vector", fv)
                    tVh[h] = SV.bump()
                    def fsp(sy, h=h, m=m, hf=hf, gb=gb):
                        sy.wait_ge(SV.sem, tVh[h])
                        sy.dma_start(out=sp_p[m][:, hf * 2048:hf * 2048 + 2048],
                                     in_=AR_B[:, gb:gb + 2048]).then_inc(LD.sem, 16)
                        tSh[h] = LD.bump(16)
                    on("sync", fsp)
                def fwB(s):
                    s.wait_ge(LD.sem, LD.v)
                    s.maybe_drain_then_inc((SA.sem, 1))
                on("scalar", fwB)
                SA.bump()
                local_absmax(bmax[:, 0:48], "lp")
                st = stage_locals(smcol("lp"))
                barrier()
                do_sync(sync_base + 3, ["gp"])
                def fs4v(v):
                    v.tensor_tensor(out=smcol("k3"), in0=smcol("s_g2"), in1=smcol("s_up"), op=Alu.mult)
                    vgap(v)
                    v.tensor_tensor(out=smcol("t41"), in0=smcol("gp"), in1=smcol("k3"), op=Alu.mult)
                    vgap(v)
                    v.tensor_scalar(out=smcol("s_o"), in0=smcol("t41"), scalar1=smcol("r32767"),
                                    scalar2=1e-12, op0=Alu.mult, op1=Alu.max)
                    vgap(v)
                    v.reciprocal(out=smcol("t42"), in_=smcol("s_o"))
                    v.tensor_scalar(out=smcol("kout"), in0=smcol("s_o"), scalar1=0.5,
                                    scalar2=None, op0=Alu.mult)
                    vgap(v)
                    v.tensor_tensor(out=smcol("co"), in0=smcol("k3"), in1=smcol("t42"), op=Alu.mult)
                    vgap(v)
                on("vector", fs4v)
                bcast_consts(["co", "kout"])
                barrier()
                if upto < 7:
                    return

                def fldw(sy):
                    WP(sy)
                    for k in range(24):
                        sy.dma_start(out=AR_A[:, k * 1024:k * 1024 + 1024],
                                     in_=hb_wdT[k]).then_inc(LD.sem, 16)
                        LD.bump(16)
                on("sync", fldw)
                tldw = LD.v
                tgv_all = {}
                tMMc = {}
                for tt in range(16):
                    tgv = {}
                    tgl = {}
                    rb = (tt % 2) * 6144
                    for g in range(3):
                        gb = (g % 2) * 2048
                        def fsy(sy, tt=tt, g=g, gb=gb):
                            if tt == 0 and g == 0:
                                WP(sy)
                                sy.wait_ge(LD.sem, tldw)
                            if g == 0 and tt >= 2:
                                sy.wait_ge(ST.sem, tMMc[tt - 2])
                            gate = tgv.get(g - 2, tgv_all.get((tt - 1, g + 1)))
                            if gate is not None:
                                sy.wait_ge(SV.sem, gate)
                            for j in range(8):
                                sy.dma_start(out=AR_B[:, gb + j * 256:gb + j * 256 + 256],
                                             in_=sp_p[g * 8 + j][:, tt * 256:tt * 256 + 256]
                                             ).then_inc(LD.sem, 16)
                                LD.bump(16)
                            tgl[g] = LD.v
                        on("sync", fsy)
                        def fv(v, tt=tt, g=g, gb=gb, rb=rb):
                            if tt == 0 and g == 0:
                                WP(v)
                            v.wait_ge(LD.sem, tgl[g])
                            b_ap = AR_B[:, gb:gb + 2048]
                            v.tensor_scalar(out=b_ap, in0=b_ap, scalar1=bc[:, 0:1],
                                            scalar2=MAGIC, op0=Alu.mult, op1=Alu.add)
                            v.tensor_scalar(out=b_ap, in0=b_ap, scalar1=-MAGIC,
                                            scalar2=None, op0=Alu.add)
                            v.tensor_copy(out=AR_W[:, rb + g * 2048:rb + g * 2048 + 2048], in_=b_ap)
                            v.maybe_drain_then_inc((SV.sem, 1))
                        on("vector", fv)
                        tgv[g] = SV.bump()
                    tgv_all.update({(tt, g): t for g, t in tgv.items()})
                    tro = {23: tgv[2]}
                    def ft(t, tt=tt):
                        if tt == 0:
                            WP(t)
                        t.wait_ge(SV.sem, tro[23])
                        if tt > 0:
                            t.wait_ge(SA.sem, SA.v)
                            t.wait_ge(SV.sem, SV.v)
                        rb2 = (tt % 2) * 6144
                        for m in range(8):
                            for k in range(24):
                                t.matmul(out=PS7[:, m * 256:m * 256 + 256],
                                         lhsT=AR_A[:, k * 1024 + m * 128:k * 1024 + m * 128 + 128],
                                         rhs=AR_W[:, rb2 + k * 256:rb2 + k * 256 + 256],
                                         start=(k == 0), stop=(k == 23))
                        t.maybe_drain_then_inc((ST.sem, 1))
                    on("tensor", ft)
                    tmm = ST.bump()
                    tMMc[tt] = tmm
                    def fs(s, tt=tt, tmm=tmm):
                        if tt == 0:
                            WP(s)
                        s.wait_ge(ST.sem, tmm)
                        for m in range(8):
                            s.copy(out=AR_C[:, m * 256:m * 256 + 256],
                                   in_=PS7[:, m * 256:m * 256 + 256])
                        s.maybe_drain_then_inc((SA.sem, 1))
                    on("scalar", fs)
                    tev = SA.bump()
                    for tb in range(2):
                        def ft2(t, tt=tt, tb=tb, tev=tev):
                            t.wait_ge(SA.sem, tev)
                            if tb == 1:
                                t.wait_ge(SV.sem, SV.v)
                            for m in range(8):
                                t.transpose(out=PS7[:, 2048 + m * 128:2048 + m * 128 + 128],
                                            in_=AR_C[:, m * 256 + tb * 128:
                                                      m * 256 + tb * 128 + 128],
                                            identity=idf[:])
                            t.maybe_drain_then_inc((ST.sem, 1))
                        on("tensor", ft2)
                        ttr = ST.bump()
                        def fsy2(sy, tt=tt, tb=tb):
                            row = tt * 256 + tb * 128
                            sy.dma_start(out=xt[:, 0:D], in_=x_src[row:row + 128, :]
                                         ).then_inc(LD.sem, 16)
                            LD.bump(16)
                        on("sync", fsy2)
                        tlx = LD.v
                        def fv3(v, tt=tt, tb=tb, ttr=ttr, tlx=tlx):
                            v.wait_ge(ST.sem, ttr)
                            v.wait_ge(LD.sem, tlx)
                            v.scalar_tensor_tensor(out=xt[:, 0:D], in0=PS7[:, 2048:3072],
                                                   scalar=bc[:, 1:2], in1=xt[:, 0:D],
                                                   op0=Alu.mult, op1=Alu.add)
                            v.maybe_drain_then_inc((SV.sem, 1))
                        on("vector", fv3)
                        tvx = SV.bump()
                        def fsy3(sy, tt=tt, tb=tb, tvx=tvx):
                            row = tt * 256 + tb * 128
                            sy.wait_ge(SV.sem, tvx)
                            sy.dma_start(out=x_dst[row:row + 128, :], in_=xt[:, 0:D]
                                         ).then_inc(LD.sem, 16)
                            LD.bump(16)
                        on("sync", fsy3)
                barrier()

            def attn_sublayer(x_src, x_dst):
                ln_stats(x_src, want_minmax=False)
                barrier()
                def fldw(g):
                    WP(g)
                    for k in range(8):
                        g.dma_start(out=AR_W[:, k * 1024:k * 1024 + 1024],
                                    in_=Wq_d[k * 128:(k + 1) * 128, :]).then_inc(GD.sem, 16)
                        GD.bump(16)
                    for k in range(8):
                        g.dma_start(out=AR_W[:, 8192 + k * 256:8192 + k * 256 + 256],
                                    in_=Wdn_d[k * 128:(k + 1) * 128, :]).then_inc(GD.sem, 16)
                        GD.bump(16)
                    for k in range(2):
                        g.dma_start(out=AR_W[:, 10240 + k * 2048:10240 + k * 2048 + 2048],
                                    in_=Wua_d[k * 128:(k + 1) * 128, :]).then_inc(GD.sem, 16)
                        GD.bump(16)
                    for k in range(8):
                        g.dma_start(out=AR_W[:, 14336 + k * 1024:14336 + k * 1024 + 1024],
                                    in_=Wo_d[k * 128:(k + 1) * 128, :]).then_inc(GD.sem, 16)
                        GD.bump(16)
                    g.maybe_drain_then_inc((SG.sem, 1))
                on("gpsimd", fldw)
                tgw = SG.bump()
                def flb(sy):
                    WP(sy)
                    sy.dma_start(out=bias1[0:1, 0:1024], in_=bq_d).then_inc(LD.sem, 16)
                    sy.dma_start(out=bias1[0:1, 1024:1280], in_=bdn_d).then_inc(LD.sem, 16)
                    sy.dma_start(out=bias1[0:1, 1280:3328], in_=bua_d).then_inc(LD.sem, 16)
                    sy.dma_start(out=bias1[0:1, 3328:4352], in_=bo_d).then_inc(LD.sem, 16)
                    LD.bump(64)
                on("sync", flb)
                tlb = LD.v
                def ftb1(t, tlb=tlb):
                    WP(t)
                    t.wait_ge(LD.sem, tlb)
                    for j in range(7):
                        t.matmul(out=PS7[:, j * 512:j * 512 + 512], lhsT=ones1[:],
                                 rhs=bias1[0:1, j * 512:j * 512 + 512], start=True, stop=True)
                    t.maybe_drain_then_inc((ST.sem, 1))
                on("tensor", ftb1)
                ST.bump()
                def fvb1(v, tgt=ST.v):
                    WP(v)
                    v.wait_ge(ST.sem, tgt)
                    v.tensor_copy(out=AR_C[:, 0:3328], in_=PS7[:, 0:3328])
                    v.maybe_drain_then_inc((SV.sem, 1))
                on("vector", fvb1)
                SV.bump()
                def ftb2(t, tgt=SV.v):
                    t.wait_ge(SV.sem, tgt)
                    t.matmul(out=PS7[:, 0:512], lhsT=ones1[:], rhs=bias1[0:1, 3328:3840],
                             start=True, stop=True)
                    t.matmul(out=PS7[:, 512:1024], lhsT=ones1[:], rhs=bias1[0:1, 3840:4352],
                             start=True, stop=True)
                    t.maybe_drain_then_inc((ST.sem, 1))
                on("tensor", ftb2)
                ST.bump()
                def fvb2(v, tgt=ST.v):
                    v.wait_ge(ST.sem, tgt)
                    v.tensor_copy(out=AR_B[:, 0:1024], in_=PS7[:, 0:1024])
                    v.maybe_drain_then_inc((SV.sem, 1))
                on("vector", fvb2)
                SV.bump()
                tbias = SV.v
                tL = {}
                tV = {}
                tT = {}
                tA = {}
                for i in range(NT):
                    off = (i % 2) * D
                    def fsy(sy, i=i, off=off):
                        if i == 0:
                            WP(sy)
                        if i >= 1:
                            sy.wait_ge(SA.sem, tA[i - 1])
                        sy.dma_start(out=xt[:, off:off + D],
                                     in_=x_src[i * 128:(i + 1) * 128, :]).then_inc(LD.sem, 16)
                        tL[i] = LD.bump(16)
                    on("sync", fsy)
                    def fv(v, i=i, off=off):
                        if i == 0:
                            WP(v)
                            v.wait_ge(SV.sem, tbias)
                        v.wait_ge(LD.sem, tL[i])
                        if i > 0:
                            v.wait_ge(ST.sem, tT[i - 1])
                        v.tensor_scalar(out=lnsc[:], in0=xt[:, off:off + D],
                                        scalar1=statM[:, i:i + 1], scalar2=statR[:, i:i + 1],
                                        op0=Alu.add, op1=Alu.mult)
                        v.tensor_copy(out=rxb[:, 0:D], in_=lnsc[:])
                        v.maybe_drain_then_inc((SV.sem, 1))
                        tV[i] = SV.bump()
                    on("vector", fv)
                    def ft(t, i=i):
                        if i == 0:
                            WP(t)
                        t.wait_ge(SV.sem, tV[i])
                        if i > 0:
                            t.wait_ge(SA.sem, tA[i - 1])
                        for k in range(8):
                            t.transpose(out=PT[:, k * 128:k * 128 + 128],
                                        in_=rxb[:, k * 128:k * 128 + 128], identity=idb[:])
                        t.maybe_drain_then_inc((ST.sem, 1))
                        tT[i] = ST.bump()
                    on("tensor", ft)
                    def fs(s, i=i):
                        if i == 0:
                            WP(s)
                        s.wait_ge(ST.sem, tT[i])
                        for k in range(8):
                            s.copy(out=AR_A[:, k * 4096 + i * 128:k * 4096 + i * 128 + 128],
                                   in_=PT[:, k * 128:k * 128 + 128])
                        s.maybe_drain_then_inc((SA.sem, 1))
                        tA[i] = SA.bump()
                    on("scalar", fs)
                def fwg(t, tgw=tgw):
                    t.wait_ge(SG.sem, tgw)
                on("tensor", fwg)
                barrier()

                kv3 = kvt[:].rearrange("p (g c) -> p g c", c=128)
                tQE = {}
                tKVE = {}
                tSCD = {}
                tAVD = {}
                tOPE = {}
                tP1 = {}
                tP2 = {}
                tP3 = {}
                tP4 = {}
                tP5 = {}
                tAO = {}
                tKLT = {}
                def emit_ft1(i):
                    def ft1(t, i=i):
                        if i == 0:
                            WP(t)
                        else:
                            t.wait_ge(SA.sem, tQE[i - 1])
                            t.wait_ge(SV.sem, tKLT[i - 1])
                        for k in range(8):
                            lhsT = AR_A[:, k * 4096 + i * 128:k * 4096 + i * 128 + 128]
                            t.matmul(out=PS7[:, 0:512], lhsT=lhsT,
                                     rhs=AR_W[:, k * 1024:k * 1024 + 512],
                                     start=(k == 0), stop=False)
                            t.matmul(out=PS7[:, 512:1024], lhsT=lhsT,
                                     rhs=AR_W[:, k * 1024 + 512:k * 1024 + 1024],
                                     start=(k == 0), stop=False)
                        t.matmul(out=PS7[:, 0:512], lhsT=ones1[:], rhs=bias1[0:1, 0:512],
                                 start=False, stop=True)
                        t.matmul(out=PS7[:, 512:1024], lhsT=ones1[:], rhs=bias1[0:1, 512:1024],
                                 start=False, stop=True)
                        for k in range(8):
                            lhsT = AR_A[:, k * 4096 + i * 128:k * 4096 + i * 128 + 128]
                            t.matmul(out=PS7[:, 3072:3200],
                                     lhsT=AR_W[:, 8192 + k * 256:8192 + k * 256 + 128],
                                     rhs=lhsT, start=(k == 0), stop=False)
                        t.matmul(out=PS7[:, 3072:3200], lhsT=bias1[0:1, 1024:1152],
                                 rhs=ones1[0:1, 0:128], start=False, stop=True)
                        for k in range(8):
                            lhsT = AR_A[:, k * 4096 + i * 128:k * 4096 + i * 128 + 128]
                            t.matmul(out=PS7[:, 3200:3328],
                                     lhsT=AR_W[:, 8192 + k * 256 + 128:8192 + k * 256 + 256],
                                     rhs=lhsT, start=(k == 0), stop=False)
                        t.matmul(out=PS7[:, 3200:3328], lhsT=bias1[0:1, 1152:1280],
                                 rhs=ones1[0:1, 0:128], start=False, stop=True)
                        t.maybe_drain_then_inc((ST.sem, 1))
                    on("tensor", ft1)
                    tP1[i] = ST.bump()

                emit_ft1(0)
                for i in range(NT):
                    def fa1(s, i=i):
                        if i == 0:
                            WP(s)
                        s.wait_ge(ST.sem, tP1[i])
                        if i > 0:
                            s.wait_ge(SV.sem, tSCD[i - 1])
                        s.copy(out=qt[:], in_=PS7[:, 0:1024])
                        s.maybe_drain_then_inc((SA.sem, 1))
                    on("scalar", fa1)
                    tQE[i] = SA.bump()
                    def fvk(v, i=i):
                        if i == 0:
                            WP(v)
                        v.wait_ge(ST.sem, tP1[i])
                        v.tensor_copy(out=kltT[:], in_=PS7[:, 3072:3328])
                        v.maybe_drain_then_inc((SV.sem, 1))
                    on("vector", fvk)
                    tKLT[i] = SV.bump()
                    def ft3(t, i=i):
                        t.wait_ge(SV.sem, tKLT[i])
                        if i > 0:
                            t.wait_ge(SA.sem, tKVE[i - 1])
                            t.wait_ge(SA.sem, tOPE[i - 1])
                        for k2 in range(2):
                            lhsT = kltT[:, k2 * 128:k2 * 128 + 128]
                            for n in range(4):
                                t.matmul(out=PS7[:, 1024 + n * 512:1024 + n * 512 + 512],
                                         lhsT=lhsT,
                                         rhs=AR_W[:, 10240 + k2 * 2048 + n * 512:
                                                   10240 + k2 * 2048 + n * 512 + 512],
                                         start=(k2 == 0), stop=False)
                        for n in range(4):
                            t.matmul(out=PS7[:, 1024 + n * 512:1024 + n * 512 + 512],
                                     lhsT=ones1[:],
                                     rhs=bias1[0:1, 1280 + n * 512:1280 + n * 512 + 512],
                                     start=False, stop=True)
                        t.maybe_drain_then_inc((ST.sem, 1))
                    on("tensor", ft3)
                    tP3[i] = ST.bump()
                    def fa3(s, i=i):
                        s.wait_ge(ST.sem, tP3[i])
                        if i > 0:
                            s.wait_ge(SV.sem, tAVD[i - 1])
                        s.copy(out=kvt[:], in_=PS7[:, 1024:3072])
                        s.maybe_drain_then_inc((SA.sem, 1))
                    on("scalar", fa3)
                    tKVE[i] = SA.bump()
                    def fv3(v, i=i):
                        if i == 0:
                            WP(v)
                        v.wait_ge(SA.sem, tKVE[i])
                        for h in range(16):
                            qh = qt[:, h * 64:h * 64 + 64].unsqueeze(1).broadcast_to((128, 16, 64))
                            v.tensor_tensor(out=prod[:].rearrange("p (g c) -> p g c", c=64),
                                            in0=qh, in1=kv3[:, :, 0:64], op=Alu.mult)
                            v.tensor_reduce(out=sco[:, h * 16:h * 16 + 16],
                                            in_=prod[:].rearrange("p (g c) -> p g c", c=64),
                                            axis=AX.X, op=Alu.add)
                        v.tensor_reduce(out=smax[:], in_=sco[:].rearrange("p (h g) -> p h g", g=16),
                                        axis=AX.X, op=Alu.max)
                        vgap(v)
                        v.tensor_tensor(out=sco[:].rearrange("p (h g) -> p h g", g=16),
                                        in0=sco[:].rearrange("p (h g) -> p h g", g=16),
                                        in1=smax[:].unsqueeze(2).broadcast_to((128, 16, 16)),
                                        op=Alu.subtract)
                        v.maybe_drain_then_inc((SV.sem, 1))
                    on("vector", fv3)
                    tSCD[i] = SV.bump()
                    def fsc(s, i=i):
                        s.wait_ge(SV.sem, tSCD[i])
                        s.activation(out=sco[:], in_=sco[:], func=Act.Exp, scale=0.125)
                        s.maybe_drain_then_inc((SA.sem, 1))
                    on("scalar", fsc)
                    t7 = SA.bump()
                    def fv4(v, i=i, t7=t7):
                        v.wait_ge(SA.sem, t7)
                        v.tensor_reduce(out=rsum[:], in_=sco[:].rearrange("p (h g) -> p h g", g=16),
                                        axis=AX.X, op=Alu.add)
                        vgap(v)
                        v.reciprocal(out=rden[:], in_=rsum[:])
                        vgap(v)
                        v.tensor_tensor(out=sco[:].rearrange("p (h g) -> p h g", g=16),
                                        in0=sco[:].rearrange("p (h g) -> p h g", g=16),
                                        in1=rden[:].unsqueeze(2).broadcast_to((128, 16, 16)),
                                        op=Alu.mult)
                        sc0 = sco[:]
                        for m in [0] + [mm for mm in range(-15, 16) if mm != 0]:
                            hs = max(0, m)
                            cnt = 16 - abs(m)
                            a_ap = bass.AP(sc0.tensor, sc0.offset + 17 * hs - m,
                                           [[256, 128], [17, cnt], [0, 64]])
                            v_ap = bass.AP(kvt[:].tensor, kvt[:].offset + 128 * (hs - m) + 64,
                                           [[2048, 128], [128, cnt], [1, 64]])
                            if m == 0:
                                o_ap = acc[:].rearrange("p (h c) -> p h c", c=64)
                                v.tensor_tensor(out=o_ap, in0=a_ap, in1=v_ap, op=Alu.mult)
                            else:
                                p_ap = bass.AP(prod[:].tensor, prod[:].offset + 64 * hs,
                                               [[1024, 128], [64, cnt], [1, 64]])
                                oa = bass.AP(acc[:].tensor, acc[:].offset + 64 * hs,
                                             [[1024, 128], [64, cnt], [1, 64]])
                                v.tensor_tensor(out=p_ap, in0=a_ap, in1=v_ap, op=Alu.mult)
                                v.tensor_tensor(out=oa, in0=oa, in1=p_ap, op=Alu.add)
                        v.tensor_copy(out=accb[:, 0:D], in_=acc[:])
                        v.maybe_drain_then_inc((SV.sem, 1))
                    on("vector", fv4)
                    tAVD[i] = SV.bump()
                    if i + 1 < NT:
                        emit_ft1(i + 1)
                    def ft4(t, i=i):
                        t.wait_ge(SV.sem, tAVD[i])
                        for k in range(8):
                            t.transpose(out=PT[:, k * 128:k * 128 + 128],
                                        in_=accb[:, k * 128:k * 128 + 128], identity=idb[:])
                        t.maybe_drain_then_inc((ST.sem, 1))
                    on("tensor", ft4)
                    tP4[i] = ST.bump()
                    def fsa(s, i=i):
                        s.wait_ge(ST.sem, tP4[i])
                        s.copy(out=aoT[:], in_=PT[:, 0:1024])
                        s.maybe_drain_then_inc((SA.sem, 1))
                    on("scalar", fsa)
                    tAO[i] = SA.bump()
                    def ft5(t, i=i):
                        t.wait_ge(SA.sem, tAO[i])
                        for k in range(8):
                            lhsT = aoT[:, k * 128:k * 128 + 128]
                            t.matmul(out=PS7[:, 1024:1536], lhsT=lhsT,
                                     rhs=AR_W[:, 14336 + k * 1024:14336 + k * 1024 + 512],
                                     start=(k == 0), stop=False)
                            t.matmul(out=PS7[:, 1536:2048], lhsT=lhsT,
                                     rhs=AR_W[:, 14336 + k * 1024 + 512:14336 + k * 1024 + 1024],
                                     start=(k == 0), stop=False)
                        t.matmul(out=PS7[:, 1024:1536], lhsT=ones1[:],
                                 rhs=bias1[0:1, 3328:3840], start=False, stop=True)
                        t.matmul(out=PS7[:, 1536:2048], lhsT=ones1[:],
                                 rhs=bias1[0:1, 3840:4352], start=False, stop=True)
                        t.maybe_drain_then_inc((ST.sem, 1))
                    on("tensor", ft5)
                    tP5[i] = ST.bump()
                    def fa5(s, i=i):
                        s.wait_ge(ST.sem, tP5[i])
                        s.copy(out=lnsc[:], in_=PS7[:, 1024:2048])
                        s.maybe_drain_then_inc((SA.sem, 1))
                    on("scalar", fa5)
                    tOPE[i] = SA.bump()
                    def fsy(sy, i=i):
                        sy.dma_start(out=xt[:, 0:D], in_=x_src[i * 128:(i + 1) * 128, :]
                                     ).then_inc(LD.sem, 16)
                        LD.bump(16)
                    on("sync", fsy)
                    t12 = LD.v
                    def fv5(v, i=i, t12=t12):
                        v.wait_ge(SA.sem, tOPE[i])
                        v.wait_ge(LD.sem, t12)
                        v.tensor_tensor(out=xt[:, 0:D], in0=lnsc[:], in1=xt[:, 0:D], op=Alu.add)
                        v.maybe_drain_then_inc((SV.sem, 1))
                    on("vector", fv5)
                    t13 = SV.bump()
                    def fsy2(sy, i=i, t13=t13):
                        sy.wait_ge(SV.sem, t13)
                        sy.dma_start(out=x_dst[i * 128:(i + 1) * 128, :], in_=xt[:, 0:D]
                                     ).then_inc(LD.sem, 16)
                        LD.bump(16)
                    on("sync", fsy2)
                barrier()

            if KSTOP >= 5:
                ffn_sublayer(x_in, hbm_x, 0, upto=KSTOP - 5)
            if KSTOP >= 13:
                attn_sublayer(hbm_x, hbm_x)
            if KSTOP >= 14:
                ffn_sublayer(hbm_x, y_out, 4, upto=(99 if KSTOP >= 99 else KSTOP - 14))

            def fdbg(g):
                WP(g)
                g.dma_start(out=dbg_out[0:1, 0:64], in_=sm[:]).then_inc(GD.sem, 16)
                GD.bump(16)
                g.dma_start(out=dbg_out[0:128, 64:72], in_=bc[:]).then_inc(GD.sem, 16)
                GD.bump(16)
                g.dma_start(out=dbg_out[0:128, 72:104], in_=statM[:]).then_inc(GD.sem, 16)
                GD.bump(16)
                g.dma_start(out=dbg_out[0:128, 104:136], in_=statR[:]).then_inc(GD.sem, 16)
                GD.bump(16)
                g.dma_start(out=dbg_out[0:128, 136:168], in_=lmax[:]).then_inc(GD.sem, 16)
                GD.bump(16)
                g.dma_start(out=dbg_out[0:128, 168:200], in_=colS[:]).then_inc(GD.sem, 16)
                GD.bump(16)
                g.dma_start(out=dbg_out[0:128, 200:232], in_=colQ[:]).then_inc(GD.sem, 16)
                GD.bump(16)
                kd = os.environ.get('KDUMP', '')
                if kd == 'hbm_x':
                    g.dma_start(out=y_out, in_=hbm_x).then_inc(GD.sem, 16)
                    GD.bump(16)
                elif kd in ('spg', 'spu', 'spv', 'spp'):
                    s = {'spg': sp_gate, 'spu': sp_up, 'spv': sp_g2v, 'spp': sp_p}[kd]
                    g.dma_start(out=y_out, in_=s[0:8]).then_inc(GD.sem, 16)
                    GD.bump(16)
                elif kd in ('wgT', 'wuT'):
                    s = {'wgT': hb_wgT, 'wuT': hb_wuT}[kd]
                    g.dma_start(out=y_out[0:3072, :], in_=s).then_inc(GD.sem, 16)
                    GD.bump(16)
                elif kd == 'wdT':
                    g.dma_start(out=y_out[0:3072, :], in_=hb_wdT).then_inc(GD.sem, 16)
                    GD.bump(16)
                elif kd == 'arA':
                    g.dma_start(out=y_out, in_=AR_A[:]).then_inc(GD.sem, 16)
                    GD.bump(16)
                elif kd == 'arB':
                    g.dma_start(out=y_out[0:512, :], in_=AR_B[:]).then_inc(GD.sem, 16)
                    GD.bump(16)
                elif kd == 'attn':
                    g.dma_start(out=y_out[0:128, :], in_=qt[:]).then_inc(GD.sem, 16)
                    GD.bump(16)
                    g.dma_start(out=y_out[128:384, :], in_=kvt[:]).then_inc(GD.sem, 16)
                    GD.bump(16)
                    g.dma_start(out=y_out[384:416, :], in_=sco[:]).then_inc(GD.sem, 16)
                    GD.bump(16)
                    g.dma_start(out=y_out[416:544, :], in_=acc[:]).then_inc(GD.sem, 16)
                    GD.bump(16)
                elif kd == 'arW':
                    g.dma_start(out=y_out[0:768, :], in_=AR_W[:, 0:6144]).then_inc(GD.sem, 16)
                    GD.bump(16)
            on("gpsimd", fdbg)
            def ffin(sy):
                WP(sy)
                sy.wait_ge(LD.sem, LD.v)
                sy.wait_ge(GD.sem, GD.v)
            on("sync", ffin)
            barrier()
    return nc


_NC_CACHE = [None]

def kernel(L0_attn_x, L0_attn_Wq, L0_attn_bq, L0_attn_Wdown, L0_attn_bdown,
           L0_attn_Wup, L0_attn_bup, L0_attn_Wout, L0_attn_bout,
           L0_ffn_w_gate, L0_ffn_w_up, L0_ffn_w_down):
    if _NC_CACHE[0] is None:
        _NC_CACHE[0] = build_nc()
    nc = _NC_CACHE[0]
    x = np.ascontiguousarray(np.asarray(L0_attn_x, dtype=np.float32))
    common = {
        "Wq": np.ascontiguousarray(np.asarray(L0_attn_Wq, np.float32)),
        "bq": np.asarray(L0_attn_bq, np.float32).reshape(1, D),
        "Wdn": np.ascontiguousarray(np.asarray(L0_attn_Wdown, np.float32)),
        "bdn": np.asarray(L0_attn_bdown, np.float32).reshape(1, 256),
        "Wua": np.ascontiguousarray(np.asarray(L0_attn_Wup, np.float32)),
        "bua": np.asarray(L0_attn_bup, np.float32).reshape(1, 2048),
        "Wo": np.ascontiguousarray(np.asarray(L0_attn_Wout, np.float32)),
        "bo": np.asarray(L0_attn_bout, np.float32).reshape(1, D),
        "wg": np.ascontiguousarray(np.asarray(L0_ffn_w_gate, np.float32)),
        "wu": np.ascontiguousarray(np.asarray(L0_ffn_w_up, np.float32)),
        "wd": np.ascontiguousarray(np.asarray(L0_ffn_w_down, np.float32)),
    }
    in_maps = [dict(common, x=np.ascontiguousarray(x[c])) for c in range(8)]
    res = run_bass_kernel_spmd(nc, in_maps, core_ids=list(range(8)),
                               trace=(os.environ.get('KTRACE') == '1'))
    if os.environ.get('KTRACE') == '1':
        print('EXEC_NS', res.exec_time_ns)
    if os.environ.get('KDBG'):
        np.save('/tmp/dbg0.npy', np.asarray(res.results[0]['dbg']))
    out = np.stack([np.asarray(res.results[c]["y"], dtype=np.float32) for c in range(8)], axis=0)
    return out


# version 17 (final)
# speedup vs baseline: 1.1696x; 1.0063x over previous
import os
import numpy as np
import contextlib
import concourse.bass as bass
import concourse.mybir as mybir
from concourse.bass_utils import run_bass_kernel_spmd

F32 = mybir.dt.float32
BF16 = mybir.dt.bfloat16
Alu = mybir.AluOpType
Act = mybir.ActivationFunctionType
AX = mybir.AxisListType

T = 4096
D = 1024
HID = 3072
NT = T // 128
MAGIC = 12582912.0
EPS = 1e-5
QMAX = 32767.0
TWO16 = 65536.0
INV1024 = 1.0 / 1024.0


class Tok:
    def __init__(self, sem, name):
        self.sem = sem
        self.v = 0
        self.name = name
    def bump(self, by=1):
        self.v += by
        return self.v


KSTOP = int(os.environ.get('KSTOP', '99'))


def build_nc():
    nc = bass.Bass("TRN2", target_bir_lowering=False)
    x_in = nc.dram_tensor("x", [T, D], F32, kind="ExternalInput").ap()
    Wq_d = nc.dram_tensor("Wq", [D, D], F32, kind="ExternalInput").ap()
    bq_d = nc.dram_tensor("bq", [1, D], F32, kind="ExternalInput").ap()
    Wdn_d = nc.dram_tensor("Wdn", [D, 256], F32, kind="ExternalInput").ap()
    bdn_d = nc.dram_tensor("bdn", [1, 256], F32, kind="ExternalInput").ap()
    Wua_d = nc.dram_tensor("Wua", [256, 2048], F32, kind="ExternalInput").ap()
    bua_d = nc.dram_tensor("bua", [1, 2048], F32, kind="ExternalInput").ap()
    Wo_d = nc.dram_tensor("Wo", [D, D], F32, kind="ExternalInput").ap()
    bo_d = nc.dram_tensor("bo", [1, D], F32, kind="ExternalInput").ap()
    wg_d = nc.dram_tensor("wg", [HID, D], F32, kind="ExternalInput").ap()
    wu_d = nc.dram_tensor("wu", [HID, D], F32, kind="ExternalInput").ap()
    wd_d = nc.dram_tensor("wd", [D, HID], F32, kind="ExternalInput").ap()
    y_out = nc.dram_tensor("y", [T, D], F32, kind="ExternalOutput").ap()
    dbg_out = nc.dram_tensor("dbg", [128, 256], F32, kind="ExternalOutput").ap()

    hbm_x = nc.dram_tensor("hbm_x", [T, D], F32).ap()
    hb_wgT = nc.dram_tensor("hb_wgT", [8, 128, HID], BF16).ap()
    hb_wuT = nc.dram_tensor("hb_wuT", [8, 128, HID], BF16).ap()
    hb_wdT = nc.dram_tensor("hb_wdT", [24, 128, D], BF16).ap()
    sp_gate = nc.dram_tensor("sp_gate", [24, 128, T], F32).ap()
    sp_up = nc.dram_tensor("sp_up", [24, 128, T], F32).ap()
    sp_g2v = nc.dram_tensor("sp_g2v", [24, 128, T], F32).ap()
    sp_p = nc.dram_tensor("sp_p", [24, 128, T], F32).ap()
    cc_ins = [nc.dram_tensor(f"cc_in{i}", [1, 2], F32).ap() for i in range(8)]
    cc_outs = [nc.dram_tensor(f"cc_out{i}", [8, 2], F32).ap() for i in range(8)]

    es = contextlib.ExitStack()
    with es:
        def sbuf(name, shape, dt=F32):
            return es.enter_context(nc.sbuf_tensor(name, shape, dt))
        AR_A = sbuf("AR_A", [128, 32768], BF16)
        AR_B = sbuf("AR_B", [128, 4096], F32)
        AR_C = sbuf("AR_C", [128, 4096], F32)
        AR_W = sbuf("AR_W", [128, 24576], BF16)
        xt = sbuf("xt", [128, 2 * D])
        lnsc = sbuf("lnsc", [128, D])
        rxb = sbuf("rxb", [128, 2 * D], BF16)
        rxsc = lnsc
        statM = sbuf("statM", [128, NT])
        statR = sbuf("statR", [128, NT])
        colS = sbuf("colS", [128, NT])
        colQ = sbuf("colQ", [128, NT])
        lmax = sbuf("lmax", [128, 32])
        red1 = sbuf("red1", [128, 1])
        tmp1 = sbuf("tmp1", [128, 1])
        flip = sbuf("flip", [1, 128])
        sm = sbuf("sm", [1, 64])
        agf = sbuf("agf", [1, 16])
        ones1 = sbuf("ones1", [1, 128])
        bcv = sbuf("bcv", [1, 8])
        bc = sbuf("bc", [128, 8])
        sevr = sbuf("sevr", [128, 1])
        epsc = sbuf("epsc", [128, 1])
        bmax = sbuf("bmax", [128, 96])
        bsS = sbuf("bsS", [128, 96])
        brs = sbuf("brs", [128, 96])
        idf = sbuf("idf", [128, 128])
        idb = sbuf("idb", [128, 128], BF16)
        qt = sbuf("qt", [128, D])
        kvt = sbuf("kvt", [128, 2048])
        klt = sbuf("klt", [128, 256])
        kltT = sbuf("kltT", [128, 256], BF16)
        prod = sbuf("prod", [128, D])
        sco = sbuf("sco", [128, 256])
        smax = sbuf("smax", [128, 16])
        rsum = sbuf("rsum", [128, 16])
        rden = sbuf("rden", [128, 16])
        acc = sbuf("acc", [128, D])
        accb = rxb
        aoT = sbuf("aoT", [128, 1024], BF16)
        rof = sbuf("rof", [128, 256])
        bias1 = sbuf("bias1", [1, 4352])
        PS7 = es.enter_context(nc.psum_tensor("PS7", [128, 3584], F32))
        PT = es.enter_context(nc.psum_tensor("PT", [128, 1024], BF16))

        SB = Tok(es.enter_context(nc.semaphore("sB")), "B")
        LD = Tok(es.enter_context(nc.semaphore("sLD")), "LD")
        GD = Tok(es.enter_context(nc.semaphore("sGD")), "GD")
        SV = Tok(es.enter_context(nc.semaphore("sV")), "V")
        SA = Tok(es.enter_context(nc.semaphore("sA")), "A")
        ST = Tok(es.enter_context(nc.semaphore("sT")), "T")
        SG = Tok(es.enter_context(nc.semaphore("sG")), "G")
        CCS = Tok(es.enter_context(nc.semaphore("sCC")), "CC")

        phase = [0]
        ENGS = ["sync", "vector", "scalar", "tensor", "gpsimd"]

        smc = {"r32767": 0, "stage": 1}
        def smcol(name):
            if name not in smc:
                smc[name] = len(smc)
            return sm[0:1, smc[name]:smc[name] + 1]

        def vgap(v, n=2):
            for _ in range(n):
                v.tensor_copy(out=sm[0:1, 62:63], in_=ones1[0:1, 0:1])

        with nc.Block() as block:
            def on(eng, fn):
                getattr(block, eng)(fn)

            def barrier():
                phase[0] += 1
                for e in ENGS:
                    def f(eng):
                        eng.maybe_drain_then_inc((SB.sem, 1))
                    on(e, f)

            def WP(eng):
                eng.wait_ge(SB.sem, 5 * phase[0])

            def p0g(g):
                g.memset(idf[:], 0.0)
                g.affine_select(out=idf[:], in_=idf[:], compare_op=Alu.not_equal,
                                fill=1.0, base=0, pattern=[[-1, 128]], channel_multiplier=1)
                g.memset(sevr[:], 7.0)
                g.memset(epsc[:], EPS)
                g.memset(ones1[:], 1.0)
                g.memset(sm[:], 0.0)
                g.memset(bcv[:], 0.0)
                g.maybe_drain_then_inc((SG.sem, 1))
            on("gpsimd", p0g)
            tok0 = SG.bump()
            def p0v(v):
                v.wait_ge(SG.sem, tok0)
                v.tensor_copy(out=idb[:], in_=idf[:])
                v.reciprocal(out=sevr[:], in_=sevr[:])
                v.tensor_scalar(out=smcol("stage"), in0=ones1[0:1, 0:1], scalar1=QMAX,
                                scalar2=None, op0=Alu.mult)
                vgap(v)
                v.reciprocal(out=smcol("r32767"), in_=smcol("stage"))
            on("vector", p0v)
            barrier()

            def dequant_matrix(w_src, n_tiles, in_cols, hbT, tag):
                assert in_cols == 1024
                nblk = in_cols // 32
                kch = in_cols // 128
                slab_stride = n_tiles * 128
                tT = {}
                tV = {}
                tL = {}
                tA = {}
                for i in range(n_tiles):
                    wb = (i % 2) * 1024
                    qb = (i % 2) * in_cols
                    def fsy(sy, i=i, wb=wb):
                        if i == 0:
                            WP(sy)
                        if i >= 2:
                            sy.wait_ge(SV.sem, tV[i - 2])
                        sy.dma_start(out=AR_B[:, wb:wb + in_cols],
                                     in_=w_src[i * 128:(i + 1) * 128, :]).then_inc(LD.sem, 16)
                        tL[i] = LD.bump(16)
                    on("sync", fsy)
                    def fv(v, i=i, wb=wb, qb=qb):
                        if i == 0:
                            WP(v)
                        v.wait_ge(LD.sem, tL[i])
                        if i > 1:
                            v.wait_ge(ST.sem, tT[i - 2])
                        w_ap = AR_B[:, wb:wb + in_cols]
                        w3 = w_ap.rearrange("p (a b) -> p a b", b=32)
                        v.tensor_reduce(out=bmax[:, 0:nblk], in_=w3, axis=AX.X,
                                        op=Alu.max, apply_absolute_value=True)
                        vgap(v)
                        v.tensor_scalar(out=bsS[:, 0:nblk], in0=bmax[:, 0:nblk],
                                        scalar1=sevr[:, 0:1], scalar2=1e-12,
                                        op0=Alu.mult, op1=Alu.max)
                        vgap(v)
                        v.reciprocal(out=brs[:, 0:nblk], in_=bsS[:, 0:nblk])
                        vgap(v)
                        rs_b = brs[:, 0:nblk].unsqueeze(2).broadcast_to((128, nblk, 32))
                        s_b = bsS[:, 0:nblk].unsqueeze(2).broadcast_to((128, nblk, 32))
                        v.tensor_tensor(out=w3, in0=w3, in1=rs_b, op=Alu.mult)
                        v.tensor_scalar(out=w_ap, in0=w_ap, scalar1=MAGIC, scalar2=MAGIC,
                                        op0=Alu.add, op1=Alu.subtract)
                        wqb = AR_W[:, qb:qb + in_cols]
                        v.tensor_tensor(out=wqb.rearrange("p (a b) -> p a b", b=32),
                                        in0=w3, in1=s_b, op=Alu.mult)
                        v.maybe_drain_then_inc((SV.sem, 1))
                        tV[i] = SV.bump()
                    on("vector", fv)
                    def ft(t, i=i, qb=qb):
                        if i == 0:
                            WP(t)
                        t.wait_ge(SV.sem, tV[i])
                        t.wait_ge(SA.sem, SA.v)
                        for k in range(kch):
                            t.transpose(out=PT[:, k * 128:k * 128 + 128],
                                        in_=AR_W[:, qb + k * 128:qb + k * 128 + 128],
                                        identity=idb[:])
                        t.maybe_drain_then_inc((ST.sem, 1))
                    on("tensor", ft)
                    tT[i] = ST.bump()
                    def fs(s, i=i):
                        if i == 0:
                            WP(s)
                        s.wait_ge(ST.sem, tT[i])
                        for k in range(kch):
                            s.copy(out=AR_A[:, k * slab_stride + i * 128:
                                            k * slab_stride + i * 128 + 128],
                                   in_=PT[:, k * 128:k * 128 + 128])
                        s.maybe_drain_then_inc((SA.sem, 1))
                        tA[i] = SA.bump()
                    on("scalar", fs)
                def fw(sy):
                    sy.wait_ge(SA.sem, SA.v)
                    for k in range(kch):
                        sy.dma_start(out=hbT[k],
                                     in_=AR_A[:, k * slab_stride:k * slab_stride + n_tiles * 128]
                                     ).then_inc(LD.sem, 16)
                        LD.bump(16)
                    sy.wait_ge(LD.sem, LD.v)
                on("sync", fw)

            dequant_matrix(wg_d, 24, D, hb_wgT, "dg")
            barrier()
            if KSTOP >= 3:
                dequant_matrix(wu_d, 24, D, hb_wuT, "du")
            barrier()
            if KSTOP >= 4:
                for cs in range(3):
                    dequant_matrix(wd_d[:, cs * 1024:(cs + 1) * 1024], 8, 1024,
                                   [hb_wdT[cs * 8 + j] for j in range(8)], f"dd{cs}")
            barrier()

            def ln_stats(x_src, want_minmax=True):
                tL = {}
                tLv = {}
                for i in range(NT):
                    off = (i % 2) * D
                    def fsy(sy, i=i, off=off):
                        if i == 0:
                            WP(sy)
                        if i >= 2:
                            sy.wait_ge(SV.sem, tLv[i - 2])
                        sy.dma_start(out=xt[:, off:off + D],
                                     in_=x_src[i * 128:(i + 1) * 128, :]).then_inc(LD.sem, 16)
                        tL[i] = LD.bump(16)
                    on("sync", fsy)
                    def fv(v, i=i, off=off):
                        if i == 0:
                            WP(v)
                        v.wait_ge(LD.sem, tL[i])
                        v.tensor_reduce(out=colS[:, i:i + 1], in_=xt[:, off:off + D],
                                        axis=AX.X, op=Alu.add)
                        v.scalar_tensor_tensor(out=lnsc[:], in0=xt[:, off:off + D], scalar=1.0,
                                               in1=xt[:, off:off + D], op0=Alu.mult, op1=Alu.mult,
                                               accum_out=colQ[:, i:i + 1])
                        if want_minmax:
                            v.tensor_reduce(out=bmax[:, i:i + 1], in_=xt[:, off:off + D],
                                            axis=AX.X, op=Alu.max)
                            v.tensor_reduce(out=bmax[:, 32 + i:33 + i], in_=xt[:, off:off + D],
                                            axis=AX.X, op=Alu.min)
                        v.maybe_drain_then_inc((SV.sem, 1))
                        tLv[i] = SV.bump()
                    on("vector", fv)
                def fv2(v):
                    v.tensor_scalar(out=statM[:], in0=colS[:], scalar1=-INV1024,
                                    scalar2=None, op0=Alu.mult)
                    v.tensor_scalar(out=colQ[:], in0=colQ[:], scalar1=INV1024,
                                    scalar2=None, op0=Alu.mult)
                    vgap(v)
                    v.tensor_tensor(out=colS[:], in0=statM[:], in1=statM[:], op=Alu.mult)
                    vgap(v)
                    v.tensor_tensor(out=colQ[:], in0=colQ[:], in1=colS[:], op=Alu.subtract)
                    v.maybe_drain_then_inc((SV.sem, 1))
                on("vector", fv2)
                tv = SV.bump()
                def fs(s):
                    WP(s)
                    s.wait_ge(SV.sem, tv)
                    s.activation(out=statR[:], in_=colQ[:], func=Act.Sqrt, bias=epsc[:, 0:1])
                    s.maybe_drain_then_inc((SA.sem, 1))
                on("scalar", fs)
                ta = SA.bump()
                def fv3(v):
                    v.wait_ge(SA.sem, ta)
                    v.reciprocal(out=statR[:], in_=statR[:])
                on("vector", fv3)

            def do_sync(cc_idx, names2):
                def fg(g):
                    WP(g)
                    g.dma_start(out=cc_ins[cc_idx], in_=sm[0:1, 60:62]).then_inc(GD.sem, 16)
                    g.wait_ge(GD.sem, GD.bump(16))
                    g.collective_compute("AllGather", Alu.bypass,
                                         replica_groups=[list(range(8))],
                                         ins=[cc_ins[cc_idx]], outs=[cc_outs[cc_idx]]
                                         ).then_inc(CCS.sem, 1)
                    g.wait_ge(CCS.sem, CCS.bump())
                    g.dma_start(out=agf[:], in_=cc_outs[cc_idx]).then_inc(GD.sem, 16)
                    g.wait_ge(GD.sem, GD.bump(16))
                    g.maybe_drain_then_inc((SG.sem, 1))
                on("gpsimd", fg)
                tg = SG.bump()
                def fv(v):
                    WP(v)
                    v.wait_ge(SG.sem, tg)
                    for j, nm in enumerate(names2):
                        src = bass.AP(agf[:].tensor, agf[:].offset + j, [[16, 1], [2, 8]])
                        v.tensor_reduce(out=smcol(nm), in_=src, axis=AX.X, op=Alu.max)
                    vgap(v)
                on("vector", fv)

            def stage_locals(*col_aps):
                def fv(v):
                    for j, ap in enumerate(col_aps):
                        v.tensor_copy(out=sm[0:1, 60 + j:61 + j], in_=ap)
                    v.maybe_drain_then_inc((SV.sem, 1))
                on("vector", fv)
                return SV.bump()

            def local_absmax(cols_ap, out_name):
                def fv(v):
                    v.tensor_reduce(out=red1[:], in_=cols_ap, axis=AX.X, op=Alu.max)
                    v.maybe_drain_then_inc((SV.sem, 1))
                on("vector", fv)
                tv = SV.bump()
                def fg(g):
                    WP(g)
                    g.wait_ge(SV.sem, tv)
                    g.dma_start(out=flip[:], in_=red1[:]).then_inc(GD.sem, 16)
                    g.wait_ge(GD.sem, GD.bump(16))
                    g.maybe_drain_then_inc((SG.sem, 1))
                on("gpsimd", fg)
                tg = SG.bump()
                def fv2(v):
                    v.wait_ge(SG.sem, tg)
                    v.tensor_reduce(out=smcol(out_name), in_=flip[:], axis=AX.X, op=Alu.max)
                    vgap(v)
                on("vector", fv2)

            def bcast_consts(names_in):
                def fv(v):
                    for j, nm in enumerate(names_in):
                        v.tensor_copy(out=bcv[0:1, j:j + 1], in_=smcol(nm))
                    v.maybe_drain_then_inc((SV.sem, 1))
                on("vector", fv)
                tv = SV.bump()
                def ft(t):
                    WP(t)
                    t.wait_ge(SV.sem, tv)
                    t.matmul(out=PS7[:, 0:8], lhsT=ones1[:], rhs=bcv[:], start=True, stop=True)
                    t.maybe_drain_then_inc((ST.sem, 1))
                on("tensor", ft)
                tt = ST.bump()
                def fv2(v):
                    v.wait_ge(ST.sem, tt)
                    v.tensor_copy(out=bc[:, 0:8], in_=PS7[:, 0:8])
                    v.maybe_drain_then_inc((SV.sem, 1))
                on("vector", fv2)
                SV.bump()

            def ffn_sublayer(x_src, x_dst, sync_base, upto=99):
                ln_stats(x_src)
                barrier()
                if upto < 1:
                    return
                def flm(v):
                    v.tensor_tensor(out=bmax[:, 0:32], in0=bmax[:, 0:32], in1=statM[:, 0:32],
                                    op=Alu.add)
                    v.tensor_tensor(out=bmax[:, 32:64], in0=bmax[:, 32:64], in1=statM[:, 0:32],
                                    op=Alu.add)
                    vgap(v)
                    v.tensor_scalar(out=bmax[:, 32:64], in0=bmax[:, 32:64], scalar1=-1.0,
                                    scalar2=None, op0=Alu.mult)
                    vgap(v)
                    v.tensor_tensor(out=bmax[:, 0:32], in0=bmax[:, 0:32], in1=bmax[:, 32:64],
                                    op=Alu.max)
                    vgap(v)
                    v.tensor_tensor(out=lmax[:, 0:32], in0=bmax[:, 0:32], in1=statR[:, 0:32],
                                    op=Alu.mult)
                    v.maybe_drain_then_inc((SV.sem, 1))
                on("vector", flm)
                SV.bump()
                local_absmax(lmax[:, 0:NT], "lmx")
                st = stage_locals(smcol("lmx"))
                barrier()
                do_sync(sync_base + 0, ["gx"])
                def fsx(v):
                    v.tensor_scalar(out=smcol("s_x"), in0=smcol("gx"), scalar1=smcol("r32767"),
                                    scalar2=1e-12, op0=Alu.mult, op1=Alu.max)
                    vgap(v)
                    v.reciprocal(out=smcol("c1"), in_=smcol("s_x"))
                    vgap(v)
                on("vector", fsx)
                bcast_consts(["c1"])
                barrier()
                if upto < 2:
                    return
                RXL = 9
                tL = {}
                tV = {}
                tT = {}
                tA = {}
                RXS = 0
                for i in range(NT):
                    off = (i % 2) * D
                    def fsy(sy, i=i, off=off):
                        if i == 0:
                            WP(sy)
                        if RXS and i >= 1:
                            sy.wait_ge(SA.sem, tA[i - 1])
                        elif i >= 2:
                            sy.wait_ge(SV.sem, tV[i - 2])
                        sy.dma_start(out=xt[:, off:off + D],
                                     in_=x_src[i * 128:(i + 1) * 128, :]).then_inc(LD.sem, 16)
                        tL[i] = LD.bump(16)
                    on("sync", fsy)
                    def fv(v, i=i, off=off):
                        if i == 0:
                            WP(v)
                        v.wait_ge(LD.sem, tL[i])
                        if i > 1:
                            v.wait_ge(ST.sem, tT[i - 2])
                        v.tensor_tensor(out=tmp1[:], in0=statR[:, i:i + 1], in1=bc[:, 0:1],
                                        op=Alu.mult)
                        vgap(v)
                        v.tensor_scalar(out=rxsc[:], in0=xt[:, off:off + D],
                                        scalar1=statM[:, i:i + 1], scalar2=tmp1[:, 0:1],
                                        op0=Alu.add, op1=Alu.mult)
                        v.tensor_scalar(out=rxsc[:], in0=rxsc[:], scalar1=MAGIC, scalar2=MAGIC,
                                        op0=Alu.add, op1=Alu.subtract)
                        v.tensor_copy(out=rxb[:, (i % 2) * D:(i % 2) * D + D], in_=rxsc[:])
                        v.maybe_drain_then_inc((SV.sem, 1))
                        tV[i] = SV.bump()
                    on("vector", fv)
                    def ft(t, i=i):
                        if i == 0:
                            WP(t)
                        t.wait_ge(SV.sem, tV[i])
                        if i > 0:
                            t.wait_ge(SA.sem, tA[i - 1])
                        for k in range(8):
                            t.transpose(out=PT[:, k * 128:k * 128 + 128],
                                        in_=rxb[:, (i % 2) * D + k * 128:(i % 2) * D + k * 128 + 128],
                                        identity=idb[:])
                        t.maybe_drain_then_inc((ST.sem, 1))
                        tT[i] = ST.bump()
                    on("tensor", ft)
                    def fs(s, i=i):
                        if i == 0:
                            WP(s)
                        s.wait_ge(ST.sem, tT[i])
                        for k in range(8):
                            s.copy(out=AR_A[:, k * 4096 + i * 128:k * 4096 + i * 128 + 128],
                                   in_=PT[:, k * 128:k * 128 + 128])
                        s.maybe_drain_then_inc((SA.sem, 1))
                        tA[i] = SA.bump()
                    on("scalar", fs)
                barrier()

                def big_mm(hbT, sp_dst, maxname):
                    tEv = {}
                    tSp = {}
                    tMM = {}
                    tBlk = {}
                    tMGdone = {}
                    for mg in range(3):
                        wb_off = (mg % 2) * 8192
                        def fsy(sy, mg=mg, wb_off=wb_off):
                            if mg == 0:
                                WP(sy)
                            elif mg >= 2:
                                sy.wait_ge(ST.sem, tMGdone[mg - 2])
                            for k in range(8):
                                sy.dma_start(out=AR_W[:, wb_off + k * 1024:wb_off + k * 1024 + 1024],
                                             in_=hbT[k][:, mg * 1024:mg * 1024 + 1024]
                                             ).then_inc(LD.sem, 16)
                                LD.bump(16)
                        on("sync", fsy)
                        tBlk[mg] = LD.v
                        for ml in range(8):
                            m = mg * 8 + ml
                            for tg in range(2):
                                def ft(t, mg=mg, ml=ml, tg=tg, m=m, wb_off=wb_off):
                                    if mg == 0 and ml == 0 and tg == 0:
                                        WP(t)
                                    if ml == 0 and tg == 0:
                                        t.wait_ge(LD.sem, tBlk[mg])
                                    prev = (m, tg - 1) if tg == 1 else (m - 1, 1)
                                    if prev in tEv:
                                        t.wait_ge(SA.sem, tEv[prev])
                                    for k in range(8):
                                        lhsT = AR_W[:, wb_off + k * 1024 + ml * 128:
                                                    wb_off + k * 1024 + ml * 128 + 128]
                                        for tt in range(4):
                                            ttok = tg * 4 + tt
                                            rhs = AR_A[:, k * 4096 + ttok * 512:k * 4096 + ttok * 512 + 512]
                                            t.matmul(out=PS7[:, tt * 512:tt * 512 + 512],
                                                     lhsT=lhsT, rhs=rhs,
                                                     start=(k == 0), stop=(k == 7))
                                    t.maybe_drain_then_inc((ST.sem, 1))
                                on("tensor", ft)
                                tMM[(m, tg)] = ST.bump()
                                slab = AR_C if m % 2 == 0 else AR_B
                                def fs(s, m=m, tg=tg, slab=slab):
                                    if m == 0 and tg == 0:
                                        WP(s)
                                    s.wait_ge(ST.sem, tMM[(m, tg)])
                                    if tg == 0 and m >= 2:
                                        s.wait_ge(LD.sem, tSp[m - 2])
                                    for tt in range(4):
                                        ttok = tg * 4 + tt
                                        s.copy(out=slab[:, ttok * 512:ttok * 512 + 512],
                                               in_=PS7[:, tt * 512:tt * 512 + 512])
                                    s.maybe_drain_then_inc((SA.sem, 1))
                                on("scalar", fs)
                                tEv[(m, tg)] = SA.bump()
                            def fv(v, m=m, slab=slab):
                                if m == 0:
                                    WP(v)
                                v.wait_ge(SA.sem, tEv[(m, 1)])
                                v.tensor_reduce(out=lmax[:, m % 32:m % 32 + 1], in_=slab[:],
                                                axis=AX.X, op=Alu.max, apply_absolute_value=True)
                                v.maybe_drain_then_inc((SV.sem, 1))
                            on("vector", fv)
                            tVr = SV.v + 1
                            SV.bump()
                            def fsp(sy, m=m, slab=slab, tVr=tVr):
                                sy.wait_ge(SV.sem, tVr)
                                sy.dma_start(out=sp_dst[m], in_=slab[:]).then_inc(LD.sem, 16)
                                LD.bump(16)
                            on("sync", fsp)
                            tSp[m] = LD.v
                        tMGdone[mg] = ST.v
                    local_absmax(lmax[:, 0:24], maxname)

                big_mm(hb_wgT, sp_gate, "lgate")
                barrier()
                if upto < 4:
                    return
                big_mm(hb_wuT, sp_up, "lup")
                st = stage_locals(smcol("lgate"), smcol("lup"))
                barrier()
                do_sync(sync_base + 1, ["ggate", "gup"])
                def fs2v(v):
                    v.tensor_tensor(out=smcol("tg1"), in0=smcol("ggate"), in1=smcol("s_x"), op=Alu.mult)
                    v.tensor_tensor(out=smcol("tu1"), in0=smcol("gup"), in1=smcol("s_x"), op=Alu.mult)
                    vgap(v)
                    v.tensor_scalar(out=smcol("s_gate"), in0=smcol("tg1"), scalar1=smcol("r32767"),
                                    scalar2=1e-12, op0=Alu.mult, op1=Alu.max)
                    v.tensor_scalar(out=smcol("s_up"), in0=smcol("tu1"), scalar1=smcol("r32767"),
                                    scalar2=1e-12, op0=Alu.mult, op1=Alu.max)
                    vgap(v)
                    v.reciprocal(out=smcol("tg2"), in_=smcol("s_gate"))
                    v.reciprocal(out=smcol("tu2"), in_=smcol("s_up"))
                    vgap(v)
                    v.tensor_tensor(out=smcol("cg1"), in0=smcol("s_x"), in1=smcol("tg2"), op=Alu.mult)
                    v.tensor_tensor(out=smcol("cup"), in0=smcol("s_x"), in1=smcol("tu2"), op=Alu.mult)
                    vgap(v)
                on("vector", fs2v)
                bcast_consts(["cg1", "s_gate", "cup"])
                barrier()
                if upto < 5:
                    return

                tLh = {}
                tVh = {}
                tAh = {}
                tSh = {}
                def A_load(h):
                    m, hf, gb = h // 2, h % 2, (h % 2) * 2048
                    def fsy(sy, h=h, m=m, hf=hf, gb=gb):
                        if h == 0:
                            WP(sy)
                        if h >= 2:
                            sy.wait_ge(LD.sem, tSh[h - 2])
                        sy.dma_start(out=AR_B[:, gb:gb + 2048],
                                     in_=sp_gate[m][:, hf * 2048:hf * 2048 + 2048]
                                     ).then_inc(LD.sem, 16)
                        tLh[h] = LD.bump(16)
                    on("sync", fsy)
                def A_rg1(h):
                    gb = (h % 2) * 2048
                    def fv(v, h=h, gb=gb):
                        if h == 0:
                            WP(v)
                        v.wait_ge(LD.sem, tLh[h])
                        g_ap = AR_B[:, gb:gb + 2048]
                        v.tensor_scalar(out=g_ap, in0=g_ap, scalar1=bc[:, 0:1],
                                        scalar2=MAGIC, op0=Alu.mult, op1=Alu.add)
                        v.tensor_scalar(out=g_ap, in0=g_ap, scalar1=-MAGIC,
                                        scalar2=None, op0=Alu.add)
                        v.maybe_drain_then_inc((SV.sem, 1))
                    on("vector", fv)
                    return SV.bump()
                def A_sig(h, t1):
                    gb = (h % 2) * 2048
                    def fsg(s, h=h, gb=gb, t1=t1):
                        if h == 0:
                            WP(s)
                        s.wait_ge(SV.sem, t1)
                        s.activation(out=AR_C[:, gb:gb + 2048], in_=AR_B[:, gb:gb + 2048],
                                     func=Act.Sigmoid, scale=bc[:, 1:2])
                        s.maybe_drain_then_inc((SA.sem, 1))
                    on("scalar", fsg)
                    tAh[h] = SA.bump()
                def A_post(h):
                    m, hf, gb = h // 2, h % 2, (h % 2) * 2048
                    def fv2(v, h=h, gb=gb):
                        v.wait_ge(SA.sem, tAh[h])
                        c_ap = AR_C[:, gb:gb + 2048]
                        g_ap = AR_B[:, gb:gb + 2048]
                        v.tensor_scalar(out=c_ap, in0=c_ap, scalar1=TWO16,
                                        scalar2=MAGIC, op0=Alu.mult, op1=Alu.add)
                        v.tensor_scalar(out=c_ap, in0=c_ap, scalar1=-MAGIC,
                                        scalar2=65535.0, op0=Alu.add, op1=Alu.min)
                        v.tensor_tensor(out=g_ap, in0=g_ap, in1=c_ap, op=Alu.mult)
                        v.tensor_reduce(out=bmax[:, h:h + 1], in_=g_ap,
                                        axis=AX.X, op=Alu.max, apply_absolute_value=True)
                        v.maybe_drain_then_inc((SV.sem, 1))
                    on("vector", fv2)
                    tVh[h] = SV.bump()
                    def fsp(s, h=h, m=m, hf=hf, gb=gb):
                        s.wait_ge(SV.sem, tVh[h])
                        s.dma_start(out=sp_g2v[m][:, hf * 2048:hf * 2048 + 2048],
                                    in_=AR_B[:, gb:gb + 2048]).then_inc(LD.sem, 16)
                        tSh[h] = LD.bump(16)
                    on("scalar", fsp)
                A_load(0)
                t1s = {0: A_rg1(0)}
                A_sig(0, t1s[0])
                for h in range(48):
                    if h + 1 < 48:
                        A_load(h + 1)
                        t1s[h + 1] = A_rg1(h + 1)
                        A_sig(h + 1, t1s[h + 1])
                    A_post(h)
                def fwA(s):
                    s.wait_ge(LD.sem, LD.v)
                    s.maybe_drain_then_inc((SA.sem, 1))
                on("scalar", fwA)
                SA.bump()
                local_absmax(bmax[:, 0:48], "lg2v")
                st = stage_locals(smcol("lg2v"))
                barrier()
                do_sync(sync_base + 2, ["gg2v"])
                def fs3v(v):
                    v.tensor_scalar(out=smcol("k2"), in0=smcol("s_gate"), scalar1=1.0 / TWO16,
                                    scalar2=None, op0=Alu.mult)
                    vgap(v)
                    v.tensor_tensor(out=smcol("t31"), in0=smcol("gg2v"), in1=smcol("k2"), op=Alu.mult)
                    vgap(v)
                    v.tensor_scalar(out=smcol("s_g2"), in0=smcol("t31"), scalar1=smcol("r32767"),
                                    scalar2=1e-12, op0=Alu.mult, op1=Alu.max)
                    vgap(v)
                    v.reciprocal(out=smcol("t32"), in_=smcol("s_g2"))
                    vgap(v)
                    v.tensor_tensor(out=smcol("cg2"), in0=smcol("k2"), in1=smcol("t32"), op=Alu.mult)
                    vgap(v)
                on("vector", fs3v)
                bcast_consts(["cg2", "cup"])
                barrier()
                if upto < 6:
                    return

                tLh = {}
                tVh = {}
                tSh = {}
                for h in range(48):
                    m, hf = h // 2, h % 2
                    gb = (h % 2) * 2048
                    def fsy(sy, h=h, m=m, hf=hf, gb=gb):
                        if h == 0:
                            WP(sy)
                        if h >= 2:
                            sy.wait_ge(LD.sem, tSh[h - 2])
                        sy.dma_start(out=AR_B[:, gb:gb + 2048],
                                     in_=sp_g2v[m][:, hf * 2048:hf * 2048 + 2048]
                                     ).then_inc(LD.sem, 16)
                        LD.bump(16)
                        sy.dma_start(out=AR_C[:, gb:gb + 2048],
                                     in_=sp_up[m][:, hf * 2048:hf * 2048 + 2048]
                                     ).then_inc(LD.sem, 16)
                        tLh[h] = LD.bump(16)
                    on("sync", fsy)
                    def fv(v, h=h, gb=gb):
                        if h == 0:
                            WP(v)
                        v.wait_ge(LD.sem, tLh[h])
                        g_ap = AR_B[:, gb:gb + 2048]
                        u_ap = AR_C[:, gb:gb + 2048]
                        v.tensor_scalar(out=g_ap, in0=g_ap, scalar1=bc[:, 0:1],
                                        scalar2=MAGIC, op0=Alu.mult, op1=Alu.add)
                        v.tensor_scalar(out=g_ap, in0=g_ap, scalar1=-MAGIC,
                                        scalar2=None, op0=Alu.add)
                        v.tensor_scalar(out=u_ap, in0=u_ap, scalar1=bc[:, 1:2],
                                        scalar2=MAGIC, op0=Alu.mult, op1=Alu.add)
                        v.tensor_scalar(out=u_ap, in0=u_ap, scalar1=-MAGIC,
                                        scalar2=None, op0=Alu.add)
                        v.tensor_tensor(out=g_ap, in0=g_ap, in1=u_ap, op=Alu.mult)
                        v.tensor_reduce(out=bmax[:, h:h + 1], in_=g_ap,
                                        axis=AX.X, op=Alu.max, apply_absolute_value=True)
                        v.maybe_drain_then_inc((SV.sem, 1))
                    on("vector", fv)
                    tVh[h] = SV.bump()
                    def fsp(sy, h=h, m=m, hf=hf, gb=gb):
                        sy.wait_ge(SV.sem, tVh[h])
                        sy.dma_start(out=sp_p[m][:, hf * 2048:hf * 2048 + 2048],
                                     in_=AR_B[:, gb:gb + 2048]).then_inc(LD.sem, 16)
                        tSh[h] = LD.bump(16)
                    on("sync", fsp)
                def fwB(s):
                    s.wait_ge(LD.sem, LD.v)
                    s.maybe_drain_then_inc((SA.sem, 1))
                on("scalar", fwB)
                SA.bump()
                local_absmax(bmax[:, 0:48], "lp")
                st = stage_locals(smcol("lp"))
                barrier()
                do_sync(sync_base + 3, ["gp"])
                def fs4v(v):
                    v.tensor_tensor(out=smcol("k3"), in0=smcol("s_g2"), in1=smcol("s_up"), op=Alu.mult)
                    vgap(v)
                    v.tensor_tensor(out=smcol("t41"), in0=smcol("gp"), in1=smcol("k3"), op=Alu.mult)
                    vgap(v)
                    v.tensor_scalar(out=smcol("s_o"), in0=smcol("t41"), scalar1=smcol("r32767"),
                                    scalar2=1e-12, op0=Alu.mult, op1=Alu.max)
                    vgap(v)
                    v.reciprocal(out=smcol("t42"), in_=smcol("s_o"))
                    v.tensor_scalar(out=smcol("kout"), in0=smcol("s_o"), scalar1=0.5,
                                    scalar2=None, op0=Alu.mult)
                    vgap(v)
                    v.tensor_tensor(out=smcol("co"), in0=smcol("k3"), in1=smcol("t42"), op=Alu.mult)
                    vgap(v)
                on("vector", fs4v)
                bcast_consts(["co", "kout"])
                barrier()
                if upto < 7:
                    return

                def fldw(sy):
                    WP(sy)
                    for k in range(24):
                        sy.dma_start(out=AR_A[:, k * 1024:k * 1024 + 1024],
                                     in_=hb_wdT[k]).then_inc(LD.sem, 16)
                        LD.bump(16)
                on("sync", fldw)
                tldw = LD.v
                tgv_all = {}
                tMMc = {}
                for tt in range(16):
                    tgv = {}
                    tgl = {}
                    rb = (tt % 2) * 6144
                    for g in range(3):
                        gb = (g % 2) * 2048
                        def fsy(sy, tt=tt, g=g, gb=gb):
                            if tt == 0 and g == 0:
                                WP(sy)
                                sy.wait_ge(LD.sem, tldw)
                            if g == 0 and tt >= 2:
                                sy.wait_ge(ST.sem, tMMc[tt - 2])
                            gate = tgv.get(g - 2, tgv_all.get((tt - 1, g + 1)))
                            if gate is not None:
                                sy.wait_ge(SV.sem, gate)
                            for j in range(8):
                                sy.dma_start(out=AR_B[:, gb + j * 256:gb + j * 256 + 256],
                                             in_=sp_p[g * 8 + j][:, tt * 256:tt * 256 + 256]
                                             ).then_inc(LD.sem, 16)
                                LD.bump(16)
                            tgl[g] = LD.v
                        on("sync", fsy)
                        def fv(v, tt=tt, g=g, gb=gb, rb=rb):
                            if tt == 0 and g == 0:
                                WP(v)
                            v.wait_ge(LD.sem, tgl[g])
                            b_ap = AR_B[:, gb:gb + 2048]
                            v.tensor_scalar(out=b_ap, in0=b_ap, scalar1=bc[:, 0:1],
                                            scalar2=MAGIC, op0=Alu.mult, op1=Alu.add)
                            v.tensor_scalar(out=b_ap, in0=b_ap, scalar1=-MAGIC,
                                            scalar2=None, op0=Alu.add)
                            v.tensor_copy(out=AR_W[:, rb + g * 2048:rb + g * 2048 + 2048], in_=b_ap)
                            v.maybe_drain_then_inc((SV.sem, 1))
                        on("vector", fv)
                        tgv[g] = SV.bump()
                    tgv_all.update({(tt, g): t for g, t in tgv.items()})
                    tro = {23: tgv[2]}
                    def ft(t, tt=tt):
                        if tt == 0:
                            WP(t)
                        t.wait_ge(SV.sem, tro[23])
                        if tt > 0:
                            t.wait_ge(SA.sem, SA.v)
                            t.wait_ge(SV.sem, SV.v)
                        rb2 = (tt % 2) * 6144
                        for m in range(8):
                            for k in range(24):
                                t.matmul(out=PS7[:, m * 256:m * 256 + 256],
                                         lhsT=AR_A[:, k * 1024 + m * 128:k * 1024 + m * 128 + 128],
                                         rhs=AR_W[:, rb2 + k * 256:rb2 + k * 256 + 256],
                                         start=(k == 0), stop=(k == 23))
                        t.maybe_drain_then_inc((ST.sem, 1))
                    on("tensor", ft)
                    tmm = ST.bump()
                    tMMc[tt] = tmm
                    def fs(s, tt=tt, tmm=tmm):
                        if tt == 0:
                            WP(s)
                        s.wait_ge(ST.sem, tmm)
                        for m in range(8):
                            s.copy(out=AR_C[:, m * 256:m * 256 + 256],
                                   in_=PS7[:, m * 256:m * 256 + 256])
                        s.maybe_drain_then_inc((SA.sem, 1))
                    on("scalar", fs)
                    tev = SA.bump()
                    for tb in range(2):
                        def ft2(t, tt=tt, tb=tb, tev=tev):
                            t.wait_ge(SA.sem, tev)
                            if tb == 1:
                                t.wait_ge(SV.sem, SV.v)
                            for m in range(8):
                                t.transpose(out=PS7[:, 2048 + m * 128:2048 + m * 128 + 128],
                                            in_=AR_C[:, m * 256 + tb * 128:
                                                      m * 256 + tb * 128 + 128],
                                            identity=idf[:])
                            t.maybe_drain_then_inc((ST.sem, 1))
                        on("tensor", ft2)
                        ttr = ST.bump()
                        def fsy2(sy, tt=tt, tb=tb):
                            row = tt * 256 + tb * 128
                            sy.dma_start(out=xt[:, 0:D], in_=x_src[row:row + 128, :]
                                         ).then_inc(LD.sem, 16)
                            LD.bump(16)
                        on("sync", fsy2)
                        tlx = LD.v
                        def fv3(v, tt=tt, tb=tb, ttr=ttr, tlx=tlx):
                            v.wait_ge(ST.sem, ttr)
                            v.wait_ge(LD.sem, tlx)
                            v.scalar_tensor_tensor(out=xt[:, 0:D], in0=PS7[:, 2048:3072],
                                                   scalar=bc[:, 1:2], in1=xt[:, 0:D],
                                                   op0=Alu.mult, op1=Alu.add)
                            v.maybe_drain_then_inc((SV.sem, 1))
                        on("vector", fv3)
                        tvx = SV.bump()
                        def fsy3(sy, tt=tt, tb=tb, tvx=tvx):
                            row = tt * 256 + tb * 128
                            sy.wait_ge(SV.sem, tvx)
                            sy.dma_start(out=x_dst[row:row + 128, :], in_=xt[:, 0:D]
                                         ).then_inc(LD.sem, 16)
                            LD.bump(16)
                        on("sync", fsy3)
                barrier()

            def attn_sublayer(x_src, x_dst):
                ln_stats(x_src, want_minmax=False)
                barrier()
                def fldw(g):
                    WP(g)
                    for k in range(8):
                        g.dma_start(out=AR_W[:, k * 1024:k * 1024 + 1024],
                                    in_=Wq_d[k * 128:(k + 1) * 128, :]).then_inc(GD.sem, 16)
                        GD.bump(16)
                    for k in range(8):
                        g.dma_start(out=AR_W[:, 8192 + k * 256:8192 + k * 256 + 256],
                                    in_=Wdn_d[k * 128:(k + 1) * 128, :]).then_inc(GD.sem, 16)
                        GD.bump(16)
                    for k in range(2):
                        g.dma_start(out=AR_W[:, 10240 + k * 2048:10240 + k * 2048 + 2048],
                                    in_=Wua_d[k * 128:(k + 1) * 128, :]).then_inc(GD.sem, 16)
                        GD.bump(16)
                    for k in range(8):
                        g.dma_start(out=AR_W[:, 14336 + k * 1024:14336 + k * 1024 + 1024],
                                    in_=Wo_d[k * 128:(k + 1) * 128, :]).then_inc(GD.sem, 16)
                        GD.bump(16)
                    g.maybe_drain_then_inc((SG.sem, 1))
                on("gpsimd", fldw)
                tgw = SG.bump()
                def flb(sy):
                    WP(sy)
                    sy.dma_start(out=bias1[0:1, 0:1024], in_=bq_d).then_inc(LD.sem, 16)
                    sy.dma_start(out=bias1[0:1, 1024:1280], in_=bdn_d).then_inc(LD.sem, 16)
                    sy.dma_start(out=bias1[0:1, 1280:3328], in_=bua_d).then_inc(LD.sem, 16)
                    sy.dma_start(out=bias1[0:1, 3328:4352], in_=bo_d).then_inc(LD.sem, 16)
                    LD.bump(64)
                on("sync", flb)
                tlb = LD.v
                def ftb1(t, tlb=tlb):
                    WP(t)
                    t.wait_ge(LD.sem, tlb)
                    for j in range(7):
                        t.matmul(out=PS7[:, j * 512:j * 512 + 512], lhsT=ones1[:],
                                 rhs=bias1[0:1, j * 512:j * 512 + 512], start=True, stop=True)
                    t.maybe_drain_then_inc((ST.sem, 1))
                on("tensor", ftb1)
                ST.bump()
                def fvb1(v, tgt=ST.v):
                    WP(v)
                    v.wait_ge(ST.sem, tgt)
                    v.tensor_copy(out=AR_C[:, 0:3328], in_=PS7[:, 0:3328])
                    v.maybe_drain_then_inc((SV.sem, 1))
                on("vector", fvb1)
                SV.bump()
                def ftb2(t, tgt=SV.v):
                    t.wait_ge(SV.sem, tgt)
                    t.matmul(out=PS7[:, 0:512], lhsT=ones1[:], rhs=bias1[0:1, 3328:3840],
                             start=True, stop=True)
                    t.matmul(out=PS7[:, 512:1024], lhsT=ones1[:], rhs=bias1[0:1, 3840:4352],
                             start=True, stop=True)
                    t.maybe_drain_then_inc((ST.sem, 1))
                on("tensor", ftb2)
                ST.bump()
                def fvb2(v, tgt=ST.v):
                    v.wait_ge(ST.sem, tgt)
                    v.tensor_copy(out=AR_B[:, 0:1024], in_=PS7[:, 0:1024])
                    v.maybe_drain_then_inc((SV.sem, 1))
                on("vector", fvb2)
                SV.bump()
                tbias = SV.v
                tL = {}
                tV = {}
                tT = {}
                tA = {}
                for i in range(NT):
                    off = (i % 2) * D
                    def fsy(sy, i=i, off=off):
                        if i == 0:
                            WP(sy)
                        if i >= 1:
                            sy.wait_ge(SA.sem, tA[i - 1])
                        sy.dma_start(out=xt[:, off:off + D],
                                     in_=x_src[i * 128:(i + 1) * 128, :]).then_inc(LD.sem, 16)
                        tL[i] = LD.bump(16)
                    on("sync", fsy)
                    def fv(v, i=i, off=off):
                        if i == 0:
                            WP(v)
                            v.wait_ge(SV.sem, tbias)
                        v.wait_ge(LD.sem, tL[i])
                        if i > 0:
                            v.wait_ge(ST.sem, tT[i - 1])
                        v.tensor_scalar(out=lnsc[:], in0=xt[:, off:off + D],
                                        scalar1=statM[:, i:i + 1], scalar2=statR[:, i:i + 1],
                                        op0=Alu.add, op1=Alu.mult)
                        v.tensor_copy(out=rxb[:, 0:D], in_=lnsc[:])
                        v.maybe_drain_then_inc((SV.sem, 1))
                        tV[i] = SV.bump()
                    on("vector", fv)
                    def ft(t, i=i):
                        if i == 0:
                            WP(t)
                        t.wait_ge(SV.sem, tV[i])
                        if i > 0:
                            t.wait_ge(SA.sem, tA[i - 1])
                        for k in range(8):
                            t.transpose(out=PT[:, k * 128:k * 128 + 128],
                                        in_=rxb[:, k * 128:k * 128 + 128], identity=idb[:])
                        t.maybe_drain_then_inc((ST.sem, 1))
                        tT[i] = ST.bump()
                    on("tensor", ft)
                    def fs(s, i=i):
                        if i == 0:
                            WP(s)
                        s.wait_ge(ST.sem, tT[i])
                        for k in range(8):
                            s.copy(out=AR_A[:, k * 4096 + i * 128:k * 4096 + i * 128 + 128],
                                   in_=PT[:, k * 128:k * 128 + 128])
                        s.maybe_drain_then_inc((SA.sem, 1))
                        tA[i] = SA.bump()
                    on("scalar", fs)
                def fwg(t, tgw=tgw):
                    t.wait_ge(SG.sem, tgw)
                on("tensor", fwg)
                barrier()

                kv3 = kvt[:].rearrange("p (g c) -> p g c", c=128)
                tQE = {}
                tKVE = {}
                tSCD = {}
                tAVD = {}
                tOPE = {}
                tP1 = {}
                tP2 = {}
                tP3 = {}
                tP4 = {}
                tP5 = {}
                tAO = {}
                tKLT = {}
                def emit_ft1(i):
                    def ft1(t, i=i):
                        if i == 0:
                            WP(t)
                        else:
                            t.wait_ge(SA.sem, tQE[i - 1])
                            t.wait_ge(SV.sem, tKLT[i - 1])
                        for k in range(8):
                            lhsT = AR_A[:, k * 4096 + i * 128:k * 4096 + i * 128 + 128]
                            t.matmul(out=PS7[:, 0:512], lhsT=lhsT,
                                     rhs=AR_W[:, k * 1024:k * 1024 + 512],
                                     start=(k == 0), stop=False)
                            t.matmul(out=PS7[:, 512:1024], lhsT=lhsT,
                                     rhs=AR_W[:, k * 1024 + 512:k * 1024 + 1024],
                                     start=(k == 0), stop=False)
                        t.matmul(out=PS7[:, 0:512], lhsT=ones1[:], rhs=bias1[0:1, 0:512],
                                 start=False, stop=True)
                        t.matmul(out=PS7[:, 512:1024], lhsT=ones1[:], rhs=bias1[0:1, 512:1024],
                                 start=False, stop=True)
                        for k in range(8):
                            lhsT = AR_A[:, k * 4096 + i * 128:k * 4096 + i * 128 + 128]
                            t.matmul(out=PS7[:, 3072:3200],
                                     lhsT=AR_W[:, 8192 + k * 256:8192 + k * 256 + 128],
                                     rhs=lhsT, start=(k == 0), stop=False)
                        t.matmul(out=PS7[:, 3072:3200], lhsT=bias1[0:1, 1024:1152],
                                 rhs=ones1[0:1, 0:128], start=False, stop=True)
                        for k in range(8):
                            lhsT = AR_A[:, k * 4096 + i * 128:k * 4096 + i * 128 + 128]
                            t.matmul(out=PS7[:, 3200:3328],
                                     lhsT=AR_W[:, 8192 + k * 256 + 128:8192 + k * 256 + 256],
                                     rhs=lhsT, start=(k == 0), stop=False)
                        t.matmul(out=PS7[:, 3200:3328], lhsT=bias1[0:1, 1152:1280],
                                 rhs=ones1[0:1, 0:128], start=False, stop=True)
                        t.maybe_drain_then_inc((ST.sem, 1))
                    on("tensor", ft1)
                    tP1[i] = ST.bump()

                emit_ft1(0)
                for i in range(NT):
                    def fa1(s, i=i):
                        if i == 0:
                            WP(s)
                        s.wait_ge(ST.sem, tP1[i])
                        if i > 0:
                            s.wait_ge(SV.sem, tSCD[i - 1])
                        s.copy(out=qt[:], in_=PS7[:, 0:1024])
                        s.maybe_drain_then_inc((SA.sem, 1))
                    on("scalar", fa1)
                    tQE[i] = SA.bump()
                    def fvk(v, i=i):
                        if i == 0:
                            WP(v)
                        v.wait_ge(ST.sem, tP1[i])
                        v.tensor_copy(out=kltT[:], in_=PS7[:, 3072:3328])
                        v.maybe_drain_then_inc((SV.sem, 1))
                    on("vector", fvk)
                    tKLT[i] = SV.bump()
                    def ft3(t, i=i):
                        t.wait_ge(SV.sem, tKLT[i])
                        if i > 0:
                            t.wait_ge(SA.sem, tKVE[i - 1])
                            t.wait_ge(SA.sem, tOPE[i - 1])
                        for k2 in range(2):
                            lhsT = kltT[:, k2 * 128:k2 * 128 + 128]
                            for n in range(4):
                                t.matmul(out=PS7[:, 1024 + n * 512:1024 + n * 512 + 512],
                                         lhsT=lhsT,
                                         rhs=AR_W[:, 10240 + k2 * 2048 + n * 512:
                                                   10240 + k2 * 2048 + n * 512 + 512],
                                         start=(k2 == 0), stop=False)
                        for n in range(4):
                            t.matmul(out=PS7[:, 1024 + n * 512:1024 + n * 512 + 512],
                                     lhsT=ones1[:],
                                     rhs=bias1[0:1, 1280 + n * 512:1280 + n * 512 + 512],
                                     start=False, stop=True)
                        t.maybe_drain_then_inc((ST.sem, 1))
                    on("tensor", ft3)
                    tP3[i] = ST.bump()
                    def fa3(s, i=i):
                        s.wait_ge(ST.sem, tP3[i])
                        if i > 0:
                            s.wait_ge(SV.sem, tAVD[i - 1])
                        s.copy(out=kvt[:], in_=PS7[:, 1024:3072])
                        s.maybe_drain_then_inc((SA.sem, 1))
                    on("scalar", fa3)
                    tKVE[i] = SA.bump()
                    def fv3(v, i=i):
                        if i == 0:
                            WP(v)
                        v.wait_ge(SA.sem, tKVE[i])
                        for h in range(16):
                            qh = qt[:, h * 64:h * 64 + 64].unsqueeze(1).broadcast_to((128, 16, 64))
                            v.tensor_tensor(out=prod[:].rearrange("p (g c) -> p g c", c=64),
                                            in0=qh, in1=kv3[:, :, 0:64], op=Alu.mult)
                            v.tensor_reduce(out=sco[:, h * 16:h * 16 + 16],
                                            in_=prod[:].rearrange("p (g c) -> p g c", c=64),
                                            axis=AX.X, op=Alu.add)
                        v.tensor_reduce(out=smax[:], in_=sco[:].rearrange("p (h g) -> p h g", g=16),
                                        axis=AX.X, op=Alu.max)
                        vgap(v)
                        v.tensor_tensor(out=sco[:].rearrange("p (h g) -> p h g", g=16),
                                        in0=sco[:].rearrange("p (h g) -> p h g", g=16),
                                        in1=smax[:].unsqueeze(2).broadcast_to((128, 16, 16)),
                                        op=Alu.subtract)
                        v.maybe_drain_then_inc((SV.sem, 1))
                    on("vector", fv3)
                    tSCD[i] = SV.bump()
                    def fsc(s, i=i):
                        s.wait_ge(SV.sem, tSCD[i])
                        s.activation(out=sco[:], in_=sco[:], func=Act.Exp, scale=0.125)
                        s.maybe_drain_then_inc((SA.sem, 1))
                    on("scalar", fsc)
                    t7 = SA.bump()
                    def fv4(v, i=i, t7=t7):
                        v.wait_ge(SA.sem, t7)
                        v.tensor_reduce(out=rsum[:], in_=sco[:].rearrange("p (h g) -> p h g", g=16),
                                        axis=AX.X, op=Alu.add)
                        vgap(v)
                        v.reciprocal(out=rden[:], in_=rsum[:])
                        vgap(v)
                        v.tensor_tensor(out=sco[:].rearrange("p (h g) -> p h g", g=16),
                                        in0=sco[:].rearrange("p (h g) -> p h g", g=16),
                                        in1=rden[:].unsqueeze(2).broadcast_to((128, 16, 16)),
                                        op=Alu.mult)
                        sc0 = sco[:]
                        o_full = acc[:].rearrange("p (h c) -> p h c", c=64)
                        p_full = prod[:].rearrange("p (h c) -> p h c", c=64)
                        for gi in range(16):
                            a_ap = bass.AP(sc0.tensor, sc0.offset + gi,
                                           [[256, 128], [16, 16], [0, 64]])
                            v_ap = bass.AP(kvt[:].tensor, kvt[:].offset + 128 * gi + 64,
                                           [[2048, 128], [0, 16], [1, 64]])
                            if gi == 0:
                                v.tensor_tensor(out=o_full, in0=a_ap, in1=v_ap, op=Alu.mult)
                            else:
                                v.tensor_tensor(out=p_full, in0=a_ap, in1=v_ap, op=Alu.mult)
                                v.tensor_tensor(out=o_full, in0=o_full, in1=p_full, op=Alu.add)
                        v.tensor_copy(out=accb[:, 0:D], in_=acc[:])
                        v.maybe_drain_then_inc((SV.sem, 1))
                    on("vector", fv4)
                    tAVD[i] = SV.bump()
                    if i + 1 < NT:
                        emit_ft1(i + 1)
                    def ft4(t, i=i):
                        t.wait_ge(SV.sem, tAVD[i])
                        for k in range(8):
                            t.transpose(out=PT[:, k * 128:k * 128 + 128],
                                        in_=accb[:, k * 128:k * 128 + 128], identity=idb[:])
                        t.maybe_drain_then_inc((ST.sem, 1))
                    on("tensor", ft4)
                    tP4[i] = ST.bump()
                    def fsa(s, i=i):
                        s.wait_ge(ST.sem, tP4[i])
                        s.copy(out=aoT[:], in_=PT[:, 0:1024])
                        s.maybe_drain_then_inc((SA.sem, 1))
                    on("scalar", fsa)
                    tAO[i] = SA.bump()
                    def ft5(t, i=i):
                        t.wait_ge(SA.sem, tAO[i])
                        for k in range(8):
                            lhsT = aoT[:, k * 128:k * 128 + 128]
                            t.matmul(out=PS7[:, 1024:1536], lhsT=lhsT,
                                     rhs=AR_W[:, 14336 + k * 1024:14336 + k * 1024 + 512],
                                     start=(k == 0), stop=False)
                            t.matmul(out=PS7[:, 1536:2048], lhsT=lhsT,
                                     rhs=AR_W[:, 14336 + k * 1024 + 512:14336 + k * 1024 + 1024],
                                     start=(k == 0), stop=False)
                        t.matmul(out=PS7[:, 1024:1536], lhsT=ones1[:],
                                 rhs=bias1[0:1, 3328:3840], start=False, stop=True)
                        t.matmul(out=PS7[:, 1536:2048], lhsT=ones1[:],
                                 rhs=bias1[0:1, 3840:4352], start=False, stop=True)
                        t.maybe_drain_then_inc((ST.sem, 1))
                    on("tensor", ft5)
                    tP5[i] = ST.bump()
                    def fa5(s, i=i):
                        s.wait_ge(ST.sem, tP5[i])
                        s.copy(out=lnsc[:], in_=PS7[:, 1024:2048])
                        s.maybe_drain_then_inc((SA.sem, 1))
                    on("scalar", fa5)
                    tOPE[i] = SA.bump()
                    def fsy(sy, i=i):
                        sy.dma_start(out=xt[:, 0:D], in_=x_src[i * 128:(i + 1) * 128, :]
                                     ).then_inc(LD.sem, 16)
                        LD.bump(16)
                    on("sync", fsy)
                    t12 = LD.v
                    def fv5(v, i=i, t12=t12):
                        v.wait_ge(SA.sem, tOPE[i])
                        v.wait_ge(LD.sem, t12)
                        v.tensor_tensor(out=xt[:, 0:D], in0=lnsc[:], in1=xt[:, 0:D], op=Alu.add)
                        v.maybe_drain_then_inc((SV.sem, 1))
                    on("vector", fv5)
                    t13 = SV.bump()
                    def fsy2(sy, i=i, t13=t13):
                        sy.wait_ge(SV.sem, t13)
                        sy.dma_start(out=x_dst[i * 128:(i + 1) * 128, :], in_=xt[:, 0:D]
                                     ).then_inc(LD.sem, 16)
                        LD.bump(16)
                    on("sync", fsy2)
                barrier()

            if KSTOP >= 5:
                ffn_sublayer(x_in, hbm_x, 0, upto=KSTOP - 5)
            if KSTOP >= 13:
                attn_sublayer(hbm_x, hbm_x)
            if KSTOP >= 14:
                ffn_sublayer(hbm_x, y_out, 4, upto=(99 if KSTOP >= 99 else KSTOP - 14))

            def fdbg(g):
                WP(g)
                g.dma_start(out=dbg_out[0:1, 0:64], in_=sm[:]).then_inc(GD.sem, 16)
                GD.bump(16)
                g.dma_start(out=dbg_out[0:128, 64:72], in_=bc[:]).then_inc(GD.sem, 16)
                GD.bump(16)
                g.dma_start(out=dbg_out[0:128, 72:104], in_=statM[:]).then_inc(GD.sem, 16)
                GD.bump(16)
                g.dma_start(out=dbg_out[0:128, 104:136], in_=statR[:]).then_inc(GD.sem, 16)
                GD.bump(16)
                g.dma_start(out=dbg_out[0:128, 136:168], in_=lmax[:]).then_inc(GD.sem, 16)
                GD.bump(16)
                g.dma_start(out=dbg_out[0:128, 168:200], in_=colS[:]).then_inc(GD.sem, 16)
                GD.bump(16)
                g.dma_start(out=dbg_out[0:128, 200:232], in_=colQ[:]).then_inc(GD.sem, 16)
                GD.bump(16)
                kd = os.environ.get('KDUMP', '')
                if kd == 'hbm_x':
                    g.dma_start(out=y_out, in_=hbm_x).then_inc(GD.sem, 16)
                    GD.bump(16)
                elif kd in ('spg', 'spu', 'spv', 'spp'):
                    s = {'spg': sp_gate, 'spu': sp_up, 'spv': sp_g2v, 'spp': sp_p}[kd]
                    g.dma_start(out=y_out, in_=s[0:8]).then_inc(GD.sem, 16)
                    GD.bump(16)
                elif kd in ('wgT', 'wuT'):
                    s = {'wgT': hb_wgT, 'wuT': hb_wuT}[kd]
                    g.dma_start(out=y_out[0:3072, :], in_=s).then_inc(GD.sem, 16)
                    GD.bump(16)
                elif kd == 'wdT':
                    g.dma_start(out=y_out[0:3072, :], in_=hb_wdT).then_inc(GD.sem, 16)
                    GD.bump(16)
                elif kd == 'arA':
                    g.dma_start(out=y_out, in_=AR_A[:]).then_inc(GD.sem, 16)
                    GD.bump(16)
                elif kd == 'arB':
                    g.dma_start(out=y_out[0:512, :], in_=AR_B[:]).then_inc(GD.sem, 16)
                    GD.bump(16)
                elif kd == 'attn':
                    g.dma_start(out=y_out[0:128, :], in_=qt[:]).then_inc(GD.sem, 16)
                    GD.bump(16)
                    g.dma_start(out=y_out[128:384, :], in_=kvt[:]).then_inc(GD.sem, 16)
                    GD.bump(16)
                    g.dma_start(out=y_out[384:416, :], in_=sco[:]).then_inc(GD.sem, 16)
                    GD.bump(16)
                    g.dma_start(out=y_out[416:544, :], in_=acc[:]).then_inc(GD.sem, 16)
                    GD.bump(16)
                elif kd == 'arW':
                    g.dma_start(out=y_out[0:768, :], in_=AR_W[:, 0:6144]).then_inc(GD.sem, 16)
                    GD.bump(16)
            on("gpsimd", fdbg)
            def ffin(sy):
                WP(sy)
                sy.wait_ge(LD.sem, LD.v)
                sy.wait_ge(GD.sem, GD.v)
            on("sync", ffin)
            barrier()
    return nc


_NC_CACHE = [None]

def kernel(L0_attn_x, L0_attn_Wq, L0_attn_bq, L0_attn_Wdown, L0_attn_bdown,
           L0_attn_Wup, L0_attn_bup, L0_attn_Wout, L0_attn_bout,
           L0_ffn_w_gate, L0_ffn_w_up, L0_ffn_w_down):
    if _NC_CACHE[0] is None:
        _NC_CACHE[0] = build_nc()
    nc = _NC_CACHE[0]
    x = np.ascontiguousarray(np.asarray(L0_attn_x, dtype=np.float32))
    common = {
        "Wq": np.ascontiguousarray(np.asarray(L0_attn_Wq, np.float32)),
        "bq": np.asarray(L0_attn_bq, np.float32).reshape(1, D),
        "Wdn": np.ascontiguousarray(np.asarray(L0_attn_Wdown, np.float32)),
        "bdn": np.asarray(L0_attn_bdown, np.float32).reshape(1, 256),
        "Wua": np.ascontiguousarray(np.asarray(L0_attn_Wup, np.float32)),
        "bua": np.asarray(L0_attn_bup, np.float32).reshape(1, 2048),
        "Wo": np.ascontiguousarray(np.asarray(L0_attn_Wout, np.float32)),
        "bo": np.asarray(L0_attn_bout, np.float32).reshape(1, D),
        "wg": np.ascontiguousarray(np.asarray(L0_ffn_w_gate, np.float32)),
        "wu": np.ascontiguousarray(np.asarray(L0_ffn_w_up, np.float32)),
        "wd": np.ascontiguousarray(np.asarray(L0_ffn_w_down, np.float32)),
    }
    in_maps = [dict(common, x=np.ascontiguousarray(x[c])) for c in range(8)]
    res = run_bass_kernel_spmd(nc, in_maps, core_ids=list(range(8)),
                               trace=(os.environ.get('KTRACE') == '1'))
    if os.environ.get('KTRACE') == '1':
        print('EXEC_NS', res.exec_time_ns)
    if os.environ.get('KDBG'):
        np.save('/tmp/dbg0.npy', np.asarray(res.results[0]['dbg']))
    out = np.stack([np.asarray(res.results[c]["y"], dtype=np.float32) for c in range(8)], axis=0)
    return out


# version 18
# speedup vs baseline: 1.2076x; 1.0324x over previous
import os
import numpy as np
import contextlib
import concourse.bass as bass
import concourse.mybir as mybir
from concourse.bass_utils import run_bass_kernel_spmd

F32 = mybir.dt.float32
BF16 = mybir.dt.bfloat16
Alu = mybir.AluOpType
Act = mybir.ActivationFunctionType
AX = mybir.AxisListType

T = 4096
D = 1024
HID = 3072
NT = T // 128
MAGIC = 12582912.0
EPS = 1e-5
QMAX = 32767.0
TWO16 = 65536.0
INV1024 = 1.0 / 1024.0


class Tok:
    def __init__(self, sem, name):
        self.sem = sem
        self.v = 0
        self.name = name
    def bump(self, by=1):
        self.v += by
        return self.v


KSTOP = int(os.environ.get('KSTOP', '99'))


def build_nc():
    nc = bass.Bass("TRN2", target_bir_lowering=False)
    x_in = nc.dram_tensor("x", [T, D], F32, kind="ExternalInput").ap()
    Wq_d = nc.dram_tensor("Wq", [D, D], F32, kind="ExternalInput").ap()
    bq_d = nc.dram_tensor("bq", [1, D], F32, kind="ExternalInput").ap()
    Wdn_d = nc.dram_tensor("Wdn", [D, 256], F32, kind="ExternalInput").ap()
    bdn_d = nc.dram_tensor("bdn", [1, 256], F32, kind="ExternalInput").ap()
    Wua_d = nc.dram_tensor("Wua", [256, 2048], F32, kind="ExternalInput").ap()
    bua_d = nc.dram_tensor("bua", [1, 2048], F32, kind="ExternalInput").ap()
    Wo_d = nc.dram_tensor("Wo", [D, D], F32, kind="ExternalInput").ap()
    bo_d = nc.dram_tensor("bo", [1, D], F32, kind="ExternalInput").ap()
    wg_d = nc.dram_tensor("wg", [HID, D], F32, kind="ExternalInput").ap()
    wu_d = nc.dram_tensor("wu", [HID, D], F32, kind="ExternalInput").ap()
    wd_d = nc.dram_tensor("wd", [D, HID], F32, kind="ExternalInput").ap()
    y_out = nc.dram_tensor("y", [T, D], F32, kind="ExternalOutput").ap()
    dbg_out = nc.dram_tensor("dbg", [128, 256], F32, kind="ExternalOutput").ap()

    hbm_x = nc.dram_tensor("hbm_x", [T, D], F32).ap()
    hb_wgT = nc.dram_tensor("hb_wgT", [8, 128, HID], BF16).ap()
    hb_wuT = nc.dram_tensor("hb_wuT", [8, 128, HID], BF16).ap()
    hb_wdT = nc.dram_tensor("hb_wdT", [24, 128, D], BF16).ap()
    sp_gate = nc.dram_tensor("sp_gate", [24, 128, T], F32).ap()
    sp_up = nc.dram_tensor("sp_up", [24, 128, T], F32).ap()
    sp_g2v = nc.dram_tensor("sp_g2v", [24, 128, T], F32).ap()
    sp_p = nc.dram_tensor("sp_p", [24, 128, T], F32).ap()
    cc_ins = [nc.dram_tensor(f"cc_in{i}", [1, 2], F32).ap() for i in range(8)]
    cc_outs = [nc.dram_tensor(f"cc_out{i}", [8, 2], F32).ap() for i in range(8)]

    es = contextlib.ExitStack()
    with es:
        def sbuf(name, shape, dt=F32):
            return es.enter_context(nc.sbuf_tensor(name, shape, dt))
        AR_A = sbuf("AR_A", [128, 32768], BF16)
        AR_B = sbuf("AR_B", [128, 4096], F32)
        AR_C = sbuf("AR_C", [128, 4096], F32)
        AR_W = sbuf("AR_W", [128, 24576], BF16)
        xt = sbuf("xt", [128, 2 * D])
        lnsc = sbuf("lnsc", [128, D])
        rxb = sbuf("rxb", [128, 2 * D], BF16)
        rxsc = lnsc
        statM = sbuf("statM", [128, NT])
        statR = sbuf("statR", [128, NT])
        colS = sbuf("colS", [128, NT])
        colQ = sbuf("colQ", [128, NT])
        lmax = sbuf("lmax", [128, 32])
        red1 = sbuf("red1", [128, 1])
        tmp1 = sbuf("tmp1", [128, 1])
        flip = sbuf("flip", [1, 128])
        sm = sbuf("sm", [1, 64])
        agf = sbuf("agf", [1, 16])
        ones1 = sbuf("ones1", [1, 128])
        bcv = sbuf("bcv", [1, 8])
        bc = sbuf("bc", [128, 8])
        sevr = sbuf("sevr", [128, 1])
        epsc = sbuf("epsc", [128, 1])
        bmax = sbuf("bmax", [128, 96])
        bsS = sbuf("bsS", [128, 96])
        brs = sbuf("brs", [128, 96])
        idf = sbuf("idf", [128, 128])
        idb = sbuf("idb", [128, 128], BF16)
        qt = sbuf("qt", [128, D])
        kvt = sbuf("kvt", [128, 2048])
        klt = sbuf("klt", [128, 256])
        kltT = sbuf("kltT", [128, 256], BF16)
        prod = sbuf("prod", [128, D])
        sco = sbuf("sco", [128, 256])
        smax = sbuf("smax", [128, 16])
        rsum = sbuf("rsum", [128, 16])
        rden = sbuf("rden", [128, 16])
        acc = sbuf("acc", [128, D])
        accb = rxb
        aoT = sbuf("aoT", [128, 1024], BF16)
        rof = sbuf("rof", [128, 256])
        bias1 = sbuf("bias1", [1, 4352])
        PS7 = es.enter_context(nc.psum_tensor("PS7", [128, 3584], F32))
        PT = es.enter_context(nc.psum_tensor("PT", [128, 1024], BF16))

        SB = Tok(es.enter_context(nc.semaphore("sB")), "B")
        LD = Tok(es.enter_context(nc.semaphore("sLD")), "LD")
        GD = Tok(es.enter_context(nc.semaphore("sGD")), "GD")
        SV = Tok(es.enter_context(nc.semaphore("sV")), "V")
        SA = Tok(es.enter_context(nc.semaphore("sA")), "A")
        ST = Tok(es.enter_context(nc.semaphore("sT")), "T")
        SG = Tok(es.enter_context(nc.semaphore("sG")), "G")
        CCS = Tok(es.enter_context(nc.semaphore("sCC")), "CC")

        phase = [0]
        ENGS = ["sync", "vector", "scalar", "tensor", "gpsimd"]

        smc = {"r32767": 0, "stage": 1}
        def smcol(name):
            if name not in smc:
                smc[name] = len(smc)
            return sm[0:1, smc[name]:smc[name] + 1]

        def vgap(v, n=2):
            for _ in range(n):
                v.tensor_copy(out=sm[0:1, 62:63], in_=ones1[0:1, 0:1])

        with nc.Block() as block:
            def on(eng, fn):
                getattr(block, eng)(fn)

            def barrier():
                phase[0] += 1
                for e in ENGS:
                    def f(eng):
                        eng.maybe_drain_then_inc((SB.sem, 1))
                    on(e, f)

            def WP(eng):
                eng.wait_ge(SB.sem, 5 * phase[0])

            def p0g(g):
                g.memset(idf[:], 0.0)
                g.affine_select(out=idf[:], in_=idf[:], compare_op=Alu.not_equal,
                                fill=1.0, base=0, pattern=[[-1, 128]], channel_multiplier=1)
                g.memset(sevr[:], 7.0)
                g.memset(epsc[:], EPS)
                g.memset(ones1[:], 1.0)
                g.memset(sm[:], 0.0)
                g.memset(bcv[:], 0.0)
                g.maybe_drain_then_inc((SG.sem, 1))
            on("gpsimd", p0g)
            tok0 = SG.bump()
            def p0v(v):
                v.wait_ge(SG.sem, tok0)
                v.tensor_copy(out=idb[:], in_=idf[:])
                v.reciprocal(out=sevr[:], in_=sevr[:])
                v.tensor_scalar(out=smcol("stage"), in0=ones1[0:1, 0:1], scalar1=QMAX,
                                scalar2=None, op0=Alu.mult)
                vgap(v)
                v.reciprocal(out=smcol("r32767"), in_=smcol("stage"))
            on("vector", p0v)
            barrier()

            def dequant_matrix(w_src, n_tiles, in_cols, hbT, tag):
                assert in_cols == 1024
                nblk = in_cols // 32
                kch = in_cols // 128
                slab_stride = n_tiles * 128
                tT = {}
                tV = {}
                tL = {}
                tA = {}
                for i in range(n_tiles):
                    wb = (i % 2) * 1024
                    qb = (i % 2) * in_cols
                    def fsy(sy, i=i, wb=wb):
                        if i == 0:
                            WP(sy)
                        if i >= 2:
                            sy.wait_ge(SV.sem, tV[i - 2])
                        sy.dma_start(out=AR_B[:, wb:wb + in_cols],
                                     in_=w_src[i * 128:(i + 1) * 128, :]).then_inc(LD.sem, 16)
                        tL[i] = LD.bump(16)
                    on("sync", fsy)
                    def fv(v, i=i, wb=wb, qb=qb):
                        if i == 0:
                            WP(v)
                        v.wait_ge(LD.sem, tL[i])
                        if i > 1:
                            v.wait_ge(ST.sem, tT[i - 2])
                        w_ap = AR_B[:, wb:wb + in_cols]
                        w3 = w_ap.rearrange("p (a b) -> p a b", b=32)
                        v.tensor_reduce(out=bmax[:, 0:nblk], in_=w3, axis=AX.X,
                                        op=Alu.max, apply_absolute_value=True)
                        vgap(v)
                        v.tensor_scalar(out=bsS[:, 0:nblk], in0=bmax[:, 0:nblk],
                                        scalar1=sevr[:, 0:1], scalar2=1e-12,
                                        op0=Alu.mult, op1=Alu.max)
                        vgap(v)
                        v.reciprocal(out=brs[:, 0:nblk], in_=bsS[:, 0:nblk])
                        vgap(v)
                        rs_b = brs[:, 0:nblk].unsqueeze(2).broadcast_to((128, nblk, 32))
                        s_b = bsS[:, 0:nblk].unsqueeze(2).broadcast_to((128, nblk, 32))
                        v.tensor_tensor(out=w3, in0=w3, in1=rs_b, op=Alu.mult)
                        v.tensor_scalar(out=w_ap, in0=w_ap, scalar1=MAGIC, scalar2=MAGIC,
                                        op0=Alu.add, op1=Alu.subtract)
                        wqb = AR_W[:, qb:qb + in_cols]
                        v.tensor_tensor(out=wqb.rearrange("p (a b) -> p a b", b=32),
                                        in0=w3, in1=s_b, op=Alu.mult)
                        v.maybe_drain_then_inc((SV.sem, 1))
                        tV[i] = SV.bump()
                    on("vector", fv)
                    def ft(t, i=i, qb=qb):
                        if i == 0:
                            WP(t)
                        t.wait_ge(SV.sem, tV[i])
                        t.wait_ge(SA.sem, SA.v)
                        for k in range(kch):
                            t.transpose(out=PT[:, k * 128:k * 128 + 128],
                                        in_=AR_W[:, qb + k * 128:qb + k * 128 + 128],
                                        identity=idb[:])
                        t.maybe_drain_then_inc((ST.sem, 1))
                    on("tensor", ft)
                    tT[i] = ST.bump()
                    def fs(s, i=i):
                        if i == 0:
                            WP(s)
                        s.wait_ge(ST.sem, tT[i])
                        for k in range(kch):
                            s.copy(out=AR_A[:, k * slab_stride + i * 128:
                                            k * slab_stride + i * 128 + 128],
                                   in_=PT[:, k * 128:k * 128 + 128])
                        s.maybe_drain_then_inc((SA.sem, 1))
                        tA[i] = SA.bump()
                    on("scalar", fs)
                def fw(sy):
                    sy.wait_ge(SA.sem, SA.v)
                    for k in range(kch):
                        sy.dma_start(out=hbT[k],
                                     in_=AR_A[:, k * slab_stride:k * slab_stride + n_tiles * 128]
                                     ).then_inc(LD.sem, 16)
                        LD.bump(16)
                    sy.wait_ge(LD.sem, LD.v)
                on("sync", fw)

            dequant_matrix(wg_d, 24, D, hb_wgT, "dg")
            barrier()
            if KSTOP >= 3:
                dequant_matrix(wu_d, 24, D, hb_wuT, "du")
            barrier()
            if KSTOP >= 4:
                for cs in range(3):
                    dequant_matrix(wd_d[:, cs * 1024:(cs + 1) * 1024], 8, 1024,
                                   [hb_wdT[cs * 8 + j] for j in range(8)], f"dd{cs}")
            barrier()

            def ln_stats(x_src, want_minmax=True):
                tL = {}
                tLv = {}
                for i in range(NT):
                    off = (i % 2) * D
                    def fsy(sy, i=i, off=off):
                        if i == 0:
                            WP(sy)
                        if i >= 2:
                            sy.wait_ge(SV.sem, tLv[i - 2])
                        sy.dma_start(out=xt[:, off:off + D],
                                     in_=x_src[i * 128:(i + 1) * 128, :]).then_inc(LD.sem, 16)
                        tL[i] = LD.bump(16)
                    on("sync", fsy)
                    def fv(v, i=i, off=off):
                        if i == 0:
                            WP(v)
                        v.wait_ge(LD.sem, tL[i])
                        v.tensor_reduce(out=colS[:, i:i + 1], in_=xt[:, off:off + D],
                                        axis=AX.X, op=Alu.add)
                        v.scalar_tensor_tensor(out=lnsc[:], in0=xt[:, off:off + D], scalar=1.0,
                                               in1=xt[:, off:off + D], op0=Alu.mult, op1=Alu.mult,
                                               accum_out=colQ[:, i:i + 1])
                        if want_minmax:
                            v.tensor_reduce(out=bmax[:, i:i + 1], in_=xt[:, off:off + D],
                                            axis=AX.X, op=Alu.max)
                            v.tensor_reduce(out=bmax[:, 32 + i:33 + i], in_=xt[:, off:off + D],
                                            axis=AX.X, op=Alu.min)
                        v.maybe_drain_then_inc((SV.sem, 1))
                        tLv[i] = SV.bump()
                    on("vector", fv)
                def fv2(v):
                    v.tensor_scalar(out=statM[:], in0=colS[:], scalar1=-INV1024,
                                    scalar2=None, op0=Alu.mult)
                    v.tensor_scalar(out=colQ[:], in0=colQ[:], scalar1=INV1024,
                                    scalar2=None, op0=Alu.mult)
                    vgap(v)
                    v.tensor_tensor(out=colS[:], in0=statM[:], in1=statM[:], op=Alu.mult)
                    vgap(v)
                    v.tensor_tensor(out=colQ[:], in0=colQ[:], in1=colS[:], op=Alu.subtract)
                    v.maybe_drain_then_inc((SV.sem, 1))
                on("vector", fv2)
                tv = SV.bump()
                def fs(s):
                    WP(s)
                    s.wait_ge(SV.sem, tv)
                    s.activation(out=statR[:], in_=colQ[:], func=Act.Sqrt, bias=epsc[:, 0:1])
                    s.maybe_drain_then_inc((SA.sem, 1))
                on("scalar", fs)
                ta = SA.bump()
                def fv3(v):
                    v.wait_ge(SA.sem, ta)
                    v.reciprocal(out=statR[:], in_=statR[:])
                on("vector", fv3)

            def do_sync(cc_idx, names2):
                def fg(g):
                    WP(g)
                    g.dma_start(out=cc_ins[cc_idx], in_=sm[0:1, 60:62]).then_inc(GD.sem, 16)
                    g.wait_ge(GD.sem, GD.bump(16))
                    g.collective_compute("AllGather", Alu.bypass,
                                         replica_groups=[list(range(8))],
                                         ins=[cc_ins[cc_idx]], outs=[cc_outs[cc_idx]]
                                         ).then_inc(CCS.sem, 1)
                    g.wait_ge(CCS.sem, CCS.bump())
                    g.dma_start(out=agf[:], in_=cc_outs[cc_idx]).then_inc(GD.sem, 16)
                    g.wait_ge(GD.sem, GD.bump(16))
                    g.maybe_drain_then_inc((SG.sem, 1))
                on("gpsimd", fg)
                tg = SG.bump()
                def fv(v):
                    WP(v)
                    v.wait_ge(SG.sem, tg)
                    for j, nm in enumerate(names2):
                        src = bass.AP(agf[:].tensor, agf[:].offset + j, [[16, 1], [2, 8]])
                        v.tensor_reduce(out=smcol(nm), in_=src, axis=AX.X, op=Alu.max)
                    vgap(v)
                on("vector", fv)

            def stage_locals(*col_aps):
                def fv(v):
                    for j, ap in enumerate(col_aps):
                        v.tensor_copy(out=sm[0:1, 60 + j:61 + j], in_=ap)
                    v.maybe_drain_then_inc((SV.sem, 1))
                on("vector", fv)
                return SV.bump()

            def local_absmax(cols_ap, out_name):
                def fv(v):
                    v.tensor_reduce(out=red1[:], in_=cols_ap, axis=AX.X, op=Alu.max)
                    v.maybe_drain_then_inc((SV.sem, 1))
                on("vector", fv)
                tv = SV.bump()
                def fg(g):
                    WP(g)
                    g.wait_ge(SV.sem, tv)
                    g.dma_start(out=flip[:], in_=red1[:]).then_inc(GD.sem, 16)
                    g.wait_ge(GD.sem, GD.bump(16))
                    g.maybe_drain_then_inc((SG.sem, 1))
                on("gpsimd", fg)
                tg = SG.bump()
                def fv2(v):
                    v.wait_ge(SG.sem, tg)
                    v.tensor_reduce(out=smcol(out_name), in_=flip[:], axis=AX.X, op=Alu.max)
                    vgap(v)
                on("vector", fv2)

            def bcast_consts(names_in):
                def fv(v):
                    for j, nm in enumerate(names_in):
                        v.tensor_copy(out=bcv[0:1, j:j + 1], in_=smcol(nm))
                    v.maybe_drain_then_inc((SV.sem, 1))
                on("vector", fv)
                tv = SV.bump()
                def ft(t):
                    WP(t)
                    t.wait_ge(SV.sem, tv)
                    t.matmul(out=PS7[:, 0:8], lhsT=ones1[:], rhs=bcv[:], start=True, stop=True)
                    t.maybe_drain_then_inc((ST.sem, 1))
                on("tensor", ft)
                tt = ST.bump()
                def fv2(v):
                    v.wait_ge(ST.sem, tt)
                    v.tensor_copy(out=bc[:, 0:8], in_=PS7[:, 0:8])
                    v.maybe_drain_then_inc((SV.sem, 1))
                on("vector", fv2)
                SV.bump()

            def ffn_sublayer(x_src, x_dst, sync_base, upto=99):
                ln_stats(x_src)
                barrier()
                if upto < 1:
                    return
                def flm(v):
                    v.tensor_tensor(out=bmax[:, 0:32], in0=bmax[:, 0:32], in1=statM[:, 0:32],
                                    op=Alu.add)
                    v.tensor_tensor(out=bmax[:, 32:64], in0=bmax[:, 32:64], in1=statM[:, 0:32],
                                    op=Alu.add)
                    vgap(v)
                    v.tensor_scalar(out=bmax[:, 32:64], in0=bmax[:, 32:64], scalar1=-1.0,
                                    scalar2=None, op0=Alu.mult)
                    vgap(v)
                    v.tensor_tensor(out=bmax[:, 0:32], in0=bmax[:, 0:32], in1=bmax[:, 32:64],
                                    op=Alu.max)
                    vgap(v)
                    v.tensor_tensor(out=lmax[:, 0:32], in0=bmax[:, 0:32], in1=statR[:, 0:32],
                                    op=Alu.mult)
                    v.maybe_drain_then_inc((SV.sem, 1))
                on("vector", flm)
                SV.bump()
                local_absmax(lmax[:, 0:NT], "lmx")
                st = stage_locals(smcol("lmx"))
                barrier()
                do_sync(sync_base + 0, ["gx"])
                def fsx(v):
                    v.tensor_scalar(out=smcol("s_x"), in0=smcol("gx"), scalar1=smcol("r32767"),
                                    scalar2=1e-12, op0=Alu.mult, op1=Alu.max)
                    vgap(v)
                    v.reciprocal(out=smcol("c1"), in_=smcol("s_x"))
                    vgap(v)
                on("vector", fsx)
                bcast_consts(["c1"])
                barrier()
                if upto < 2:
                    return
                RXL = 9
                tL = {}
                tV = {}
                tT = {}
                tA = {}
                RXS = 0
                for i in range(NT):
                    off = (i % 2) * D
                    def fsy(sy, i=i, off=off):
                        if i == 0:
                            WP(sy)
                        if RXS and i >= 1:
                            sy.wait_ge(SA.sem, tA[i - 1])
                        elif i >= 2:
                            sy.wait_ge(SV.sem, tV[i - 2])
                        sy.dma_start(out=xt[:, off:off + D],
                                     in_=x_src[i * 128:(i + 1) * 128, :]).then_inc(LD.sem, 16)
                        tL[i] = LD.bump(16)
                    on("sync", fsy)
                    def fv(v, i=i, off=off):
                        if i == 0:
                            WP(v)
                        v.wait_ge(LD.sem, tL[i])
                        if i > 1:
                            v.wait_ge(ST.sem, tT[i - 2])
                        v.tensor_tensor(out=tmp1[:], in0=statR[:, i:i + 1], in1=bc[:, 0:1],
                                        op=Alu.mult)
                        vgap(v)
                        v.tensor_scalar(out=rxsc[:], in0=xt[:, off:off + D],
                                        scalar1=statM[:, i:i + 1], scalar2=tmp1[:, 0:1],
                                        op0=Alu.add, op1=Alu.mult)
                        v.tensor_scalar(out=rxsc[:], in0=rxsc[:], scalar1=MAGIC, scalar2=MAGIC,
                                        op0=Alu.add, op1=Alu.subtract)
                        v.tensor_copy(out=rxb[:, (i % 2) * D:(i % 2) * D + D], in_=rxsc[:])
                        v.maybe_drain_then_inc((SV.sem, 1))
                        tV[i] = SV.bump()
                    on("vector", fv)
                    def ft(t, i=i):
                        if i == 0:
                            WP(t)
                        t.wait_ge(SV.sem, tV[i])
                        if i > 0:
                            t.wait_ge(SA.sem, tA[i - 1])
                        for k in range(8):
                            t.transpose(out=PT[:, k * 128:k * 128 + 128],
                                        in_=rxb[:, (i % 2) * D + k * 128:(i % 2) * D + k * 128 + 128],
                                        identity=idb[:])
                        t.maybe_drain_then_inc((ST.sem, 1))
                        tT[i] = ST.bump()
                    on("tensor", ft)
                    def fs(s, i=i):
                        if i == 0:
                            WP(s)
                        s.wait_ge(ST.sem, tT[i])
                        for k in range(8):
                            s.copy(out=AR_A[:, k * 4096 + i * 128:k * 4096 + i * 128 + 128],
                                   in_=PT[:, k * 128:k * 128 + 128])
                        s.maybe_drain_then_inc((SA.sem, 1))
                        tA[i] = SA.bump()
                    on("scalar", fs)
                barrier()

                def big_mm(hbT, sp_dst, maxname):
                    tEv = {}
                    tSp = {}
                    tMM = {}
                    tBlk = {}
                    tMGdone = {}
                    for mg in range(3):
                        wb_off = (mg % 2) * 8192
                        def fsy(sy, mg=mg, wb_off=wb_off):
                            if mg == 0:
                                WP(sy)
                            elif mg >= 2:
                                sy.wait_ge(ST.sem, tMGdone[mg - 2])
                            for k in range(8):
                                sy.dma_start(out=AR_W[:, wb_off + k * 1024:wb_off + k * 1024 + 1024],
                                             in_=hbT[k][:, mg * 1024:mg * 1024 + 1024]
                                             ).then_inc(LD.sem, 16)
                                LD.bump(16)
                        on("sync", fsy)
                        tBlk[mg] = LD.v
                        for ml in range(8):
                            m = mg * 8 + ml
                            for tg in range(2):
                                def ft(t, mg=mg, ml=ml, tg=tg, m=m, wb_off=wb_off):
                                    if mg == 0 and ml == 0 and tg == 0:
                                        WP(t)
                                    if ml == 0 and tg == 0:
                                        t.wait_ge(LD.sem, tBlk[mg])
                                    prev = (m, tg - 1) if tg == 1 else (m - 1, 1)
                                    if prev in tEv:
                                        t.wait_ge(SA.sem, tEv[prev])
                                    for k in range(8):
                                        lhsT = AR_W[:, wb_off + k * 1024 + ml * 128:
                                                    wb_off + k * 1024 + ml * 128 + 128]
                                        for tt in range(4):
                                            ttok = tg * 4 + tt
                                            rhs = AR_A[:, k * 4096 + ttok * 512:k * 4096 + ttok * 512 + 512]
                                            t.matmul(out=PS7[:, tt * 512:tt * 512 + 512],
                                                     lhsT=lhsT, rhs=rhs,
                                                     start=(k == 0), stop=(k == 7))
                                    t.maybe_drain_then_inc((ST.sem, 1))
                                on("tensor", ft)
                                tMM[(m, tg)] = ST.bump()
                                slab = AR_C if m % 2 == 0 else AR_B
                                def fs(s, m=m, tg=tg, slab=slab):
                                    if m == 0 and tg == 0:
                                        WP(s)
                                    s.wait_ge(ST.sem, tMM[(m, tg)])
                                    if tg == 0 and m >= 2:
                                        s.wait_ge(LD.sem, tSp[m - 2])
                                    for tt in range(4):
                                        ttok = tg * 4 + tt
                                        s.copy(out=slab[:, ttok * 512:ttok * 512 + 512],
                                               in_=PS7[:, tt * 512:tt * 512 + 512])
                                    s.maybe_drain_then_inc((SA.sem, 1))
                                on("scalar", fs)
                                tEv[(m, tg)] = SA.bump()
                            def fv(v, m=m, slab=slab):
                                if m == 0:
                                    WP(v)
                                v.wait_ge(SA.sem, tEv[(m, 1)])
                                v.tensor_reduce(out=lmax[:, m % 32:m % 32 + 1], in_=slab[:],
                                                axis=AX.X, op=Alu.max, apply_absolute_value=True)
                                v.maybe_drain_then_inc((SV.sem, 1))
                            on("vector", fv)
                            tVr = SV.v + 1
                            SV.bump()
                            def fsp(sy, m=m, slab=slab, tVr=tVr):
                                sy.wait_ge(SV.sem, tVr)
                                sy.dma_start(out=sp_dst[m], in_=slab[:]).then_inc(LD.sem, 16)
                                LD.bump(16)
                            on("sync", fsp)
                            tSp[m] = LD.v
                        tMGdone[mg] = ST.v
                    local_absmax(lmax[:, 0:24], maxname)

                big_mm(hb_wgT, sp_gate, "lgate")
                st = stage_locals(smcol("lgate"))
                barrier()
                if upto < 4:
                    return
                do_sync(sync_base + 1, ["ggate"])
                def fs2v(v):
                    v.tensor_tensor(out=smcol("tg1"), in0=smcol("ggate"), in1=smcol("s_x"), op=Alu.mult)
                    vgap(v)
                    v.tensor_scalar(out=smcol("s_gate"), in0=smcol("tg1"), scalar1=smcol("r32767"),
                                    scalar2=1e-12, op0=Alu.mult, op1=Alu.max)
                    vgap(v)
                    v.reciprocal(out=smcol("tg2"), in_=smcol("s_gate"))
                    vgap(v)
                    v.tensor_tensor(out=smcol("cg1"), in0=smcol("s_x"), in1=smcol("tg2"), op=Alu.mult)
                    vgap(v)
                on("vector", fs2v)
                bcast_consts(["cg1", "s_gate"])
                barrier()
                if upto < 5:
                    return

                uEv = {}
                uSp = {}
                uAb = {}
                uMM = {}
                uBlk = {}
                def up_qcol(q):
                    if q < 32:
                        return colS[:, q:q + 1]
                    if q < 64:
                        return colQ[:, q - 32:q - 31]
                    return lmax[:, q - 64:q - 63]
                def emit_up_w(mg):
                    wb_off = (mg % 2) * 8192
                    def fsy(sy, mg=mg, wb_off=wb_off):
                        if mg >= 2:
                            sy.wait_ge(ST.sem, uMM[(mg * 8 - 9, 1)])
                        for k in range(8):
                            sy.dma_start(out=AR_W[:, wb_off + k * 1024:wb_off + k * 1024 + 1024],
                                         in_=hb_wuT[k][:, mg * 1024:mg * 1024 + 1024]
                                         ).then_inc(LD.sem, 16)
                            LD.bump(16)
                        uBlk[mg] = LD.v
                    on("sync", fsy)
                def emit_up_m(m):
                    mg, ml = m // 8, m % 8
                    wb_off = (mg % 2) * 8192
                    for tg in range(2):
                        def ft(t, m=m, ml=ml, tg=tg, wb_off=wb_off, mg=mg):
                            if ml == 0 and tg == 0:
                                t.wait_ge(LD.sem, uBlk[mg])
                            prev = (m, tg - 1) if tg == 1 else (m - 1, 1)
                            if prev in uEv:
                                t.wait_ge(SA.sem, uEv[prev][1])
                            for k in range(8):
                                lhsT = AR_W[:, wb_off + k * 1024 + ml * 128:
                                            wb_off + k * 1024 + ml * 128 + 128]
                                for tt in range(4):
                                    ttok = tg * 4 + tt
                                    rhs = AR_A[:, k * 4096 + ttok * 512:k * 4096 + ttok * 512 + 512]
                                    t.matmul(out=PS7[:, tt * 512:tt * 512 + 512],
                                             lhsT=lhsT, rhs=rhs,
                                             start=(k == 0), stop=(k == 7))
                            t.maybe_drain_then_inc((ST.sem, 1))
                        on("tensor", ft)
                        uMM[(m, tg)] = ST.bump()
                        toks = []
                        for jl in range(2):
                            q = m * 4 + tg * 2 + jl
                            buf = qt if q % 2 == 0 else lnsc
                            def fs(s, m=m, tg=tg, jl=jl, q=q, buf=buf):
                                s.wait_ge(ST.sem, uMM[(m, tg)])
                                if q >= 2:
                                    s.wait_ge(LD.sem, uSp[q - 2])
                                    s.wait_ge(SV.sem, uAb[q - 2])
                                s.copy(out=buf[:], in_=PS7[:, jl * 1024:jl * 1024 + 1024])
                                s.maybe_drain_then_inc((SA.sem, 1))
                            on("scalar", fs)
                            toks.append(SA.bump())
                            def fv(v, q=q, buf=buf, tok=toks[-1]):
                                v.wait_ge(SA.sem, tok)
                                v.tensor_reduce(out=up_qcol(q), in_=buf[:], axis=AX.X,
                                                op=Alu.max, apply_absolute_value=True)
                                v.maybe_drain_then_inc((SV.sem, 1))
                            on("vector", fv)
                            uAb[q] = SV.bump()
                            def fsp(sy, m=m, q=q, buf=buf, tok=toks[-1]):
                                sy.wait_ge(SA.sem, tok)
                                sy.dma_start(out=sp_up[m][:, (q % 4) * 1024:(q % 4) * 1024 + 1024],
                                             in_=buf[:]).then_inc(LD.sem, 16)
                                uSp[q] = LD.bump(16)
                            on("sync", fsp)
                        uEv[(m, tg)] = (toks[0], toks[1])

                tLh = {}
                tVh = {}
                tAh = {}
                tSh = {}
                def A_load(h):
                    m, hf, gb = h // 2, h % 2, (h % 2) * 2048
                    def fsy(sy, h=h, m=m, hf=hf, gb=gb):
                        if h == 0:
                            WP(sy)
                        if h >= 2:
                            sy.wait_ge(LD.sem, tSh[h - 2])
                        sy.dma_start(out=AR_B[:, gb:gb + 2048],
                                     in_=sp_gate[m][:, hf * 2048:hf * 2048 + 2048]
                                     ).then_inc(LD.sem, 16)
                        tLh[h] = LD.bump(16)
                    on("sync", fsy)
                def A_rg1(h):
                    gb = (h % 2) * 2048
                    def fv(v, h=h, gb=gb):
                        if h == 0:
                            WP(v)
                        v.wait_ge(LD.sem, tLh[h])
                        g_ap = AR_B[:, gb:gb + 2048]
                        v.tensor_scalar(out=g_ap, in0=g_ap, scalar1=bc[:, 0:1],
                                        scalar2=MAGIC, op0=Alu.mult, op1=Alu.add)
                        v.tensor_scalar(out=g_ap, in0=g_ap, scalar1=-MAGIC,
                                        scalar2=None, op0=Alu.add)
                        v.maybe_drain_then_inc((SV.sem, 1))
                    on("vector", fv)
                    return SV.bump()
                def A_sig(h, t1):
                    gb = (h % 2) * 2048
                    def fsg(s, h=h, gb=gb, t1=t1):
                        if h == 0:
                            WP(s)
                        s.wait_ge(SV.sem, t1)
                        s.activation(out=AR_C[:, gb:gb + 2048], in_=AR_B[:, gb:gb + 2048],
                                     func=Act.Sigmoid, scale=bc[:, 1:2])
                        s.maybe_drain_then_inc((SA.sem, 1))
                    on("scalar", fsg)
                    tAh[h] = SA.bump()
                def A_post(h):
                    m, hf, gb = h // 2, h % 2, (h % 2) * 2048
                    def fv2(v, h=h, gb=gb):
                        v.wait_ge(SA.sem, tAh[h])
                        c_ap = AR_C[:, gb:gb + 2048]
                        g_ap = AR_B[:, gb:gb + 2048]
                        v.tensor_scalar(out=c_ap, in0=c_ap, scalar1=TWO16,
                                        scalar2=MAGIC, op0=Alu.mult, op1=Alu.add)
                        v.tensor_scalar(out=c_ap, in0=c_ap, scalar1=-MAGIC,
                                        scalar2=65535.0, op0=Alu.add, op1=Alu.min)
                        v.tensor_tensor(out=g_ap, in0=g_ap, in1=c_ap, op=Alu.mult)
                        v.tensor_reduce(out=bmax[:, h:h + 1], in_=g_ap,
                                        axis=AX.X, op=Alu.max, apply_absolute_value=True)
                        v.maybe_drain_then_inc((SV.sem, 1))
                    on("vector", fv2)
                    tVh[h] = SV.bump()
                    def fsp(s, h=h, m=m, hf=hf, gb=gb):
                        s.wait_ge(SV.sem, tVh[h])
                        s.dma_start(out=sp_g2v[m][:, hf * 2048:hf * 2048 + 2048],
                                    in_=AR_B[:, gb:gb + 2048]).then_inc(LD.sem, 16)
                        tSh[h] = LD.bump(16)
                    on("scalar", fsp)
                emit_up_w(0)
                emit_up_w(1)
                A_load(0)
                t1s = {0: A_rg1(0)}
                A_sig(0, t1s[0])
                for h in range(48):
                    if h + 1 < 48:
                        A_load(h + 1)
                        t1s[h + 1] = A_rg1(h + 1)
                        A_sig(h + 1, t1s[h + 1])
                    A_post(h)
                    if h % 2 == 1:
                        m = h // 2
                        emit_up_m(m)
                        if m == 7:
                            emit_up_w(2)
                def fwA(s):
                    s.wait_ge(LD.sem, LD.v)
                    s.maybe_drain_then_inc((SA.sem, 1))
                on("scalar", fwA)
                SA.bump()
                local_absmax(bmax[:, 0:48], "lg2v")
                def fup(v):
                    v.tensor_reduce(out=red1[:], in_=colS[:, 0:32], axis=AX.X, op=Alu.max)
                    v.tensor_reduce(out=tmp1[:], in_=colQ[:, 0:32], axis=AX.X, op=Alu.max)
                    vgap(v)
                    v.tensor_tensor(out=red1[:], in0=red1[:], in1=tmp1[:], op=Alu.max)
                    v.tensor_reduce(out=tmp1[:], in_=lmax[:, 0:32], axis=AX.X, op=Alu.max)
                    vgap(v)
                    v.tensor_tensor(out=red1[:], in0=red1[:], in1=tmp1[:], op=Alu.max)
                    v.maybe_drain_then_inc((SV.sem, 1))
                on("vector", fup)
                tfu = SV.bump()
                def fgu(g):
                    g.wait_ge(SV.sem, tfu)
                    g.dma_start(out=flip[:], in_=red1[:]).then_inc(GD.sem, 16)
                    g.wait_ge(GD.sem, GD.bump(16))
                    g.maybe_drain_then_inc((SG.sem, 1))
                on("gpsimd", fgu)
                tgu = SG.bump()
                def fvu(v):
                    v.wait_ge(SG.sem, tgu)
                    v.tensor_reduce(out=smcol("lup"), in_=flip[:], axis=AX.X, op=Alu.max)
                    vgap(v)
                on("vector", fvu)
                st = stage_locals(smcol("lup"), smcol("lg2v"))
                barrier()
                do_sync(sync_base + 2, ["gup", "gg2v"])
                def fs3v(v):
                    v.tensor_tensor(out=smcol("tu1"), in0=smcol("gup"), in1=smcol("s_x"), op=Alu.mult)
                    v.tensor_scalar(out=smcol("k2"), in0=smcol("s_gate"), scalar1=1.0 / TWO16,
                                    scalar2=None, op0=Alu.mult)
                    vgap(v)
                    v.tensor_scalar(out=smcol("s_up"), in0=smcol("tu1"), scalar1=smcol("r32767"),
                                    scalar2=1e-12, op0=Alu.mult, op1=Alu.max)
                    v.tensor_tensor(out=smcol("t31"), in0=smcol("gg2v"), in1=smcol("k2"), op=Alu.mult)
                    vgap(v)
                    v.reciprocal(out=smcol("tu2"), in_=smcol("s_up"))
                    v.tensor_scalar(out=smcol("s_g2"), in0=smcol("t31"), scalar1=smcol("r32767"),
                                    scalar2=1e-12, op0=Alu.mult, op1=Alu.max)
                    vgap(v)
                    v.tensor_tensor(out=smcol("cup"), in0=smcol("s_x"), in1=smcol("tu2"), op=Alu.mult)
                    v.reciprocal(out=smcol("t32"), in_=smcol("s_g2"))
                    vgap(v)
                    v.tensor_tensor(out=smcol("cg2"), in0=smcol("k2"), in1=smcol("t32"), op=Alu.mult)
                    vgap(v)
                on("vector", fs3v)
                bcast_consts(["cg2", "cup"])
                barrier()
                if upto < 6:
                    return

                tLh = {}
                tVh = {}
                tSh = {}
                for h in range(48):
                    m, hf = h // 2, h % 2
                    gb = (h % 2) * 2048
                    def fsy(sy, h=h, m=m, hf=hf, gb=gb):
                        if h == 0:
                            WP(sy)
                        if h >= 2:
                            sy.wait_ge(LD.sem, tSh[h - 2])
                        sy.dma_start(out=AR_B[:, gb:gb + 2048],
                                     in_=sp_g2v[m][:, hf * 2048:hf * 2048 + 2048]
                                     ).then_inc(LD.sem, 16)
                        LD.bump(16)
                        sy.dma_start(out=AR_C[:, gb:gb + 2048],
                                     in_=sp_up[m][:, hf * 2048:hf * 2048 + 2048]
                                     ).then_inc(LD.sem, 16)
                        tLh[h] = LD.bump(16)
                    on("sync", fsy)
                    def fv(v, h=h, gb=gb):
                        if h == 0:
                            WP(v)
                        v.wait_ge(LD.sem, tLh[h])
                        g_ap = AR_B[:, gb:gb + 2048]
                        u_ap = AR_C[:, gb:gb + 2048]
                        v.tensor_scalar(out=g_ap, in0=g_ap, scalar1=bc[:, 0:1],
                                        scalar2=MAGIC, op0=Alu.mult, op1=Alu.add)
                        v.tensor_scalar(out=g_ap, in0=g_ap, scalar1=-MAGIC,
                                        scalar2=None, op0=Alu.add)
                        v.tensor_scalar(out=u_ap, in0=u_ap, scalar1=bc[:, 1:2],
                                        scalar2=MAGIC, op0=Alu.mult, op1=Alu.add)
                        v.tensor_scalar(out=u_ap, in0=u_ap, scalar1=-MAGIC,
                                        scalar2=None, op0=Alu.add)
                        v.tensor_tensor(out=g_ap, in0=g_ap, in1=u_ap, op=Alu.mult)
                        v.tensor_reduce(out=bmax[:, h:h + 1], in_=g_ap,
                                        axis=AX.X, op=Alu.max, apply_absolute_value=True)
                        v.maybe_drain_then_inc((SV.sem, 1))
                    on("vector", fv)
                    tVh[h] = SV.bump()
                    def fsp(sy, h=h, m=m, hf=hf, gb=gb):
                        sy.wait_ge(SV.sem, tVh[h])
                        sy.dma_start(out=sp_p[m][:, hf * 2048:hf * 2048 + 2048],
                                     in_=AR_B[:, gb:gb + 2048]).then_inc(LD.sem, 16)
                        tSh[h] = LD.bump(16)
                    on("sync", fsp)
                def fwB(s):
                    s.wait_ge(LD.sem, LD.v)
                    s.maybe_drain_then_inc((SA.sem, 1))
                on("scalar", fwB)
                SA.bump()
                local_absmax(bmax[:, 0:48], "lp")
                st = stage_locals(smcol("lp"))
                barrier()
                do_sync(sync_base + 3, ["gp"])
                def fs4v(v):
                    v.tensor_tensor(out=smcol("k3"), in0=smcol("s_g2"), in1=smcol("s_up"), op=Alu.mult)
                    vgap(v)
                    v.tensor_tensor(out=smcol("t41"), in0=smcol("gp"), in1=smcol("k3"), op=Alu.mult)
                    vgap(v)
                    v.tensor_scalar(out=smcol("s_o"), in0=smcol("t41"), scalar1=smcol("r32767"),
                                    scalar2=1e-12, op0=Alu.mult, op1=Alu.max)
                    vgap(v)
                    v.reciprocal(out=smcol("t42"), in_=smcol("s_o"))
                    v.tensor_scalar(out=smcol("kout"), in0=smcol("s_o"), scalar1=0.5,
                                    scalar2=None, op0=Alu.mult)
                    vgap(v)
                    v.tensor_tensor(out=smcol("co"), in0=smcol("k3"), in1=smcol("t42"), op=Alu.mult)
                    vgap(v)
                on("vector", fs4v)
                bcast_consts(["co", "kout"])
                barrier()
                if upto < 7:
                    return

                def fldw(sy):
                    WP(sy)
                    for k in range(24):
                        sy.dma_start(out=AR_A[:, k * 1024:k * 1024 + 1024],
                                     in_=hb_wdT[k]).then_inc(LD.sem, 16)
                        LD.bump(16)
                on("sync", fldw)
                tldw = LD.v
                tgv_all = {}
                tMMc = {}
                for tt in range(16):
                    tgv = {}
                    tgl = {}
                    rb = (tt % 2) * 6144
                    for g in range(3):
                        gb = (g % 2) * 2048
                        def fsy(sy, tt=tt, g=g, gb=gb):
                            if tt == 0 and g == 0:
                                WP(sy)
                                sy.wait_ge(LD.sem, tldw)
                            if g == 0 and tt >= 2:
                                sy.wait_ge(ST.sem, tMMc[tt - 2])
                            gate = tgv.get(g - 2, tgv_all.get((tt - 1, g + 1)))
                            if gate is not None:
                                sy.wait_ge(SV.sem, gate)
                            for j in range(8):
                                sy.dma_start(out=AR_B[:, gb + j * 256:gb + j * 256 + 256],
                                             in_=sp_p[g * 8 + j][:, tt * 256:tt * 256 + 256]
                                             ).then_inc(LD.sem, 16)
                                LD.bump(16)
                            tgl[g] = LD.v
                        on("sync", fsy)
                        def fv(v, tt=tt, g=g, gb=gb, rb=rb):
                            if tt == 0 and g == 0:
                                WP(v)
                            v.wait_ge(LD.sem, tgl[g])
                            b_ap = AR_B[:, gb:gb + 2048]
                            v.tensor_scalar(out=b_ap, in0=b_ap, scalar1=bc[:, 0:1],
                                            scalar2=MAGIC, op0=Alu.mult, op1=Alu.add)
                            v.tensor_scalar(out=b_ap, in0=b_ap, scalar1=-MAGIC,
                                            scalar2=None, op0=Alu.add)
                            v.tensor_copy(out=AR_W[:, rb + g * 2048:rb + g * 2048 + 2048], in_=b_ap)
                            v.maybe_drain_then_inc((SV.sem, 1))
                        on("vector", fv)
                        tgv[g] = SV.bump()
                    tgv_all.update({(tt, g): t for g, t in tgv.items()})
                    tro = {23: tgv[2]}
                    def ft(t, tt=tt):
                        if tt == 0:
                            WP(t)
                        t.wait_ge(SV.sem, tro[23])
                        if tt > 0:
                            t.wait_ge(SA.sem, SA.v)
                            t.wait_ge(SV.sem, SV.v)
                        rb2 = (tt % 2) * 6144
                        for m in range(8):
                            for k in range(24):
                                t.matmul(out=PS7[:, m * 256:m * 256 + 256],
                                         lhsT=AR_A[:, k * 1024 + m * 128:k * 1024 + m * 128 + 128],
                                         rhs=AR_W[:, rb2 + k * 256:rb2 + k * 256 + 256],
                                         start=(k == 0), stop=(k == 23))
                        t.maybe_drain_then_inc((ST.sem, 1))
                    on("tensor", ft)
                    tmm = ST.bump()
                    tMMc[tt] = tmm
                    def fs(s, tt=tt, tmm=tmm):
                        if tt == 0:
                            WP(s)
                        s.wait_ge(ST.sem, tmm)
                        for m in range(8):
                            s.copy(out=AR_C[:, m * 256:m * 256 + 256],
                                   in_=PS7[:, m * 256:m * 256 + 256])
                        s.maybe_drain_then_inc((SA.sem, 1))
                    on("scalar", fs)
                    tev = SA.bump()
                    for tb in range(2):
                        def ft2(t, tt=tt, tb=tb, tev=tev):
                            t.wait_ge(SA.sem, tev)
                            if tb == 1:
                                t.wait_ge(SV.sem, SV.v)
                            for m in range(8):
                                t.transpose(out=PS7[:, 2048 + m * 128:2048 + m * 128 + 128],
                                            in_=AR_C[:, m * 256 + tb * 128:
                                                      m * 256 + tb * 128 + 128],
                                            identity=idf[:])
                            t.maybe_drain_then_inc((ST.sem, 1))
                        on("tensor", ft2)
                        ttr = ST.bump()
                        def fsy2(sy, tt=tt, tb=tb):
                            row = tt * 256 + tb * 128
                            sy.dma_start(out=xt[:, 0:D], in_=x_src[row:row + 128, :]
                                         ).then_inc(LD.sem, 16)
                            LD.bump(16)
                        on("sync", fsy2)
                        tlx = LD.v
                        def fv3(v, tt=tt, tb=tb, ttr=ttr, tlx=tlx):
                            v.wait_ge(ST.sem, ttr)
                            v.wait_ge(LD.sem, tlx)
                            v.scalar_tensor_tensor(out=xt[:, 0:D], in0=PS7[:, 2048:3072],
                                                   scalar=bc[:, 1:2], in1=xt[:, 0:D],
                                                   op0=Alu.mult, op1=Alu.add)
                            v.maybe_drain_then_inc((SV.sem, 1))
                        on("vector", fv3)
                        tvx = SV.bump()
                        def fsy3(sy, tt=tt, tb=tb, tvx=tvx):
                            row = tt * 256 + tb * 128
                            sy.wait_ge(SV.sem, tvx)
                            sy.dma_start(out=x_dst[row:row + 128, :], in_=xt[:, 0:D]
                                         ).then_inc(LD.sem, 16)
                            LD.bump(16)
                        on("sync", fsy3)
                barrier()

            def attn_sublayer(x_src, x_dst):
                ln_stats(x_src, want_minmax=False)
                barrier()
                def fldw(g):
                    WP(g)
                    for k in range(8):
                        g.dma_start(out=AR_W[:, k * 1024:k * 1024 + 1024],
                                    in_=Wq_d[k * 128:(k + 1) * 128, :]).then_inc(GD.sem, 16)
                        GD.bump(16)
                    for k in range(8):
                        g.dma_start(out=AR_W[:, 8192 + k * 256:8192 + k * 256 + 256],
                                    in_=Wdn_d[k * 128:(k + 1) * 128, :]).then_inc(GD.sem, 16)
                        GD.bump(16)
                    for k in range(2):
                        g.dma_start(out=AR_W[:, 10240 + k * 2048:10240 + k * 2048 + 2048],
                                    in_=Wua_d[k * 128:(k + 1) * 128, :]).then_inc(GD.sem, 16)
                        GD.bump(16)
                    for k in range(8):
                        g.dma_start(out=AR_W[:, 14336 + k * 1024:14336 + k * 1024 + 1024],
                                    in_=Wo_d[k * 128:(k + 1) * 128, :]).then_inc(GD.sem, 16)
                        GD.bump(16)
                    g.maybe_drain_then_inc((SG.sem, 1))
                on("gpsimd", fldw)
                tgw = SG.bump()
                def flb(sy):
                    WP(sy)
                    sy.dma_start(out=bias1[0:1, 0:1024], in_=bq_d).then_inc(LD.sem, 16)
                    sy.dma_start(out=bias1[0:1, 1024:1280], in_=bdn_d).then_inc(LD.sem, 16)
                    sy.dma_start(out=bias1[0:1, 1280:3328], in_=bua_d).then_inc(LD.sem, 16)
                    sy.dma_start(out=bias1[0:1, 3328:4352], in_=bo_d).then_inc(LD.sem, 16)
                    LD.bump(64)
                on("sync", flb)
                tlb = LD.v
                def ftb1(t, tlb=tlb):
                    WP(t)
                    t.wait_ge(LD.sem, tlb)
                    for j in range(7):
                        t.matmul(out=PS7[:, j * 512:j * 512 + 512], lhsT=ones1[:],
                                 rhs=bias1[0:1, j * 512:j * 512 + 512], start=True, stop=True)
                    t.maybe_drain_then_inc((ST.sem, 1))
                on("tensor", ftb1)
                ST.bump()
                def fvb1(v, tgt=ST.v):
                    WP(v)
                    v.wait_ge(ST.sem, tgt)
                    v.tensor_copy(out=AR_C[:, 0:3328], in_=PS7[:, 0:3328])
                    v.maybe_drain_then_inc((SV.sem, 1))
                on("vector", fvb1)
                SV.bump()
                def ftb2(t, tgt=SV.v):
                    t.wait_ge(SV.sem, tgt)
                    t.matmul(out=PS7[:, 0:512], lhsT=ones1[:], rhs=bias1[0:1, 3328:3840],
                             start=True, stop=True)
                    t.matmul(out=PS7[:, 512:1024], lhsT=ones1[:], rhs=bias1[0:1, 3840:4352],
                             start=True, stop=True)
                    t.maybe_drain_then_inc((ST.sem, 1))
                on("tensor", ftb2)
                ST.bump()
                def fvb2(v, tgt=ST.v):
                    v.wait_ge(ST.sem, tgt)
                    v.tensor_copy(out=AR_B[:, 0:1024], in_=PS7[:, 0:1024])
                    v.maybe_drain_then_inc((SV.sem, 1))
                on("vector", fvb2)
                SV.bump()
                tbias = SV.v
                tL = {}
                tV = {}
                tT = {}
                tA = {}
                for i in range(NT):
                    off = (i % 2) * D
                    def fsy(sy, i=i, off=off):
                        if i == 0:
                            WP(sy)
                        if i >= 1:
                            sy.wait_ge(SA.sem, tA[i - 1])
                        sy.dma_start(out=xt[:, off:off + D],
                                     in_=x_src[i * 128:(i + 1) * 128, :]).then_inc(LD.sem, 16)
                        tL[i] = LD.bump(16)
                    on("sync", fsy)
                    def fv(v, i=i, off=off):
                        if i == 0:
                            WP(v)
                            v.wait_ge(SV.sem, tbias)
                        v.wait_ge(LD.sem, tL[i])
                        if i > 0:
                            v.wait_ge(ST.sem, tT[i - 1])
                        v.tensor_scalar(out=lnsc[:], in0=xt[:, off:off + D],
                                        scalar1=statM[:, i:i + 1], scalar2=statR[:, i:i + 1],
                                        op0=Alu.add, op1=Alu.mult)
                        v.tensor_copy(out=rxb[:, 0:D], in_=lnsc[:])
                        v.maybe_drain_then_inc((SV.sem, 1))
                        tV[i] = SV.bump()
                    on("vector", fv)
                    def ft(t, i=i):
                        if i == 0:
                            WP(t)
                        t.wait_ge(SV.sem, tV[i])
                        if i > 0:
                            t.wait_ge(SA.sem, tA[i - 1])
                        for k in range(8):
                            t.transpose(out=PT[:, k * 128:k * 128 + 128],
                                        in_=rxb[:, k * 128:k * 128 + 128], identity=idb[:])
                        t.maybe_drain_then_inc((ST.sem, 1))
                        tT[i] = ST.bump()
                    on("tensor", ft)
                    def fs(s, i=i):
                        if i == 0:
                            WP(s)
                        s.wait_ge(ST.sem, tT[i])
                        for k in range(8):
                            s.copy(out=AR_A[:, k * 4096 + i * 128:k * 4096 + i * 128 + 128],
                                   in_=PT[:, k * 128:k * 128 + 128])
                        s.maybe_drain_then_inc((SA.sem, 1))
                        tA[i] = SA.bump()
                    on("scalar", fs)
                def fwg(t, tgw=tgw):
                    t.wait_ge(SG.sem, tgw)
                on("tensor", fwg)
                barrier()

                kv3 = kvt[:].rearrange("p (g c) -> p g c", c=128)
                tQE = {}
                tKVE = {}
                tSCD = {}
                tAVD = {}
                tOPE = {}
                tP1 = {}
                tP2 = {}
                tP3 = {}
                tP4 = {}
                tP5 = {}
                tAO = {}
                tKLT = {}
                def emit_ft1(i):
                    def ft1(t, i=i):
                        if i == 0:
                            WP(t)
                        else:
                            t.wait_ge(SA.sem, tQE[i - 1])
                            t.wait_ge(SV.sem, tKLT[i - 1])
                        for k in range(8):
                            lhsT = AR_A[:, k * 4096 + i * 128:k * 4096 + i * 128 + 128]
                            t.matmul(out=PS7[:, 0:512], lhsT=lhsT,
                                     rhs=AR_W[:, k * 1024:k * 1024 + 512],
                                     start=(k == 0), stop=False)
                            t.matmul(out=PS7[:, 512:1024], lhsT=lhsT,
                                     rhs=AR_W[:, k * 1024 + 512:k * 1024 + 1024],
                                     start=(k == 0), stop=False)
                        t.matmul(out=PS7[:, 0:512], lhsT=ones1[:], rhs=bias1[0:1, 0:512],
                                 start=False, stop=True)
                        t.matmul(out=PS7[:, 512:1024], lhsT=ones1[:], rhs=bias1[0:1, 512:1024],
                                 start=False, stop=True)
                        for k in range(8):
                            lhsT = AR_A[:, k * 4096 + i * 128:k * 4096 + i * 128 + 128]
                            t.matmul(out=PS7[:, 3072:3200],
                                     lhsT=AR_W[:, 8192 + k * 256:8192 + k * 256 + 128],
                                     rhs=lhsT, start=(k == 0), stop=False)
                        t.matmul(out=PS7[:, 3072:3200], lhsT=bias1[0:1, 1024:1152],
                                 rhs=ones1[0:1, 0:128], start=False, stop=True)
                        for k in range(8):
                            lhsT = AR_A[:, k * 4096 + i * 128:k * 4096 + i * 128 + 128]
                            t.matmul(out=PS7[:, 3200:3328],
                                     lhsT=AR_W[:, 8192 + k * 256 + 128:8192 + k * 256 + 256],
                                     rhs=lhsT, start=(k == 0), stop=False)
                        t.matmul(out=PS7[:, 3200:3328], lhsT=bias1[0:1, 1152:1280],
                                 rhs=ones1[0:1, 0:128], start=False, stop=True)
                        t.maybe_drain_then_inc((ST.sem, 1))
                    on("tensor", ft1)
                    tP1[i] = ST.bump()

                emit_ft1(0)
                for i in range(NT):
                    def fa1(s, i=i):
                        if i == 0:
                            WP(s)
                        s.wait_ge(ST.sem, tP1[i])
                        if i > 0:
                            s.wait_ge(SV.sem, tSCD[i - 1])
                        s.copy(out=qt[:], in_=PS7[:, 0:1024])
                        s.maybe_drain_then_inc((SA.sem, 1))
                    on("scalar", fa1)
                    tQE[i] = SA.bump()
                    def fvk(v, i=i):
                        if i == 0:
                            WP(v)
                        v.wait_ge(ST.sem, tP1[i])
                        v.tensor_copy(out=kltT[:], in_=PS7[:, 3072:3328])
                        v.maybe_drain_then_inc((SV.sem, 1))
                    on("vector", fvk)
                    tKLT[i] = SV.bump()
                    def ft3(t, i=i):
                        t.wait_ge(SV.sem, tKLT[i])
                        if i > 0:
                            t.wait_ge(SA.sem, tKVE[i - 1])
                            t.wait_ge(SA.sem, tOPE[i - 1])
                        for k2 in range(2):
                            lhsT = kltT[:, k2 * 128:k2 * 128 + 128]
                            for n in range(4):
                                t.matmul(out=PS7[:, 1024 + n * 512:1024 + n * 512 + 512],
                                         lhsT=lhsT,
                                         rhs=AR_W[:, 10240 + k2 * 2048 + n * 512:
                                                   10240 + k2 * 2048 + n * 512 + 512],
                                         start=(k2 == 0), stop=False)
                        for n in range(4):
                            t.matmul(out=PS7[:, 1024 + n * 512:1024 + n * 512 + 512],
                                     lhsT=ones1[:],
                                     rhs=bias1[0:1, 1280 + n * 512:1280 + n * 512 + 512],
                                     start=False, stop=True)
                        t.maybe_drain_then_inc((ST.sem, 1))
                    on("tensor", ft3)
                    tP3[i] = ST.bump()
                    def fa3(s, i=i):
                        s.wait_ge(ST.sem, tP3[i])
                        if i > 0:
                            s.wait_ge(SV.sem, tAVD[i - 1])
                        s.copy(out=kvt[:], in_=PS7[:, 1024:3072])
                        s.maybe_drain_then_inc((SA.sem, 1))
                    on("scalar", fa3)
                    tKVE[i] = SA.bump()
                    def fv3(v, i=i):
                        if i == 0:
                            WP(v)
                        v.wait_ge(SA.sem, tKVE[i])
                        for h in range(16):
                            qh = qt[:, h * 64:h * 64 + 64].unsqueeze(1).broadcast_to((128, 16, 64))
                            v.tensor_tensor(out=prod[:].rearrange("p (g c) -> p g c", c=64),
                                            in0=qh, in1=kv3[:, :, 0:64], op=Alu.mult)
                            v.tensor_reduce(out=sco[:, h * 16:h * 16 + 16],
                                            in_=prod[:].rearrange("p (g c) -> p g c", c=64),
                                            axis=AX.X, op=Alu.add)
                        v.tensor_reduce(out=smax[:], in_=sco[:].rearrange("p (h g) -> p h g", g=16),
                                        axis=AX.X, op=Alu.max)
                        vgap(v)
                        v.tensor_tensor(out=sco[:].rearrange("p (h g) -> p h g", g=16),
                                        in0=sco[:].rearrange("p (h g) -> p h g", g=16),
                                        in1=smax[:].unsqueeze(2).broadcast_to((128, 16, 16)),
                                        op=Alu.subtract)
                        v.maybe_drain_then_inc((SV.sem, 1))
                    on("vector", fv3)
                    tSCD[i] = SV.bump()
                    def fsc(s, i=i):
                        s.wait_ge(SV.sem, tSCD[i])
                        s.activation(out=sco[:], in_=sco[:], func=Act.Exp, scale=0.125)
                        s.maybe_drain_then_inc((SA.sem, 1))
                    on("scalar", fsc)
                    t7 = SA.bump()
                    def fv4(v, i=i, t7=t7):
                        v.wait_ge(SA.sem, t7)
                        v.tensor_reduce(out=rsum[:], in_=sco[:].rearrange("p (h g) -> p h g", g=16),
                                        axis=AX.X, op=Alu.add)
                        vgap(v)
                        v.reciprocal(out=rden[:], in_=rsum[:])
                        vgap(v)
                        v.tensor_tensor(out=sco[:].rearrange("p (h g) -> p h g", g=16),
                                        in0=sco[:].rearrange("p (h g) -> p h g", g=16),
                                        in1=rden[:].unsqueeze(2).broadcast_to((128, 16, 16)),
                                        op=Alu.mult)
                        sc0 = sco[:]
                        o_full = acc[:].rearrange("p (h c) -> p h c", c=64)
                        p_full = prod[:].rearrange("p (h c) -> p h c", c=64)
                        for gi in range(16):
                            a_ap = bass.AP(sc0.tensor, sc0.offset + gi,
                                           [[256, 128], [16, 16], [0, 64]])
                            v_ap = bass.AP(kvt[:].tensor, kvt[:].offset + 128 * gi + 64,
                                           [[2048, 128], [0, 16], [1, 64]])
                            if gi == 0:
                                v.tensor_tensor(out=o_full, in0=a_ap, in1=v_ap, op=Alu.mult)
                            else:
                                v.tensor_tensor(out=p_full, in0=a_ap, in1=v_ap, op=Alu.mult)
                                v.tensor_tensor(out=o_full, in0=o_full, in1=p_full, op=Alu.add)
                        v.tensor_copy(out=accb[:, 0:D], in_=acc[:])
                        v.maybe_drain_then_inc((SV.sem, 1))
                    on("vector", fv4)
                    tAVD[i] = SV.bump()
                    if i + 1 < NT:
                        emit_ft1(i + 1)
                    def ft4(t, i=i):
                        t.wait_ge(SV.sem, tAVD[i])
                        for k in range(8):
                            t.transpose(out=PT[:, k * 128:k * 128 + 128],
                                        in_=accb[:, k * 128:k * 128 + 128], identity=idb[:])
                        t.maybe_drain_then_inc((ST.sem, 1))
                    on("tensor", ft4)
                    tP4[i] = ST.bump()
                    def fsa(s, i=i):
                        s.wait_ge(ST.sem, tP4[i])
                        s.copy(out=aoT[:], in_=PT[:, 0:1024])
                        s.maybe_drain_then_inc((SA.sem, 1))
                    on("scalar", fsa)
                    tAO[i] = SA.bump()
                    def ft5(t, i=i):
                        t.wait_ge(SA.sem, tAO[i])
                        for k in range(8):
                            lhsT = aoT[:, k * 128:k * 128 + 128]
                            t.matmul(out=PS7[:, 1024:1536], lhsT=lhsT,
                                     rhs=AR_W[:, 14336 + k * 1024:14336 + k * 1024 + 512],
                                     start=(k == 0), stop=False)
                            t.matmul(out=PS7[:, 1536:2048], lhsT=lhsT,
                                     rhs=AR_W[:, 14336 + k * 1024 + 512:14336 + k * 1024 + 1024],
                                     start=(k == 0), stop=False)
                        t.matmul(out=PS7[:, 1024:1536], lhsT=ones1[:],
                                 rhs=bias1[0:1, 3328:3840], start=False, stop=True)
                        t.matmul(out=PS7[:, 1536:2048], lhsT=ones1[:],
                                 rhs=bias1[0:1, 3840:4352], start=False, stop=True)
                        t.maybe_drain_then_inc((ST.sem, 1))
                    on("tensor", ft5)
                    tP5[i] = ST.bump()
                    def fa5(s, i=i):
                        s.wait_ge(ST.sem, tP5[i])
                        s.copy(out=lnsc[:], in_=PS7[:, 1024:2048])
                        s.maybe_drain_then_inc((SA.sem, 1))
                    on("scalar", fa5)
                    tOPE[i] = SA.bump()
                    def fsy(sy, i=i):
                        sy.dma_start(out=xt[:, 0:D], in_=x_src[i * 128:(i + 1) * 128, :]
                                     ).then_inc(LD.sem, 16)
                        LD.bump(16)
                    on("sync", fsy)
                    t12 = LD.v
                    def fv5(v, i=i, t12=t12):
                        v.wait_ge(SA.sem, tOPE[i])
                        v.wait_ge(LD.sem, t12)
                        v.tensor_tensor(out=xt[:, 0:D], in0=lnsc[:], in1=xt[:, 0:D], op=Alu.add)
                        v.maybe_drain_then_inc((SV.sem, 1))
                    on("vector", fv5)
                    t13 = SV.bump()
                    def fsy2(sy, i=i, t13=t13):
                        sy.wait_ge(SV.sem, t13)
                        sy.dma_start(out=x_dst[i * 128:(i + 1) * 128, :], in_=xt[:, 0:D]
                                     ).then_inc(LD.sem, 16)
                        LD.bump(16)
                    on("sync", fsy2)
                barrier()

            if KSTOP >= 5:
                ffn_sublayer(x_in, hbm_x, 0, upto=KSTOP - 5)
            if KSTOP >= 13:
                attn_sublayer(hbm_x, hbm_x)
            if KSTOP >= 14:
                ffn_sublayer(hbm_x, y_out, 4, upto=(99 if KSTOP >= 99 else KSTOP - 14))

            def fdbg(g):
                WP(g)
                g.dma_start(out=dbg_out[0:1, 0:64], in_=sm[:]).then_inc(GD.sem, 16)
                GD.bump(16)
                g.dma_start(out=dbg_out[0:128, 64:72], in_=bc[:]).then_inc(GD.sem, 16)
                GD.bump(16)
                g.dma_start(out=dbg_out[0:128, 72:104], in_=statM[:]).then_inc(GD.sem, 16)
                GD.bump(16)
                g.dma_start(out=dbg_out[0:128, 104:136], in_=statR[:]).then_inc(GD.sem, 16)
                GD.bump(16)
                g.dma_start(out=dbg_out[0:128, 136:168], in_=lmax[:]).then_inc(GD.sem, 16)
                GD.bump(16)
                g.dma_start(out=dbg_out[0:128, 168:200], in_=colS[:]).then_inc(GD.sem, 16)
                GD.bump(16)
                g.dma_start(out=dbg_out[0:128, 200:232], in_=colQ[:]).then_inc(GD.sem, 16)
                GD.bump(16)
                kd = os.environ.get('KDUMP', '')
                if kd == 'hbm_x':
                    g.dma_start(out=y_out, in_=hbm_x).then_inc(GD.sem, 16)
                    GD.bump(16)
                elif kd in ('spg', 'spu', 'spv', 'spp'):
                    s = {'spg': sp_gate, 'spu': sp_up, 'spv': sp_g2v, 'spp': sp_p}[kd]
                    g.dma_start(out=y_out, in_=s[0:8]).then_inc(GD.sem, 16)
                    GD.bump(16)
                elif kd in ('wgT', 'wuT'):
                    s = {'wgT': hb_wgT, 'wuT': hb_wuT}[kd]
                    g.dma_start(out=y_out[0:3072, :], in_=s).then_inc(GD.sem, 16)
                    GD.bump(16)
                elif kd == 'wdT':
                    g.dma_start(out=y_out[0:3072, :], in_=hb_wdT).then_inc(GD.sem, 16)
                    GD.bump(16)
                elif kd == 'arA':
                    g.dma_start(out=y_out, in_=AR_A[:]).then_inc(GD.sem, 16)
                    GD.bump(16)
                elif kd == 'arB':
                    g.dma_start(out=y_out[0:512, :], in_=AR_B[:]).then_inc(GD.sem, 16)
                    GD.bump(16)
                elif kd == 'attn':
                    g.dma_start(out=y_out[0:128, :], in_=qt[:]).then_inc(GD.sem, 16)
                    GD.bump(16)
                    g.dma_start(out=y_out[128:384, :], in_=kvt[:]).then_inc(GD.sem, 16)
                    GD.bump(16)
                    g.dma_start(out=y_out[384:416, :], in_=sco[:]).then_inc(GD.sem, 16)
                    GD.bump(16)
                    g.dma_start(out=y_out[416:544, :], in_=acc[:]).then_inc(GD.sem, 16)
                    GD.bump(16)
                elif kd == 'arW':
                    g.dma_start(out=y_out[0:768, :], in_=AR_W[:, 0:6144]).then_inc(GD.sem, 16)
                    GD.bump(16)
            on("gpsimd", fdbg)
            def ffin(sy):
                WP(sy)
                sy.wait_ge(LD.sem, LD.v)
                sy.wait_ge(GD.sem, GD.v)
            on("sync", ffin)
            barrier()
    return nc


_NC_CACHE = [None]

def kernel(L0_attn_x, L0_attn_Wq, L0_attn_bq, L0_attn_Wdown, L0_attn_bdown,
           L0_attn_Wup, L0_attn_bup, L0_attn_Wout, L0_attn_bout,
           L0_ffn_w_gate, L0_ffn_w_up, L0_ffn_w_down):
    if _NC_CACHE[0] is None:
        _NC_CACHE[0] = build_nc()
    nc = _NC_CACHE[0]
    x = np.ascontiguousarray(np.asarray(L0_attn_x, dtype=np.float32))
    common = {
        "Wq": np.ascontiguousarray(np.asarray(L0_attn_Wq, np.float32)),
        "bq": np.asarray(L0_attn_bq, np.float32).reshape(1, D),
        "Wdn": np.ascontiguousarray(np.asarray(L0_attn_Wdown, np.float32)),
        "bdn": np.asarray(L0_attn_bdown, np.float32).reshape(1, 256),
        "Wua": np.ascontiguousarray(np.asarray(L0_attn_Wup, np.float32)),
        "bua": np.asarray(L0_attn_bup, np.float32).reshape(1, 2048),
        "Wo": np.ascontiguousarray(np.asarray(L0_attn_Wout, np.float32)),
        "bo": np.asarray(L0_attn_bout, np.float32).reshape(1, D),
        "wg": np.ascontiguousarray(np.asarray(L0_ffn_w_gate, np.float32)),
        "wu": np.ascontiguousarray(np.asarray(L0_ffn_w_up, np.float32)),
        "wd": np.ascontiguousarray(np.asarray(L0_ffn_w_down, np.float32)),
    }
    in_maps = [dict(common, x=np.ascontiguousarray(x[c])) for c in range(8)]
    res = run_bass_kernel_spmd(nc, in_maps, core_ids=list(range(8)),
                               trace=(os.environ.get('KTRACE') == '1'))
    if os.environ.get('KTRACE') == '1':
        print('EXEC_NS', res.exec_time_ns)
    if os.environ.get('KDBG'):
        np.save('/tmp/dbg0.npy', np.asarray(res.results[0]['dbg']))
    out = np.stack([np.asarray(res.results[c]["y"], dtype=np.float32) for c in range(8)], axis=0)
    return out
